# version 08
# baseline (speedup 1.0000x reference)
import contextlib
import math
import os

import numpy as np
import ml_dtypes

import concourse.bass as bass
import concourse.mybir as mybir
from concourse.bass_utils import run_bass_kernel_spmd

F32 = mybir.dt.float32
BF16 = mybir.dt.bfloat16
F16 = mybir.dt.float16
I32 = mybir.dt.int32
U8 = mybir.dt.uint8
ALU = mybir.AluOpType
AF = mybir.ActivationFunctionType
AX = mybir.AxisListType.X
NPBF = ml_dtypes.bfloat16

NF4 = np.array([
    -1.0, -0.6961928009986877, -0.5250730514526367, -0.39491748809814453,
    -0.28444138169288635, -0.18477343022823334, -0.09105003625154495, 0.0,
    0.07958029955625534, 0.16093020141124725, 0.24611230194568634,
    0.33791524171829224, 0.44070982933044434, 0.5626170039176941,
    0.7229568362236023, 1.0], dtype=np.float64)

DQ_CA = 4.5
NFG = (NF4[1:] - NF4[:-1])
NFDA = (NF4[:-1] - NFG * (np.arange(15) - DQ_CA))
NFD0 = (NF4[:-1] - NFG * np.arange(15))
NFG = [float(v) for v in NFG]
NFDA = [float(v) for v in NFDA]
NFD0 = [float(v) for v in NFD0]

DEAD = -60000.0


def alibi_slopes(n):
    cp2 = 2 ** math.floor(math.log2(n))
    base = 2 ** (-(2 ** (-(math.log2(cp2) - 3))))
    slopes = base ** np.arange(1, cp2 + 1, dtype=np.float32)
    if cp2 != n:
        eb = 2 ** (-(2 ** (-(math.log2(2 * cp2) - 3))))
        extra = eb ** np.arange(1, 1 + 2 * min(cp2, n - cp2), 2,
                                dtype=np.float32)
        slopes = np.concatenate([slopes, extra])
    return slopes.astype(np.float32)


class Cfg:
    def __init__(self, NH=32, B=4, S=2048, I=11008):
        self.NC = 8
        self.D = 128
        self.NH = NH
        self.DM = NH * self.D
        self.HC = self.DM // 128
        self.B = B
        self.S = S
        self.TB = S
        self.TPC = S // 2
        self.KC = S // 128
        self.KH = self.KC // 2
        self.CC = 3 * NH
        self.CCH = 6
        self.NPASS = self.CC // self.CCH
        self.TSUP = min(512, S)
        self.NTS = self.TB // self.TSUP
        self.I = I
        self.JC = I // 128
        self.SCALE = 1.0 / self.D
        self.DQW = 1024
        self.NCGU = -(-I // self.DQW)
        self.NCD = self.DM // self.DQW
        self.CPT = self.TPC // 128


def stage_inputs(cfg, x, w_attn, b_attn, w_proj, b_proj,
                 g_codes, g_scales, u_codes, u_scales, d_codes, d_scales):
    c = cfg
    slopes = alibi_slopes(c.NH)
    x = np.asarray(x, dtype=np.float32)
    w_attn = np.asarray(w_attn, dtype=np.float32)
    b_attn = np.asarray(b_attn, dtype=np.float32)
    w_proj = np.asarray(w_proj, dtype=np.float32)
    b_proj = np.asarray(b_proj, dtype=np.float32)

    wq = np.concatenate(
        [w_attn[:, (g * c.DM + h * 128):(g * c.DM + (h + 1) * 128)]
         for h in range(c.NH) for g in range(3)], axis=1)
    wq3 = np.ascontiguousarray(wq).reshape(c.HC, 128, c.CC * 128)
    bqcols = np.stack(
        [b_attn[g * c.DM + h * 128:g * c.DM + (h + 1) * 128]
         for h in range(c.NH) for g in range(3)], axis=1)
    bq = np.ascontiguousarray(bqcols.astype(np.float32))

    wp4 = np.ascontiguousarray(
        w_proj.reshape(c.HC, 128, c.HC, 128).transpose(2, 0, 1, 3))
    bpc = np.ascontiguousarray(b_proj.reshape(c.HC, 128).T)

    gT = np.ascontiguousarray(np.asarray(g_codes).T.astype(np.uint8))
    uT = np.ascontiguousarray(np.asarray(u_codes).T.astype(np.uint8))
    dTr = np.ascontiguousarray(np.asarray(d_codes).T.astype(np.uint8))
    gcT = gT.reshape(c.HC, 128, c.I)
    ucT = uT.reshape(c.HC, 128, c.I)
    dcT = dTr.reshape(c.JC, 128, c.DM)
    gsT = np.ascontiguousarray(np.asarray(g_scales).T)
    usT = np.ascontiguousarray(np.asarray(u_scales).T)
    dsT = np.ascontiguousarray(np.asarray(d_scales).T)

    common = {
        "wq3": wq3, "bq": bq, "wp4": wp4, "bpc": bpc,
        "ones22": np.ones((2, 128), dtype=NPBF),
        "onesc": np.ones((128, 1), dtype=NPBF),
        "onesf": np.ones((128, 1), dtype=np.float32),
        "masku": np.ascontiguousarray(
            np.where(np.arange(128)[:, None] > np.arange(128)[None, :],
                     np.float32(-1e30), np.float32(0.0)).astype(np.float32)),
        "gcT": gcT, "ucT": ucT, "dcT": dcT,
        "gsT": gsT, "usT": usT, "dsT": dsT,
    }

    in_maps = []
    for r in range(c.NC):
        b, h = r // 2, r % 2
        m = dict(common)
        xb = x[b]
        xT = np.ascontiguousarray(xb.T)
        mine = xT[:, h * c.TPC:(h + 1) * c.TPC]
        othr = xT[:, (1 - h) * c.TPC:(2 - h) * c.TPC]
        m["xT3"] = np.ascontiguousarray(
            np.concatenate([mine, othr], axis=1)).reshape(c.HC, 128, c.TB)
        ak = np.zeros((128, c.NH * c.KC), np.float32)
        p = np.arange(128, dtype=np.float32)
        for hh in range(c.NH):
            sl = slopes[hh]
            for ci in range(c.KC):
                col = hh * c.KC + ci
                if ci < c.KH:
                    kg = h * c.TPC + ci * 128 + p
                    ak[:, col] = sl * kg
                elif h == 1:
                    kg = (ci - c.KH) * 128 + p
                    ak[:, col] = sl * kg
                else:
                    ak[:, col] = DEAD
        m["alik"] = ak
        aq = np.zeros((c.NH, c.TPC), np.float32)
        qg = h * c.TPC + np.arange(c.TPC, dtype=np.float32)
        for hh in range(c.NH):
            aq[hh] = -float(c.D) * slopes[hh] * qg
        m["aliqm"] = np.ascontiguousarray(aq)
        in_maps.append(m)
    return in_maps


class Q:
    def __init__(self, mksem, prefix):
        self.mksem = mksem
        self.prefix = prefix
        self.streams = {}

    def _s(self, stream):
        if stream not in self.streams:
            self.streams[stream] = [self.mksem(f"{self.prefix}_{stream}"), 0]
        return self.streams[stream]

    def dma(self, eng, stream, out, in_, **kw):
        s = self._s(stream)
        eng.dma_start(out=out, in_=in_, **kw).then_inc(s[0], 16)
        s[1] += 16
        return s[1]

    def dmat(self, eng, stream, out, in_):
        s = self._s(stream)
        eng.dma_start(out=out, in_=in_, transpose=True).then_inc(s[0], 16)
        s[1] += 16
        return s[1]

    def tot(self, stream):
        return self._s(stream)

    def w(self, eng, stream):
        s = self._s(stream)
        eng.wait_ge(s[0], s[1])


class E:
    def __init__(self, sem):
        self.sem = sem
        self.n = 0

    def c(self, inst):
        inst.then_inc(self.sem, 1)
        self.n += 1
        return self.n


def chunks(total, w=512):
    return [(i, min(w, total - i)) for i in range(0, total, w)]


def bpieces(lo, hi):
    out = []
    while lo < hi:
        nxt = min(hi, (lo // 512 + 1) * 512)
        out.append((lo, nxt - lo))
        lo = nxt
    return out


def build_program(cfg):
    c = cfg
    nc = bass.Bass("TRN2", target_bir_lowering=False)
    dt = nc.dram_tensor

    xT3 = dt("xT3", [c.HC, 128, c.TB], F32, kind="ExternalInput")
    wq3 = dt("wq3", [c.HC, 128, c.CC * 128], F32, kind="ExternalInput")
    bq = dt("bq", [128, c.CC], F32, kind="ExternalInput")
    aliqm = dt("aliqm", [c.NH, c.TPC], F32, kind="ExternalInput")
    alik = dt("alik", [128, c.NH * c.KC], F32, kind="ExternalInput")
    masku = dt("masku", [128, 128], F32, kind="ExternalInput")
    ones22 = dt("ones22", [2, 128], BF16, kind="ExternalInput")
    onesc = dt("onesc", [128, 1], BF16, kind="ExternalInput")
    onesf = dt("onesf", [128, 1], F32, kind="ExternalInput")
    wp4 = dt("wp4", [c.HC, c.HC, 128, 128], F32, kind="ExternalInput")
    bpc = dt("bpc", [128, c.HC], F32, kind="ExternalInput")
    gcT = dt("gcT", [c.HC, 128, c.I], U8, kind="ExternalInput")
    ucT = dt("ucT", [c.HC, 128, c.I], U8, kind="ExternalInput")
    dcT = dt("dcT", [c.JC, 128, c.DM], U8, kind="ExternalInput")
    gsT = dt("gsT", [c.DM // 64, c.I], F32, kind="ExternalInput")
    usT = dt("usT", [c.DM // 64, c.I], F32, kind="ExternalInput")
    dsT = dt("dsT", [c.I // 64, c.DM], F32, kind="ExternalInput")
    y = dt("y", [c.TPC, c.DM], F32, kind="ExternalOutput")

    xbf_d = dt("xbf_d", [c.HC, 128, c.TB], BF16)
    qkv_sp = dt("qkv_sp", [c.CC * 128, c.TB], BF16)
    q2k2_d = dt("q2k2_d", [2 * c.NH, c.TB], F32)
    frow_d = dt("frow_d", [2, c.TPC], BF16)
    ot_d = dt("ot_d", [c.NH, 128, c.TPC], BF16)
    rrl_d = dt("rrl_d", [2 * c.NH, c.TPC], BF16)
    pres_d = dt("pres_d", [c.HC, 128, c.TPC], F32)
    xlnh_d = dt("xlnh_d", [c.DM, c.TPC], BF16)
    xlnl_d = dt("xlnl_d", [c.DM, c.TPC], BF16)
    abrow_d = dt("abrow_d", [4, c.TPC], BF16)
    wg_deq = dt("wg_deq", [128, c.JC, c.HC, 128], BF16)
    wu_deq = dt("wu_deq", [128, c.JC, c.HC, 128], BF16)
    wd_deq = dt("wd_deq", [c.JC, 128, c.DM], BF16)
    inter_d = dt("inter_d", [c.JC, 128, c.TPC], BF16)
    yd = dt("yd", [c.TPC, c.DM], F32)

    st = contextlib.ExitStack()
    with st:
        def sbin(stack, name, shape, dtt):
            return stack.enter_context(nc.sbuf_tensor(name, shape, dtt))

        def sb(name, shape, dtt):
            return st.enter_context(nc.sbuf_tensor(name, shape, dtt))

        def psum(name, shape, dtt=F32):
            return st.enter_context(nc.psum_tensor(name, shape, dtt))

        def sem(name):
            return st.enter_context(nc.semaphore(name))

        ones22_s = sb("ones22_s", [2, 128], BF16)
        onesc_s = sb("onesc_s", [128, 1], BF16)
        onescf_s = sb("onescf_s", [128, 1], F32)
        masku_s = sb("masku_s", [128, 128], F32)
        rows_s = sb("rows_s", [128, 16], F32)
        bq_s = sb("bq_s", [128, c.CC], F32)
        alik_s = sb("alik_s", [128, c.NH * c.KC], F32)
        aliq_sb = sb("aliq_sb", [c.NH, c.TPC], F32)
        bpc_s = sb("bpc_s", [128, c.HC], F32)
        rwm = sb("rwm", [c.NH, 1], F32)
        rwf = sb("rwf", [c.NH, c.TB], F32)
        rwfh = sb("rwfh", [c.NH, c.TPC], BF16)
        rwfl = sb("rwfl", [c.NH, c.TPC], BF16)
        q2st = sb("q2st", [1, 2 * c.TSUP], F32)
        psA = psum("psA", [128, 1024])
        psB = psum("psB", [128, 1024])
        psC = psum("psC", [128, 1024])
        psD = psum("psD", [128, 1024])

        tS = sem("tS")
        aS = sem("aS")
        vS = sem("vS")

        block = st.enter_context(nc.Block())
        qsy = Q(sem, "sy")
        qsc = Q(sem, "sc")
        qgp = Q(sem, "gp")
        pe = E(tS)
        act = E(aS)
        dve = E(vS)

        SY, SC, GP, VE, TE = (block.sync, block.scalar, block.gpsimd,
                              block.vector, block.tensor)
        rt = {}
        last_mm = [None]

        def MM(t, *a, **kw):
            last_mm[0] = t.matmul(*a, **kw)
            return last_mm[0]

        def pemark():
            return pe.c(last_mm[0])

        DQW = c.DQW
        st_dq = contextlib.ExitStack()
        dq_c = sbin(st_dq, "dq_c", [128, 2 * DQW], F16)
        dq_ca = sbin(st_dq, "dq_ca", [128, DQW], F16)
        dq_tb = [sbin(st_dq, f"dq_tb{i}", [128, DQW], F16) for i in range(2)]
        dq_m = sbin(st_dq, "dq_m", [128, DQW], F16)
        dq_x = sbin(st_dq, "dq_x", [128, DQW], F16)
        dq_l7 = sbin(st_dq, "dq_l7", [128, DQW], F16)
        dq_t = sbin(st_dq, "dq_t", [128, DQW], F16)
        dq_s = sbin(st_dq, "dq_s", [128, 2 * DQW], F16)
        dq_w = sbin(st_dq, "dq_w", [128, 2 * DQW], BF16)

        dq_tiles = []
        for colt in range(c.NCGU):
            col = colt * DQW
            w = min(DQW, c.I - col)
            for mat in ("g", "u"):
                for hc in range(c.HC):
                    dq_tiles.append((mat, hc, col, w))
        for colt in range(c.NCD):
            for jc in range(c.JC):
                dq_tiles.append(("d", jc, colt * DQW, DQW))
        NDQ = len(dq_tiles)
        NGU_T = 2 * c.HC * c.NCGU
        dq_vd = {}
        dq_spd = {}
        dq_bl = {}
        dq_bcons = {}
        dq_next = [0]

        def dq_emit_one(i):
            mat, a, col, w = dq_tiles[i]
            cs = i % 2
            cw = slice(cs * DQW, cs * DQW + w)
            ww = slice(0, w)

            def _ld(g):
                if i >= 2:
                    g.wait_ge(vS, dq_vd[i - 2])
                src, srows = {"g": (gcT, gsT), "u": (ucT, usT),
                              "d": (dcT, dsT)}[mat]
                qgp.dma(g, f"dqc{i % 2}", dq_c[:, cw], src[a, :, col:col + w])
                qgp.dma(g, f"dqc{i % 2}", dq_s[0:64, cw],
                        srows[2 * a:2 * a + 1, col:col + w]
                        .partition_broadcast(64))
                qgp.dma(g, f"dqc{i % 2}", dq_s[64:128, cw],
                        srows[2 * a + 1:2 * a + 2, col:col + w]
                        .partition_broadcast(64))
            GP(_ld)

            BSEG = [8, 9, 10, 11, 12, 13, 14, 7]

            def _bl01(s):
                qgp.w(s, f"dqc{i % 2}")
                for k in (0, 1):
                    if i >= 1:
                        s.wait_ge(vS, dq_bcons[(i - 1, 6 + k)])
                    act.c(s.activation(dq_tb[k % 2][:, ww], dq_c[:, cw],
                                       AF.Copy, bias=NFD0[BSEG[k]],
                                       scale=NFG[BSEG[k]]))
                    dq_bl[(i, k)] = act.n
            SC(_bl01)

            def _treeA(v):
                qgp.w(v, f"dqc{i % 2}")
                if i >= 2:
                    qgp.w(v, f"dqw{i % 2}")
                cc_ = dq_c[:, cw]
                ca_ = dq_ca[:, ww]
                m_, x_, l7_, t_ = (dq_m[:, ww], dq_x[:, ww],
                                   dq_l7[:, ww], dq_t[:, ww])
                dve.c(v.tensor_scalar(ca_, cc_, -DQ_CA, None, ALU.add))
                v.wait_ge(vS, dve.n)
                dve.c(v.tensor_scalar(l7_, ca_, NFG[7], NFDA[7],
                                      ALU.mult, ALU.add))
                dve.c(v.tensor_scalar(m_, ca_, NFG[0], NFDA[0],
                                      ALU.mult, ALU.add))
                v.wait_ge(vS, dve.n)
                for i2 in range(1, 7):
                    dve.c(v.tensor_scalar(t_, ca_, NFG[i2], NFDA[i2],
                                          ALU.mult, ALU.add))
                    v.wait_ge(vS, dve.n)
                    dve.c(v.tensor_tensor(m_, m_, t_, ALU.min))
                    v.wait_ge(vS, dve.n)
                dve.c(v.tensor_tensor(m_, m_, l7_, ALU.min))
                v.wait_ge(aS, dq_bl[(i, 1)])
                dve.c(v.tensor_tensor(x_, dq_tb[0][:, ww], dq_tb[1][:, ww],
                                      ALU.max))
                dq_bcons[(i, 0)] = dve.n
                dq_bcons[(i, 1)] = dve.n
            VE(_treeA)

            for k in range(2, 8):
                def _blk(s, k=k):
                    s.wait_ge(vS, dq_bcons[(i, k - 2)])
                    act.c(s.activation(dq_tb[k % 2][:, ww], dq_c[:, cw],
                                       AF.Copy, bias=NFD0[BSEG[k]],
                                       scale=NFG[BSEG[k]]))
                    dq_bl[(i, k)] = act.n
                SC(_blk)

                def _vek(v, k=k):
                    x_ = dq_x[:, ww]
                    v.wait_ge(vS, dq_bcons[(i, k - 1)])
                    v.wait_ge(aS, dq_bl[(i, k)])
                    dve.c(v.tensor_tensor(x_, x_, dq_tb[k % 2][:, ww],
                                          ALU.max))
                    dq_bcons[(i, k)] = dve.n
                VE(_vek)

            def _treeC(v):
                m_, x_, l7_ = dq_m[:, ww], dq_x[:, ww], dq_l7[:, ww]
                v.wait_ge(vS, dq_bcons[(i, 7)])
                dve.c(v.tensor_tensor(m_, m_, x_, ALU.add))
                v.wait_ge(vS, dve.n)
                dve.c(v.tensor_tensor(m_, m_, l7_, ALU.subtract))
                v.wait_ge(vS, dve.n)
                dve.c(v.tensor_tensor(dq_w[:, cw], m_, dq_s[:, cw], ALU.mult))
                dq_vd[i] = dve.n
            VE(_treeC)

            def _sp(g):
                g.wait_ge(vS, dq_vd[i])
                if mat in ("g", "u"):
                    dst = wg_deq if mat == "g" else wu_deq
                    tot = qgp.dma(g, f"dqw{i % 2}",
                                  dst[:, col // 128:(col + w) // 128, a, :],
                                  dq_w[:, cs * DQW:cs * DQW + w]
                                  .rearrange("p (j q) -> p j q", q=128))
                else:
                    tot = qgp.dma(g, f"dqw{i % 2}", wd_deq[a, :, col:col + w],
                                  dq_w[:, cw])
                dq_spd[i] = (f"dqw{i % 2}", tot)
            GP(_sp)

        def dq_emit_some(k):
            e = min(dq_next[0] + k, NDQ)
            for i in range(dq_next[0], e):
                dq_emit_one(i)
            dq_next[0] = e

        def dq_wait_spilled(eng, upto_i):
            tots = {}
            for i in range(upto_i + 1):
                s, t = dq_spd[i]
                tots[s] = max(tots.get(s, 0), t)
            for s, t in tots.items():
                eng.wait_ge(qgp.tot(s)[0], t)

        def _p0(g):
            qgp.dma(g, "const", ones22_s[:, :], ones22[:, :])
            qgp.dma(g, "const", onesc_s[:, :], onesc[:, :])
            qgp.dma(g, "const", onescf_s[:, :], onesf[:, :])
            qgp.dma(g, "const", masku_s[:, :], masku[:, :])
            qgp.dma(g, "const", bq_s[:, :], bq[:, :])
            qgp.dma(g, "const", alik_s[:, :], alik[:, :])
            qgp.dma(g, "const", aliq_sb[:, :], aliqm[:, :])
            qgp.dma(g, "const", bpc_s[:, :], bpc[:, :])
        GP(_p0)

        st_p1a = contextlib.ExitStack()
        xsb = [sbin(st_p1a, f"xsb{i}", [128, c.HC * c.TSUP], BF16)
               for i in range(2)]
        wq_sb = sbin(st_p1a, "wq_sb", [128, c.HC * c.CCH * 128], BF16)
        qe_sb = sbin(st_p1a, "qe_sb", [128, 4 * c.TSUP], BF16)
        sqe_sb = sbin(st_p1a, "sqe_sb", [128, 2 * c.TSUP], BF16)
        xc_sb = [sbin(st_p1a, f"xc{i}", [128, c.TB], BF16) for i in range(2)]

        xcast_done = {}
        for hc in range(c.HC):
            def _xc(g, hc=hc):
                sl = hc % 2
                if hc >= 2:
                    qgp.w(g, f"xcb{sl}")
                qgp.dma(g, f"xca{sl}", xc_sb[sl][:, :], xT3[hc, :, :])
                qgp.w(g, f"xca{sl}")
                tot = qgp.dma(g, f"xcb{sl}", xbf_d[hc, :, :], xc_sb[sl][:, :])
                xcast_done[hc] = tot
            GP(_xc)

        x_ld = {}
        p1a_mm = {}
        p1a_ev = {}
        p1a_sq = {}
        p1a_q2mm = {}
        p1a_q2ev = {}
        nsq = [0]

        def p1a_wqload(ps):
            def _wq(g, ps=ps):
                if ps >= 1:
                    g.wait_ge(tS, p1a_mm[(ps * c.NTS - 1, c.CCH - 1)])
                nq = c.HC // 4
                for qtr in range(4):
                    qgp.dma(g, "wqp", wq_sb[:, qtr * nq * c.CCH * 128:
                                            (qtr + 1) * nq * c.CCH * 128]
                            .rearrange("p (h q) -> p h q", q=c.CCH * 128),
                            wq3[qtr * nq:(qtr + 1) * nq, :,
                                ps * c.CCH * 128:(ps + 1) * c.CCH * 128]
                            .rearrange("h p q -> p h q"))
            GP(_wq)

        def p1a_load(ts2):
            def _l(g, ts2=ts2):
                if ts2 == 0:
                    qgp.w(g, "xcb0")
                    qgp.w(g, "xcb1")
                if ts2 >= 2:
                    g.wait_ge(tS, p1a_mm[(ts2 - 2, c.CCH - 1)])
                nq4 = c.HC // 4
                ts = ts2 % c.NTS
                for q4 in range(4):
                    qgp.dma(g, f"xs{ts2 % 2}",
                            xsb[ts2 % 2][:, q4 * nq4 * c.TSUP:
                                         (q4 + 1) * nq4 * c.TSUP]
                            .rearrange("p (h t) -> p h t", t=c.TSUP),
                            xbf_d[q4 * nq4:(q4 + 1) * nq4, :,
                                  ts * c.TSUP:(ts + 1) * c.TSUP]
                            .rearrange("h p t -> p h t"))
            GP(_l)

        p1a_wqload(0)
        p1a_load(0)
        p1a_load(1)
        for ts2 in range(c.NPASS * c.NTS):
            ps = ts2 // c.NTS
            ts = ts2 % c.NTS
            for ccl in range(c.CCH):
                cc = ps * c.CCH + ccl
                idx = ts2 * c.CCH + ccl
                slot = idx % 4
                pp = (psA, psB)[slot // 2]
                col = (slot % 2) * 512

                def _mm(t, ts2=ts2, ccl=ccl, cc=cc, idx=idx, pp=pp, col=col):
                    if ccl == 0:
                        qgp.w(t, f"xs{ts2 % 2}")
                        if ts2 == 0:
                            qgp.w(t, "const")
                        if ts2 % c.NTS == 0:
                            qgp.w(t, "wqp")
                    if idx >= 4:
                        t.wait_ge(aS, p1a_ev[idx - 4])
                    for hc in range(c.HC):
                        MM(t, pp[:, col:col + c.TSUP],
                           wq_sb[:, (hc * c.CCH + ccl) * 128:
                                 (hc * c.CCH + ccl + 1) * 128],
                           xsb[ts2 % 2][:, hc * c.TSUP:(hc + 1) * c.TSUP],
                           start=(hc == 0), stop=(hc == c.HC - 1))
                    pemark()
                    p1a_mm[(ts2, ccl)] = pe.n
                TE(_mm)

                def _ev(s, ts2=ts2, ts=ts, cc=cc, idx=idx, pp=pp, col=col):
                    es = idx % 4
                    s.wait_ge(tS, p1a_mm[(ts2, cc % c.CCH)])
                    if idx >= 4:
                        qsc.w(s, f"sp{idx % 4}")
                    act.c(s.activation(qe_sb[:, es * c.TSUP:(es + 1) * c.TSUP],
                                       pp[:, col:col + c.TSUP],
                                       AF.Identity, bias=bq_s[:, cc:cc + 1]))
                    p1a_ev[idx] = act.n
                    s.wait_ge(aS, p1a_ev[idx])
                    qsc.dma(s, f"sp{idx % 4}",
                            qkv_sp[cc * 128:(cc + 1) * 128,
                                   ts * c.TSUP:(ts + 1) * c.TSUP],
                            qe_sb[:, es * c.TSUP:(es + 1) * c.TSUP])
                SC(_ev)

                if cc % 3 < 2:
                    si = nsq[0]
                    nsq[0] += 1

                    def _sq(s, idx=idx, si=si):
                        ss = si % 2
                        s.wait_ge(aS, p1a_ev[idx])
                        if si >= 2:
                            s.wait_ge(tS, p1a_q2mm[si - 2])
                        act.c(s.activation(
                            sqe_sb[:, ss * c.TSUP:(ss + 1) * c.TSUP],
                            qe_sb[:, (idx % 4) * c.TSUP:
                                  (idx % 4 + 1) * c.TSUP], AF.Square))
                        p1a_sq[si] = act.n
                    SC(_sq)

                    def _q2(t, si=si):
                        ss = si % 2
                        t.wait_ge(aS, p1a_sq[si])
                        if si >= 2:
                            t.wait_ge(vS, p1a_q2ev[si - 2])
                        MM(t, psC[0:1, ss * 512:ss * 512 + c.TSUP],
                           onesc_s[:, :],
                           sqe_sb[:, ss * c.TSUP:(ss + 1) * c.TSUP],
                           start=True, stop=True)
                        pemark()
                        p1a_q2mm[si] = pe.n
                    TE(_q2)

                    def _q2e(v, si=si):
                        ss = si % 2
                        v.wait_ge(tS, p1a_q2mm[si])
                        if si >= 2:
                            qgp.w(v, "q2st")
                        dve.c(v.tensor_copy(
                            q2st[0:1, ss * c.TSUP:(ss + 1) * c.TSUP],
                            psC[0:1, ss * 512:ss * 512 + c.TSUP]))
                        p1a_q2ev[si] = dve.n
                    VE(_q2e)

                    def _q2w(g, ts=ts, cc=cc, si=si):
                        ss = si % 2
                        g.wait_ge(vS, p1a_q2ev[si])
                        g3 = cc % 3
                        hh = cc // 3
                        row = hh if g3 == 0 else c.NH + hh
                        scol = ts * c.TSUP
                        qgp.dma(g, "q2st", q2k2_d[row:row + 1,
                                                  scol:scol + c.TSUP],
                                q2st[0:1, ss * c.TSUP:(ss + 1) * c.TSUP])
                    GP(_q2w)

            if ts2 + 2 < c.NPASS * c.NTS:
                p1a_load(ts2 + 2)
            if ts2 % c.NTS == c.NTS - 1 and ts2 // c.NTS + 1 < c.NPASS:
                p1a_wqload(ts2 // c.NTS + 1)
            dq_emit_some(2)

        if os.environ.get("MK_UPTO") == "p1a":
            def _fin0(sy):
                qsc.w(sy, "sp0")
                qsc.w(sy, "sp1")
                qsc.w(sy, "sp2")
                qsc.w(sy, "sp3")
            SY(_fin0)
            st_p1a.close()
            st_dq.close()
            return nc

        def _rows_gp(g):
            g.wait_ge(vS, p1a_q2ev[nsq[0] - 1])
            qgp.w(g, "q2st")
            qgp.dma(g, "rows", rwf[:, :], q2k2_d[c.NH:2 * c.NH, :])
            rt["rows_k2"] = True
        GP(_rows_gp)

        def _rows_v(v):
            qgp.w(v, "rows")
            dve.c(v.tensor_reduce(rwm[:, :], rwf[:, :], AX, ALU.max))
            rt["rows_max"] = dve.n
        VE(_rows_v)

        def _rows_g2(g):
            g.wait_ge(vS, rt["rows_max"])
            qgp.dma(g, "rows", rwf[:, 0:c.TPC], q2k2_d[0:c.NH, 0:c.TPC])
        GP(_rows_g2)

        def _rows_v1(v):
            qgp.w(v, "rows")
            dve.c(v.tensor_scalar(rwf[:, 0:c.TPC], rwf[:, 0:c.TPC],
                                  rwm[:, 0:1], None, ALU.mult))
            v.wait_ge(vS, dve.n)
            dve.c(v.tensor_scalar(rwf[:, 0:c.TPC], rwf[:, 0:c.TPC], 1e-12,
                                  None, ALU.max))
            rt["rows_mul"] = dve.n
        VE(_rows_v1)

        def _rows_s(s):
            s.wait_ge(vS, rt["rows_mul"])
            act.c(s.activation(rwf[:, 0:c.TPC], rwf[:, 0:c.TPC], AF.Sqrt))
            rt["rows_sqrt"] = act.n
        SC(_rows_s)

        def _rows_v2(v):
            v.wait_ge(aS, rt["rows_sqrt"])
            dve.c(v.scalar_tensor_tensor(rwf[:, 0:c.TPC], rwf[:, 0:c.TPC],
                                         -1.0, aliq_sb[:, :], ALU.mult,
                                         ALU.add))
            v.wait_ge(vS, dve.n)
            dve.c(v.tensor_copy(rwfh[:, :], rwf[:, 0:c.TPC]))
            v.wait_ge(vS, dve.n)
            dve.c(v.tensor_tensor(rwfl[:, :], rwf[:, 0:c.TPC], rwfh[:, :],
                                  ALU.subtract))
            rt["rows_done"] = dve.n
        VE(_rows_v2)

        if os.environ.get("MK_UPTO") == "rows":
            def _fin0(sy):
                sy.wait_ge(vS, rt["rows_done"])
            SY(_fin0)
            st_p1a.close()
            st_dq.close()
            return nc

        st_p1a.close()
        st_p1b = contextlib.ExitStack()
        qt_sb = [sbin(st_p1b, f"qt{i}", [128, c.TPC], BF16) for i in range(2)]
        kt_sb = [sbin(st_p1b, f"kt{i}", [128, c.TB], BF16) for i in range(2)]
        vn_sb = [sbin(st_p1b, f"vn{i}", [128, c.TB], BF16) for i in range(2)]
        pt_sb = sbin(st_p1b, "pt_sb", [128, c.KC * c.TPC], BF16)
        frow_s = sbin(st_p1b, "frow_s", [2, c.TPC], BF16)
        rrow_s = sbin(st_p1b, "rrow_s", [1, 2 * c.TPC], F32)
        rrbf_s = sbin(st_p1b, "rrbf_s", [1, 2 * c.TPC], BF16)
        ot_f = sbin(st_p1b, "ot_f", [128, c.TPC], F32)
        ot_b = sbin(st_p1b, "ot_b", [128, c.TPC], BF16)

        p1b_smm = {}
        p1b_mask = {}
        p1b_exp = {}
        p1b_av = {}
        p1b_otf = {}
        p1b_otb = {}
        p1b_rr = {}
        p1b_done_pe = {}

        def qstart(ci):
            return 128 * ci if ci < c.KH else 0

        for hh in range(c.NH):
            idx = hh
            buf = idx % 2

            def _ld(sy, idx=idx, hh=hh, buf=buf):
                if idx == 0:
                    for kq in range(4):
                        qsc.w(sy, f"sp{kq}")
                    qgp.w(sy, "rows")
                    qgp.w(sy, "xcb0")
                    qgp.w(sy, "xcb1")
                    qgp.w(sy, "q2st")
                if idx >= 2:
                    sy.wait_ge(tS, p1b_done_pe[idx - 2])
                r0 = (3 * hh) * 128
                qsy.dma(sy, f"qk{buf}", qt_sb[buf][:, :],
                        qkv_sp[r0:r0 + 128, 0:c.TPC])
                qsy.dma(sy, f"qk{buf}", kt_sb[buf][:, :],
                        qkv_sp[r0 + 128:r0 + 256, :])
                for kc in range(c.KC):
                    qsy.dmat(sy, f"qk{buf}",
                             vn_sb[buf][:, kc * 128:(kc + 1) * 128],
                             qkv_sp[r0 + 256:r0 + 384,
                                    kc * 128:(kc + 1) * 128])
            SY(_ld)

            def _frow(g, idx=idx, hh=hh):
                if idx == 0:
                    g.wait_ge(vS, rt["rows_done"])
                if idx >= 1:
                    qgp.w(g, "frow")
                qgp.dma(g, "frow", frow_d[0:1, :], rwfh[hh:hh + 1, :])
                qgp.dma(g, "frow", frow_d[1:2, :], rwfl[hh:hh + 1, :])
                qgp.w(g, "frow")
                if idx >= 1:
                    g.wait_ge(tS, p1b_done_pe[idx - 1])
                qgp.dma(g, "frow", frow_s[:, :], frow_d[:, :])
            GP(_frow)

            for kci in range(c.KC):
                qs = qstart(kci)
                wdt = c.TPC - qs

                def _smm1(t, idx=idx, hh=hh, buf=buf, kci=kci, qs=qs,
                          wdt=wdt):
                    if kci == 0:
                        qsy.w(t, f"qk{buf}")
                        qgp.w(t, "frow")
                        if idx >= 1:
                            t.wait_ge(aS, p1b_otf[idx - 1])
                            t.wait_ge(vS, p1b_rr[idx - 1])
                    sp = (psA, psB)[kci % 2]
                    if kci >= 2:
                        t.wait_ge(aS, p1b_exp[(idx, kci - 2)])
                    for p0, pw in bpieces(qs, c.TPC):
                        cs = slice(p0, p0 + pw)
                        MM(t, sp[:, cs],
                           kt_sb[buf][:, kci * 128:(kci + 1) * 128],
                           qt_sb[buf][:, p0:p0 + pw],
                           start=True, stop=False)
                        MM(t, sp[:, cs], ones22_s[:, :],
                           frow_s[:, p0:p0 + pw],
                           start=False, stop=True)
                    pemark()
                    p1b_smm[(idx, kci)] = pe.n
                TE(_smm1)

                if kci < c.KH:
                    def _mask1(v, idx=idx, kci=kci, qs=qs):
                        sp = (psA, psB)[kci % 2]
                        v.wait_ge(tS, p1b_smm[(idx, kci)])
                        dve.c(v.tensor_tensor(sp[:, qs:qs + 128],
                                              sp[:, qs:qs + 128],
                                              masku_s[:, :], ALU.add))
                        p1b_mask[(idx, kci)] = dve.n
                    VE(_mask1)

                def _exp1(s, idx=idx, hh=hh, kci=kci, qs=qs):
                    sp = (psA, psB)[kci % 2]
                    s.wait_ge(tS, p1b_smm[(idx, kci)])
                    if (idx, kci) in p1b_mask:
                        s.wait_ge(vS, p1b_mask[(idx, kci)])
                    if idx >= 1 and kci < 2:
                        s.wait_ge(tS, p1b_av[idx - 1])
                    act.c(s.activation(
                        pt_sb[:, kci * c.TPC + qs:(kci + 1) * c.TPC],
                        sp[:, qs:c.TPC], AF.Exp,
                        bias=alik_s[:, hh * c.KC + kci:hh * c.KC + kci + 1],
                        scale=c.SCALE))
                    p1b_exp[(idx, kci)] = act.n
                SC(_exp1)

            def _rsav(t, idx=idx, buf=buf):
                for kci in range(c.KC):
                    qs = qstart(kci)
                    t.wait_ge(aS, p1b_exp[(idx, kci)])
                    first = kci == 0
                    last = kci == c.KC - 1
                    for p0, pw in bpieces(qs, c.TPC):
                        pts = pt_sb[:, kci * c.TPC + p0:
                                    kci * c.TPC + p0 + pw]
                        cs = slice(p0, p0 + pw)
                        MM(t, psD[0:1, cs], onesc_s[:, :], pts,
                           start=first, stop=last)
                        MM(t, psC[:, cs],
                           vn_sb[buf][:, kci * 128:(kci + 1) * 128],
                           pts, start=first, stop=last)
                pemark()
                p1b_av[idx] = pe.n
            TE(_rsav)

            def _rr(v, idx=idx):
                v.wait_ge(tS, p1b_av[idx])
                if idx >= 1:
                    qsc.w(v, "otw")
                dve.c(v.reciprocal(rrow_s[0:1, 0:c.TPC], psD[0:1, 0:c.TPC]))
                v.wait_ge(vS, dve.n)
                dve.c(v.tensor_copy(rrbf_s[0:1, 0:c.TPC],
                                    rrow_s[0:1, 0:c.TPC]))
                v.wait_ge(vS, dve.n)
                dve.c(v.tensor_tensor(rrow_s[0:1, c.TPC:2 * c.TPC],
                                      rrow_s[0:1, 0:c.TPC],
                                      rrbf_s[0:1, 0:c.TPC], ALU.subtract))
                v.wait_ge(vS, dve.n)
                dve.c(v.tensor_copy(rrbf_s[0:1, c.TPC:2 * c.TPC],
                                    rrow_s[0:1, c.TPC:2 * c.TPC]))
                p1b_rr[idx] = dve.n
            VE(_rr)

            def _otf(s, idx=idx):
                s.wait_ge(tS, p1b_av[idx])
                if idx >= 1:
                    s.wait_ge(vS, p1b_otb[idx - 1])
                act.c(s.copy(ot_f[:, :], psC[:, 0:c.TPC]))
                p1b_otf[idx] = act.n
            SC(_otf)

            def _otb(v, idx=idx):
                v.wait_ge(aS, p1b_otf[idx])
                if idx >= 1:
                    qsc.w(v, "otw")
                dve.c(v.tensor_copy(ot_b[:, :], ot_f[:, :]))
                p1b_otb[idx] = dve.n
            VE(_otb)

            def _otw(s, idx=idx, hh=hh):
                s.wait_ge(vS, p1b_otb[idx])
                qsc.dma(s, "otw", ot_d[hh, :, :], ot_b[:, :])
                s.wait_ge(vS, p1b_rr[idx])
                qsc.dma(s, "otw", rrl_d[2 * hh:2 * hh + 1, :],
                        rrbf_s[0:1, 0:c.TPC])
                qsc.dma(s, "otw", rrl_d[2 * hh + 1:2 * hh + 2, :],
                        rrbf_s[0:1, c.TPC:2 * c.TPC])
            SC(_otw)

            p1b_done_pe[idx] = pe.n
            dq_emit_some(2)

        if os.environ.get("MK_UPTO") == "p1b":
            def _fin0(sy):
                qsc.w(sy, "otw")
            SY(_fin0)
            st_p1b.close()
            st_dq.close()
            return nc

        st_p1b.close()
        st_p2 = contextlib.ExitStack()
        otm_sb = sbin(st_p2, "otm_sb", [128, c.NH * c.TPC], BF16)
        rr2_sb = sbin(st_p2, "rr2_sb", [2, 2 * c.TPC], BF16)
        wpl_sb = [sbin(st_p2, f"wpl{i}", [128, c.HC * 128], BF16)
                  for i in range(2)]
        xrt_sb = [sbin(st_p2, f"xrt{i}", [128, c.TPC], F32) for i in range(2)]
        rch_sb = sbin(st_p2, "rch_sb", [128, 2 * c.TPC], F32)
        sqp_sb = sbin(st_p2, "sqp_sb", [128, c.TPC], F32)
        ab_sb = sbin(st_p2, "ab_sb", [2, 2 * c.TPC], BF16)
        xlh_sb = sbin(st_p2, "xlh_sb", [128, 2 * c.TPC], BF16)
        xll_sb = sbin(st_p2, "xll_sb", [128, 2 * c.TPC], BF16)

        p2_norm_mm = {}
        p2_norm = {}
        p2_mm = {}
        p2_ev = {}
        p2_row = {}

        def _p2_ld(g):
            qsc.w(g, "otw")
            g.wait_ge(tS, p1b_done_pe[c.NH - 1])
            qsy.w(g, "qk0")
            qsy.w(g, "qk1")
            qgp.w(g, "frow")
            nq4 = c.NH // 4
            for q4 in range(4):
                qgp.dma(g, "p2ld",
                        otm_sb[:, q4 * nq4 * c.TPC:(q4 + 1) * nq4 * c.TPC]
                        .rearrange("p (h t) -> p h t", t=c.TPC),
                        ot_d[q4 * nq4:(q4 + 1) * nq4, :, :]
                        .rearrange("h p t -> p h t"))
        GP(_p2_ld)

        for hc in range(c.NH):
            def _nld(g, hc=hc):
                if hc == 0:
                    qgp.w(g, "p2ld")
                if hc >= 2:
                    g.wait_ge(tS, p2_norm_mm[hc - 2])
                qgp.dma(g, f"rr{hc % 2}",
                        rr2_sb[:, (hc % 2) * c.TPC:(hc % 2 + 1) * c.TPC],
                        rrl_d[2 * hc:2 * hc + 2, :])
            GP(_nld)

            def _nmm(t, hc=hc):
                qgp.w(t, f"rr{hc % 2}")
                pp = (psA, psB)[hc % 2]
                if hc >= 2:
                    t.wait_ge(vS, p2_norm[hc - 2])
                for h0, hw in chunks(c.TPC):
                    MM(t, pp[:, h0:h0 + hw], ones22_s[:, :],
                       rr2_sb[:, (hc % 2) * c.TPC + h0:
                              (hc % 2) * c.TPC + h0 + hw],
                       start=True, stop=True)
                pemark()
                p2_norm_mm[hc] = pe.n
            TE(_nmm)

            def _nmul(v, hc=hc):
                v.wait_ge(tS, p2_norm_mm[hc])
                pp = (psA, psB)[hc % 2]
                dve.c(v.tensor_tensor(otm_sb[:, hc * c.TPC:(hc + 1) * c.TPC],
                                      otm_sb[:, hc * c.TPC:(hc + 1) * c.TPC],
                                      pp[:, 0:c.TPC], ALU.mult))
                p2_norm[hc] = dve.n
            VE(_nmul)
            dq_emit_some(1)

        def p2_wld(oc):
            def _w(g, oc=oc):
                if oc >= 2:
                    g.wait_ge(tS, p2_mm[oc - 2])
                qgp.dma(g, f"wp{oc % 2}", wpl_sb[oc % 2][:, :]
                        .rearrange("p (h q) -> p h q", q=128),
                        wp4[oc, :, :, :].rearrange("h p q -> p h q"))
            GP(_w)

        def p2_xld(oc):
            def _x(sy, oc=oc):
                if oc < 2:
                    sy.wait_ge(tS, p1b_done_pe[c.NH - 1])
                    sy.wait_ge(aS, p1b_otf[c.NH - 1])
                    sy.wait_ge(vS, p1b_rr[c.NH - 1])
                    sy.wait_ge(vS, p1b_otb[c.NH - 1])
                    qsc.w(sy, "otw")
                if oc >= 2:
                    sy.wait_ge(vS, p2_ev[(oc - 2, "res")])
                qsy.dma(sy, f"xr{oc % 2}", xrt_sb[oc % 2][:, :],
                        xT3[oc, :, 0:c.TPC])
            SY(_x)

        p2_wld(0)
        p2_wld(1)
        p2_xld(0)
        p2_xld(1)
        for oc in range(c.HC):
            def _pmm(t, oc=oc):
                pp = (psA, psB)[oc % 2]
                qgp.w(t, f"wp{oc % 2}")
                if oc == 0:
                    t.wait_ge(vS, p2_norm[c.NH - 1])
                if oc >= 2:
                    t.wait_ge(aS, p2_ev[(oc - 2, "copy")])
                for hc in range(c.NH):
                    for h0, hw in chunks(c.TPC):
                        MM(t, pp[:, h0:h0 + hw],
                           wpl_sb[oc % 2][:, hc * 128:(hc + 1) * 128],
                           otm_sb[:, hc * c.TPC + h0:hc * c.TPC + h0 + hw],
                           start=(hc == 0), stop=(hc == c.NH - 1))
                pemark()
                p2_mm[oc] = pe.n
            TE(_pmm)
            if oc + 2 < c.HC:
                p2_wld(oc + 2)

            def _pev(s, oc=oc):
                pp = (psA, psB)[oc % 2]
                es = oc % 2
                s.wait_ge(tS, p2_mm[oc])
                if oc >= 2:
                    qsc.w(s, f"prest{oc % 2}")
                act.c(s.activation(rch_sb[:, es * c.TPC:(es + 1) * c.TPC],
                                   pp[:, 0:c.TPC], AF.Identity,
                                   bias=bpc_s[:, oc:oc + 1]))
                p2_ev[(oc, "copy")] = act.n
            SC(_pev)

            def _pres(v, oc=oc):
                es = oc % 2
                rr = rch_sb[:, es * c.TPC:(es + 1) * c.TPC]
                v.wait_ge(aS, p2_ev[(oc, "copy")])
                qsy.w(v, f"xr{oc % 2}")
                dve.c(v.tensor_tensor(rr, rr, xrt_sb[oc % 2][:, :], ALU.add))
                p2_ev[(oc, "res")] = dve.n
            VE(_pres)
            if oc + 2 < c.HC:
                p2_xld(oc + 2)

            def _psq(s, oc=oc):
                es = oc % 2
                s.wait_ge(vS, p2_ev[(oc, "res")])
                act.c(s.activation(sqp_sb[:, :],
                                   rch_sb[:, es * c.TPC:(es + 1) * c.TPC],
                                   AF.Square))
                p2_ev[(oc, "sq")] = act.n
            SC(_psq)

            def _prow(t, oc=oc):
                es = oc % 2
                t.wait_ge(vS, p2_ev[(oc, "res")])
                for h0, hw in chunks(c.TPC):
                    MM(t, psC[0:1, h0:h0 + hw], onescf_s[:, :],
                       rch_sb[:, es * c.TPC + h0:es * c.TPC + h0 + hw],
                       start=(oc == 0), stop=(oc == c.HC - 1))
                t.wait_ge(aS, p2_ev[(oc, "sq")])
                for h0, hw in chunks(c.TPC):
                    MM(t, psD[0:1, h0:h0 + hw], onescf_s[:, :],
                       sqp_sb[:, h0:h0 + hw],
                       start=(oc == 0), stop=(oc == c.HC - 1))
                pemark()
                p2_row[oc] = pe.n
            TE(_prow)

            def _pst(s, oc=oc):
                es = oc % 2
                s.wait_ge(tS, p2_row[oc])
                qsc.dma(s, f"prest{oc % 2}", pres_d[oc, :, :],
                        rch_sb[:, es * c.TPC:(es + 1) * c.TPC])
            SC(_pst)

            dq_emit_some(1)

        lnr_s = sbin(st_p2, "lnr_s", [1, 8 * c.TPC], F32)

        def _l1r(v):
            v.wait_ge(tS, p2_row[c.HC - 1])
            dve.c(v.tensor_scalar(lnr_s[0:1, 0:c.TPC], psC[0:1, 0:c.TPC],
                                  1.0 / c.DM, None, ALU.mult))
            dve.c(v.tensor_scalar(lnr_s[0:1, c.TPC:2 * c.TPC],
                                  psD[0:1, 0:c.TPC], 1.0 / c.DM, None,
                                  ALU.mult))
            v.wait_ge(vS, dve.n)
            dve.c(v.tensor_tensor(lnr_s[0:1, 2 * c.TPC:3 * c.TPC],
                                  lnr_s[0:1, 0:c.TPC], lnr_s[0:1, 0:c.TPC],
                                  ALU.mult))
            v.wait_ge(vS, dve.n)
            dve.c(v.tensor_tensor(lnr_s[0:1, c.TPC:2 * c.TPC],
                                  lnr_s[0:1, c.TPC:2 * c.TPC],
                                  lnr_s[0:1, 2 * c.TPC:3 * c.TPC],
                                  ALU.subtract))
            v.wait_ge(vS, dve.n)
            dve.c(v.tensor_scalar(lnr_s[0:1, c.TPC:2 * c.TPC],
                                  lnr_s[0:1, c.TPC:2 * c.TPC], 1e-5, None,
                                  ALU.add))
            rt["l1var"] = dve.n
        VE(_l1r)

        def _l1s(s):
            s.wait_ge(vS, rt["l1var"])
            act.c(s.activation(lnr_s[0:1, 3 * c.TPC:4 * c.TPC],
                               lnr_s[0:1, c.TPC:2 * c.TPC], AF.Sqrt))
            rt["l1std"] = act.n
        SC(_l1s)

        def _l1v2(v):
            v.wait_ge(aS, rt["l1std"])
            dve.c(v.reciprocal(lnr_s[0:1, 4 * c.TPC:5 * c.TPC],
                               lnr_s[0:1, 3 * c.TPC:4 * c.TPC]))
            v.wait_ge(vS, dve.n)
            dve.c(v.tensor_tensor(lnr_s[0:1, 5 * c.TPC:6 * c.TPC],
                                  lnr_s[0:1, 0:c.TPC],
                                  lnr_s[0:1, 4 * c.TPC:5 * c.TPC], ALU.mult))
            v.wait_ge(vS, dve.n)
            dve.c(v.tensor_scalar(lnr_s[0:1, 5 * c.TPC:6 * c.TPC],
                                  lnr_s[0:1, 5 * c.TPC:6 * c.TPC], -1.0,
                                  None, ALU.mult))
            rt["l1ab"] = dve.n
        VE(_l1v2)

        def _l1b1(g):
            g.wait_ge(vS, rt["l1ab"])
            qgp.dma(g, "l1b", abrow_d[0:1, :], lnr_s[0:1, 4 * c.TPC:5 * c.TPC])
            qgp.dma(g, "l1b", abrow_d[2:3, :], lnr_s[0:1, 5 * c.TPC:6 * c.TPC])
            qgp.w(g, "l1b")
            qgp.dma(g, "l1b", ab_sb[0:1, 0:c.TPC], abrow_d[0:1, :])
            qgp.dma(g, "l1b", ab_sb[0:1, c.TPC:2 * c.TPC], abrow_d[2:3, :])
        GP(_l1b1)

        def _l1lo(v):
            qgp.w(v, "l1b")
            dve.c(v.tensor_tensor(lnr_s[0:1, 6 * c.TPC:7 * c.TPC],
                                  lnr_s[0:1, 4 * c.TPC:5 * c.TPC],
                                  ab_sb[0:1, 0:c.TPC], ALU.subtract))
            dve.c(v.tensor_tensor(lnr_s[0:1, 7 * c.TPC:8 * c.TPC],
                                  lnr_s[0:1, 5 * c.TPC:6 * c.TPC],
                                  ab_sb[0:1, c.TPC:2 * c.TPC], ALU.subtract))
            rt["l1lo"] = dve.n
        VE(_l1lo)

        def _l1b2(g):
            g.wait_ge(vS, rt["l1lo"])
            qgp.dma(g, "l1b", abrow_d[1:2, :], lnr_s[0:1, 6 * c.TPC:7 * c.TPC])
            qgp.dma(g, "l1b", abrow_d[3:4, :], lnr_s[0:1, 7 * c.TPC:8 * c.TPC])
            qgp.w(g, "l1b")
            qgp.dma(g, "l1b", ab_sb[:, 0:c.TPC], abrow_d[0:2, :])
            qgp.dma(g, "l1b", ab_sb[:, c.TPC:2 * c.TPC], abrow_d[2:4, :])
        GP(_l1b2)

        def _l1bc(t):
            qgp.w(t, "l1b")
            for h0, hw in chunks(c.TPC):
                MM(t, psA[:, h0:h0 + hw], ones22_s[:, :],
                   ab_sb[:, h0:h0 + hw], start=True, stop=True)
                MM(t, psB[:, h0:h0 + hw], ones22_s[:, :],
                   ab_sb[:, c.TPC + h0:c.TPC + h0 + hw],
                   start=True, stop=True)
            pemark()
            rt["l1bc"] = pe.n
        TE(_l1bc)

        p2b = {}
        for oc in range(c.HC):
            def _n2ld(sy, oc=oc):
                if oc < 2:
                    qsc.w(sy, f"prest{oc % 2}")
                else:
                    sy.wait_ge(vS, p2b[(oc - 2, "lo")])
                qsy.dma(sy, f"prl{oc % 2}",
                        rch_sb[:, (oc % 2) * c.TPC:(oc % 2 + 1) * c.TPC],
                        pres_d[oc, :, :])
            SY(_n2ld)

            def _n2v(v, oc=oc):
                es = oc % 2
                rr = rch_sb[:, es * c.TPC:(es + 1) * c.TPC]
                qsy.w(v, f"prl{oc % 2}")
                if oc == 0:
                    v.wait_ge(tS, rt["l1bc"])
                if oc >= 2:
                    qsc.w(v, f"xst{oc % 2}")
                dve.c(v.tensor_tensor(rr, rr, psA[:, 0:c.TPC], ALU.mult))
                v.wait_ge(vS, dve.n)
                dve.c(v.tensor_tensor(rr, rr, psB[:, 0:c.TPC], ALU.add))
                v.wait_ge(vS, dve.n)
                xh = xlh_sb[:, es * c.TPC:(es + 1) * c.TPC]
                xl = xll_sb[:, es * c.TPC:(es + 1) * c.TPC]
                dve.c(v.tensor_copy(xh, rr))
                v.wait_ge(vS, dve.n)
                p2b[(oc, "hi")] = dve.n
                dve.c(v.tensor_tensor(xl, rr, xh, ALU.subtract))
                p2b[(oc, "lo")] = dve.n
            VE(_n2v)

            def _n2st(s, oc=oc):
                es = oc % 2
                xh = xlh_sb[:, es * c.TPC:(es + 1) * c.TPC]
                xl = xll_sb[:, es * c.TPC:(es + 1) * c.TPC]
                s.wait_ge(vS, p2b[(oc, "lo")])
                qsc.dma(s, f"xst{oc % 2}", xlnh_d[oc * 128:(oc + 1) * 128, :],
                        xh)
                qsc.dma(s, f"xst{oc % 2}", xlnl_d[oc * 128:(oc + 1) * 128, :],
                        xl)
            SC(_n2st)
            dq_emit_some(1)

        if os.environ.get("MK_UPTO") == "p2":
            def _fin0(sy):
                qsc.w(sy, "xst0")
                qsc.w(sy, "xst1")
            SY(_fin0)
            st_p2.close()
            st_dq.close()
            return nc

        st_p2.close()
        st_p3a = contextlib.ExitStack()
        xf_sb = sbin(st_p3a, "xf_sb", [128, c.HC * c.TPC], BF16)
        wgl_sb = [sbin(st_p3a, f"wgl{i}", [128, c.HC * 128], BF16)
                  for i in range(2)]
        wul_sb = [sbin(st_p3a, f"wul{i}", [128, c.HC * 128], BF16)
                  for i in range(2)]
        gu_sb = sbin(st_p3a, "gu_sb", [128, c.TPC], BF16)
        int_sb = [sbin(st_p3a, f"int{i}", [128, c.TPC], BF16)
                  for i in range(2)]

        xlnv = xlnh_d.rearrange("(h p) t -> h p t", p=128)
        p3_gu = {}
        p3_int = {}
        pgS = sem("pgS")

        def _pgate(sy):
            qsc.w(sy, "xst0")
            qsc.w(sy, "xst1")
            qsc.w(sy, "prest0")
            qsc.w(sy, "prest1")
            qsy.w(sy, "prl0")
            qsy.w(sy, "prl1")
            qsy.w(sy, "xr0")
            qsy.w(sy, "xr1")
            qgp.w(sy, "l1b")
            qgp.w(sy, "p2ld")
            qgp.w(sy, "rr0")
            qgp.w(sy, "rr1")
            qgp.w(sy, "wp0")
            qgp.w(sy, "wp1")
            sy.wait_ge(vS, p2b[(c.HC - 1, "lo")])
            sy.wait_ge(aS, rt["l1std"])
            sy.wait_ge(tS, rt["l1bc"])
            sy.nop().then_inc(pgS, 1)
        SY(_pgate)

        def _p3x(sy):
            sy.wait_ge(pgS, 1)
            nq4 = c.HC // 4
            for q4 in range(4):
                qsy.dma(sy, "p3x", xf_sb[:, q4 * nq4 * c.TPC:
                                         (q4 + 1) * nq4 * c.TPC]
                        .rearrange("p (h t) -> p h t", t=c.TPC),
                        xlnv[q4 * nq4:(q4 + 1) * nq4, :, :]
                        .rearrange("h p t -> p h t"))
        SY(_p3x)

        def p3_guload(j):
            def _w(s, j=j):
                sl = j % 2
                if j < 2:
                    s.wait_ge(pgS, 1)
                if j >= 2:
                    s.wait_ge(tS, p3_gu[j - 2])
                dq_wait_spilled(s, (min(j // 8, c.NCGU - 1) + 1)
                                * 2 * c.HC - 1)
                nq4 = c.HC // 4
                for q4 in range(4):
                    qsc.dma(s, f"wgu{sl}", wgl_sb[sl][:, q4 * nq4 * 128:
                                                      (q4 + 1) * nq4 * 128]
                            .rearrange("p (h q) -> p h q", q=128),
                            wg_deq[:, j, q4 * nq4:(q4 + 1) * nq4, :])
                    qsc.dma(s, f"wgu{sl}", wul_sb[sl][:, q4 * nq4 * 128:
                                                      (q4 + 1) * nq4 * 128]
                            .rearrange("p (h q) -> p h q", q=128),
                            wu_deq[:, j, q4 * nq4:(q4 + 1) * nq4, :])
            SC(_w)

        p3_guload(0)
        p3_guload(1)
        for j in range(c.JC):
            def _gu(t, j=j):
                sl = j % 2
                psG = (psA, psB)[j % 2]
                psU = (psC, psD)[j % 2]
                qsc.w(t, f"wgu{sl}")
                if j == 0:
                    qsy.w(t, "p3x")
                    t.wait_ge(tS, rt["l1bc"])
                    t.wait_ge(vS, p2b[(c.HC - 1, "lo")])
                if j >= 2:
                    t.wait_ge(vS, p3_int[j - 2])
                for hc in range(c.HC):
                    lg = wgl_sb[sl][:, hc * 128:(hc + 1) * 128]
                    lu = wul_sb[sl][:, hc * 128:(hc + 1) * 128]
                    for h0, hw in chunks(c.TPC):
                        xr_ = xf_sb[:, hc * c.TPC + h0:hc * c.TPC + h0 + hw]
                        MM(t, psG[:, h0:h0 + hw], lg, xr_,
                           start=(hc == 0), stop=(hc == c.HC - 1))
                        MM(t, psU[:, h0:h0 + hw], lu, xr_,
                           start=(hc == 0), stop=(hc == c.HC - 1))
                pemark()
                p3_gu[j] = pe.n
            TE(_gu)
            if j + 2 < c.JC:
                p3_guload(j + 2)

            def _silu(s, j=j):
                psG = (psA, psB)[j % 2]
                s.wait_ge(tS, p3_gu[j])
                if j >= 1:
                    s.wait_ge(vS, p3_int[j - 1])
                act.c(s.activation(gu_sb[:, :], psG[:, 0:c.TPC], AF.Silu))
                p3_gu[(j, "silu")] = act.n
            SC(_silu)

            def _mul(v, j=j):
                psU = (psC, psD)[j % 2]
                v.wait_ge(aS, p3_gu[(j, "silu")])
                if j >= 2:
                    qsy.w(v, f"ist{j % 2}")
                dve.c(v.tensor_tensor(int_sb[j % 2][:, :], gu_sb[:, :],
                                      psU[:, 0:c.TPC], ALU.mult))
                p3_int[j] = dve.n
            VE(_mul)

            def _ist(sy, j=j):
                sy.wait_ge(vS, p3_int[j])
                qsy.dma(sy, f"ist{j % 2}", inter_d[j, :, :],
                        int_sb[j % 2][:, :])
            SY(_ist)
            dq_emit_some(12)

        if os.environ.get("MK_UPTO") == "p3a":
            def _fin0(sy):
                qsy.w(sy, "ist0")
                qsy.w(sy, "ist1")
            SY(_fin0)
            st_p3a.close()
            st_dq.close()
            return nc

        st_p3a.close()
        st_p3b = contextlib.ExitStack()
        JH = c.JC // 2
        wdl_sb = [sbin(st_p3b, f"wdl{i}", [128, JH * 512], BF16)
                  for i in range(2)]
        it_sb = [sbin(st_p3b, f"it{i}", [128, c.JC * 128], BF16)
                 for i in range(2)]
        yev_sb = [sbin(st_p3b, f"yev{i}", [128, 512], F32) for i in range(2)]
        l2a_sb = sbin(st_p3b, "l2a_sb", [128, c.DM], F32)
        DH = c.DM // 2
        l2h_sb = sbin(st_p3b, "l2h_sb", [128, DH], BF16)

        NDB = c.DM // 512
        p3_dn = {}
        p3_ev = {}
        p3_ln = {}

        def p3_wdload(db, half):
            def _w(s, db=db, half=half):
                sl = half
                if db >= 1:
                    s.wait_ge(tS, p3_dn[(db - 1, c.CPT - 1)])
                if db == 0:
                    qsy.w(s, "ist0")
                    qsy.w(s, "ist1")
                dq_wait_spilled(s, NGU_T + (db // 2 + 1) * c.JC - 1)
                j0 = half * JH
                qsc.dma(s, f"wdq{sl}", wdl_sb[sl][:, :]
                        .rearrange("p (j q) -> p j q", q=512),
                        wd_deq[j0:j0 + JH, :, db * 512:(db + 1) * 512]
                        .rearrange("j p q -> p j q"))
            SC(_w)

        def p3_itload(db, tc):
            def _i(sy, db=db, tc=tc):
                idx = db * c.CPT + tc
                if idx >= 2:
                    pdb, ptc = (idx - 2) // c.CPT, (idx - 2) % c.CPT
                    sy.wait_ge(tS, p3_dn[(pdb, ptc)])
                if idx == 0:
                    qsy.w(sy, "ist0")
                    qsy.w(sy, "ist1")
                qsy.dma(sy, f"it{idx % 2}", it_sb[idx % 2][:, :]
                        .rearrange("p (j q) -> p j q", q=128),
                        inter_d[:, :, tc * 128:(tc + 1) * 128]
                        .rearrange("j p q -> p j q"))
            SY(_i)

        p3_wdload(0, 0)
        p3_wdload(0, 1)
        for pre in range(min(2, NDB * c.CPT)):
            p3_itload(pre // c.CPT, pre % c.CPT)
        for db in range(NDB):
            for tc in range(c.CPT):
                idx = db * c.CPT + tc
                slot = idx % 4
                pp = (psA, psB)[slot // 2]
                pcol = (slot % 2) * 512

                def _dn(t, db=db, tc=tc, idx=idx, pp=pp, pcol=pcol):
                    if tc == 0:
                        qsc.w(t, "wdq0")
                        qsc.w(t, "wdq1")
                    qsy.w(t, f"it{idx % 2}")
                    if idx >= 4:
                        t.wait_ge(aS, p3_ev[((idx - 4) // c.CPT,
                                            (idx - 4) % c.CPT)])
                    for j in range(c.JC):
                        half, jl = divmod(j, JH)
                        lh = it_sb[idx % 2][:, j * 128:(j + 1) * 128]
                        MM(t, pp[:, pcol:pcol + 512], lh,
                           wdl_sb[half][:, jl * 512:(jl + 1) * 512],
                           start=(j == 0), stop=(j == c.JC - 1))
                    pemark()
                    p3_dn[(db, tc)] = pe.n
                TE(_dn)
                if idx + 2 < NDB * c.CPT:
                    p3_itload((idx + 2) // c.CPT, (idx + 2) % c.CPT)

                def _ev(s, db=db, tc=tc, idx=idx, pp=pp, pcol=pcol):
                    es = idx % 2
                    s.wait_ge(tS, p3_dn[(db, tc)])
                    if idx >= 2:
                        qsc.w(s, f"ydw{idx % 2}")
                    act.c(s.copy(yev_sb[es][:, :], pp[:, pcol:pcol + 512]))
                    p3_ev[(db, tc)] = act.n
                    s.wait_ge(aS, p3_ev[(db, tc)])
                    qsc.dma(s, f"ydw{idx % 2}",
                            yd[tc * 128:(tc + 1) * 128,
                               db * 512:(db + 1) * 512], yev_sb[es][:, :])
                SC(_ev)
                dq_emit_some(6)
            if db + 1 < NDB:
                p3_wdload(db + 1, 0)
                p3_wdload(db + 1, 1)

        def emit_ln2(w2):
            def _l2ld(sy, w2=w2):
                if w2 == 0:
                    qsc.w(sy, "ydw0")
                    qsc.w(sy, "ydw1")
                if w2 >= 1:
                    qsy.w(sy, "yst")
                qsy.dma(sy, "l2ld", l2a_sb[:, :],
                        yd[w2 * 128:(w2 + 1) * 128, :])
                qsy.dmat(sy, "l2ld", l2h_sb[:, :],
                         xlnh_d[0:DH, w2 * 128:(w2 + 1) * 128])
            SY(_l2ld)

            rsrc = [(xlnh_d, 1), (xlnl_d, 0), (xlnl_d, 1)]
            for rnd in range(4):
                def _l2a0(v, w2=w2, rnd=rnd):
                    qsy.w(v, "l2ld")
                    hf = rnd % 2
                    dve.c(v.tensor_tensor(l2a_sb[:, hf * DH:(hf + 1) * DH],
                                          l2a_sb[:, hf * DH:(hf + 1) * DH],
                                          l2h_sb[:, :], ALU.add))
                    p3_ln[(w2, "add", rnd)] = dve.n
                VE(_l2a0)
                if rnd < 3:
                    def _l2ld2(sy, w2=w2, rnd=rnd):
                        sy.wait_ge(vS, p3_ln[(w2, "add", rnd)])
                        src, hf = rsrc[rnd]
                        qsy.dmat(sy, "l2ld", l2h_sb[:, :],
                                 src[hf * DH:(hf + 1) * DH,
                                     w2 * 128:(w2 + 1) * 128])
                    SY(_l2ld2)

            def _l2a(v, w2=w2):
                v.wait_ge(vS, p3_ln[(w2, "add", 3)])
                dve.c(v.tensor_reduce(rows_s[:, 0:1], l2a_sb[:, :], AX,
                                      ALU.add))
                p3_ln[(w2, "sum")] = dve.n
            VE(_l2a)

            def _l2sq(s, w2=w2):
                s.wait_ge(vS, p3_ln[(w2, "sum")])
                act.c(s.activation(l2h_sb[:, :], l2a_sb[:, 0:DH], AF.Square,
                                   accum_out=rows_s[:, 1:2]))
                s.wait_ge(aS, act.n)
                act.c(s.activation(l2h_sb[:, :], l2a_sb[:, DH:c.DM],
                                   AF.Square, accum_out=rows_s[:, 9:10]))
                p3_ln[(w2, "sq")] = act.n
            SC(_l2sq)

            def _l2v0(v, w2=w2):
                v.wait_ge(aS, p3_ln[(w2, "sq")])
                dve.c(v.tensor_tensor(rows_s[:, 1:2], rows_s[:, 1:2],
                                      rows_s[:, 9:10], ALU.add))
                v.wait_ge(vS, dve.n)
                dve.c(v.tensor_scalar(rows_s[:, 2:3], rows_s[:, 0:1],
                                      1.0 / c.DM, None, ALU.mult))
                v.wait_ge(vS, dve.n)
                dve.c(v.tensor_tensor(rows_s[:, 3:4], rows_s[:, 2:3],
                                      rows_s[:, 2:3], ALU.mult))
                v.wait_ge(vS, dve.n)
                dve.c(v.tensor_scalar(rows_s[:, 4:5], rows_s[:, 1:2],
                                      1.0 / c.DM, None, ALU.mult))
                v.wait_ge(vS, dve.n)
                dve.c(v.tensor_tensor(rows_s[:, 5:6], rows_s[:, 4:5],
                                      rows_s[:, 3:4], ALU.subtract))
                v.wait_ge(vS, dve.n)
                dve.c(v.tensor_scalar(rows_s[:, 5:6], rows_s[:, 5:6], 1e-5,
                                      None, ALU.add))
                p3_ln[(w2, "var")] = dve.n
            VE(_l2v0)

            def _l2s(s, w2=w2):
                s.wait_ge(vS, p3_ln[(w2, "var")])
                act.c(s.activation(rows_s[:, 6:7], rows_s[:, 5:6], AF.Sqrt))
                p3_ln[(w2, "std")] = act.n
            SC(_l2s)

            def _l2ap(v, w2=w2):
                v.wait_ge(aS, p3_ln[(w2, "std")])
                dve.c(v.reciprocal(rows_s[:, 7:8], rows_s[:, 6:7]))
                v.wait_ge(vS, dve.n)
                dve.c(v.tensor_scalar(l2a_sb[:, :], l2a_sb[:, :],
                                      rows_s[:, 2:3], rows_s[:, 7:8],
                                      ALU.subtract, ALU.mult))
                p3_ln[(w2, "apply")] = dve.n
            VE(_l2ap)

            def _l2y(sy, w2=w2):
                sy.wait_ge(vS, p3_ln[(w2, "apply")])
                qsy.dma(sy, "yst", y[w2 * 128:(w2 + 1) * 128, :],
                        l2a_sb[:, :])
            SY(_l2y)

        for w2 in range(c.CPT):
            emit_ln2(w2)

        def _fin(sy):
            qsy.w(sy, "yst")
        SY(_fin)
        st_p3b.close()
        st_dq.close()

    return nc


_CACHE = {}


def kernel(L0_attn_hidden_states, L0_attn_w_attn, L0_attn_b_attn,
           L0_attn_w_proj, L0_attn_b_proj, L0_ffn_gate_codes,
           L0_ffn_gate_scales, L0_ffn_up_codes, L0_ffn_up_scales,
           L0_ffn_down_codes, L0_ffn_down_scales):
    cfg = Cfg()
    in_maps = stage_inputs(
        cfg, L0_attn_hidden_states, L0_attn_w_attn, L0_attn_b_attn,
        L0_attn_w_proj, L0_attn_b_proj, L0_ffn_gate_codes, L0_ffn_gate_scales,
        L0_ffn_up_codes, L0_ffn_up_scales, L0_ffn_down_codes,
        L0_ffn_down_scales)
    if "nc" not in _CACHE:
        _CACHE["nc"] = build_program(cfg)
    nc = _CACHE["nc"]
    res = run_bass_kernel_spmd(nc, in_maps, core_ids=list(range(cfg.NC)))
    out = np.concatenate([r["y"] for r in res.results], axis=0)
    return out.reshape(cfg.B, cfg.S, cfg.DM).astype(np.float32)


# version 09
# speedup vs baseline: 1.1946x; 1.1946x over previous
import contextlib
import math
import os

import numpy as np
import ml_dtypes

import concourse.bass as bass
import concourse.mybir as mybir
from concourse.bass_utils import run_bass_kernel_spmd

F32 = mybir.dt.float32
BF16 = mybir.dt.bfloat16
F16 = mybir.dt.float16
I32 = mybir.dt.int32
U8 = mybir.dt.uint8
ALU = mybir.AluOpType
AF = mybir.ActivationFunctionType
AX = mybir.AxisListType.X
NPBF = ml_dtypes.bfloat16

NF4 = np.array([
    -1.0, -0.6961928009986877, -0.5250730514526367, -0.39491748809814453,
    -0.28444138169288635, -0.18477343022823334, -0.09105003625154495, 0.0,
    0.07958029955625534, 0.16093020141124725, 0.24611230194568634,
    0.33791524171829224, 0.44070982933044434, 0.5626170039176941,
    0.7229568362236023, 1.0], dtype=np.float64)

DQ_CA = 4.5
DQ_CB = 10.5
NFG = (NF4[1:] - NF4[:-1])
NFDA = (NF4[:-1] - NFG * (np.arange(15) - DQ_CA))
NFDB = (NF4[:-1] - NFG * (np.arange(15) - DQ_CB))
NFG = [float(v) for v in NFG]
NFDA = [float(v) for v in NFDA]
NFDB = [float(v) for v in NFDB]

DEAD = -60000.0


def alibi_slopes(n):
    cp2 = 2 ** math.floor(math.log2(n))
    base = 2 ** (-(2 ** (-(math.log2(cp2) - 3))))
    slopes = base ** np.arange(1, cp2 + 1, dtype=np.float32)
    if cp2 != n:
        eb = 2 ** (-(2 ** (-(math.log2(2 * cp2) - 3))))
        extra = eb ** np.arange(1, 1 + 2 * min(cp2, n - cp2), 2,
                                dtype=np.float32)
        slopes = np.concatenate([slopes, extra])
    return slopes.astype(np.float32)


class Cfg:
    def __init__(self, NH=32, B=4, S=2048, I=11008):
        self.NC = 8
        self.D = 128
        self.NH = NH
        self.DM = NH * self.D
        self.HC = self.DM // 128
        self.B = B
        self.S = S
        self.TB = S
        self.TPC = S // 2
        self.KC = S // 128
        self.KH = self.KC // 2
        self.CC = 3 * NH
        self.CCH = 6
        self.NPASS = self.CC // self.CCH
        self.TSUP = min(512, S)
        self.NTS = self.TB // self.TSUP
        self.I = I
        self.JC = I // 128
        self.SCALE = 1.0 / self.D
        self.DQW = 1024
        self.NCGU = -(-I // self.DQW)
        self.NCD = self.DM // self.DQW
        self.CPT = self.TPC // 128


def stage_inputs(cfg, x, w_attn, b_attn, w_proj, b_proj,
                 g_codes, g_scales, u_codes, u_scales, d_codes, d_scales):
    c = cfg
    slopes = alibi_slopes(c.NH)
    x = np.asarray(x, dtype=np.float32)
    w_attn = np.asarray(w_attn, dtype=np.float32)
    b_attn = np.asarray(b_attn, dtype=np.float32)
    w_proj = np.asarray(w_proj, dtype=np.float32)
    b_proj = np.asarray(b_proj, dtype=np.float32)

    wq = np.concatenate(
        [w_attn[:, (g * c.DM + h * 128):(g * c.DM + (h + 1) * 128)]
         for h in range(c.NH) for g in range(3)], axis=1)
    wq3 = np.ascontiguousarray(wq).reshape(c.HC, 128, c.CC * 128)
    bqcols = np.stack(
        [b_attn[g * c.DM + h * 128:g * c.DM + (h + 1) * 128]
         for h in range(c.NH) for g in range(3)], axis=1)
    bq = np.ascontiguousarray(bqcols.astype(np.float32))

    wp4 = np.ascontiguousarray(
        w_proj.reshape(c.HC, 128, c.HC, 128).transpose(2, 0, 1, 3))
    bpc = np.ascontiguousarray(b_proj.reshape(c.HC, 128).T)

    gT = np.ascontiguousarray(np.asarray(g_codes).T.astype(np.uint8))
    uT = np.ascontiguousarray(np.asarray(u_codes).T.astype(np.uint8))
    dTr = np.ascontiguousarray(np.asarray(d_codes).T.astype(np.uint8))
    gcT = gT.reshape(c.HC, 128, c.I)
    ucT = uT.reshape(c.HC, 128, c.I)
    dcT = dTr.reshape(c.JC, 128, c.DM)
    gsT = np.ascontiguousarray(np.asarray(g_scales).T)
    usT = np.ascontiguousarray(np.asarray(u_scales).T)
    dsT = np.ascontiguousarray(np.asarray(d_scales).T)

    common = {
        "wq3": wq3, "bq": bq, "wp4": wp4, "bpc": bpc,
        "ones22": np.ones((2, 128), dtype=NPBF),
        "onesc": np.ones((128, 1), dtype=NPBF),
        "onesf": np.ones((128, 1), dtype=np.float32),
        "masku": np.ascontiguousarray(
            np.where(np.arange(128)[:, None] > np.arange(128)[None, :],
                     np.float32(-1e30), np.float32(0.0)).astype(np.float32)),
        "gcT": gcT, "ucT": ucT, "dcT": dcT,
        "gsT": gsT, "usT": usT, "dsT": dsT,
    }

    in_maps = []
    for r in range(c.NC):
        b, h = r // 2, r % 2
        m = dict(common)
        xb = x[b]
        xT = np.ascontiguousarray(xb.T)
        mine = xT[:, h * c.TPC:(h + 1) * c.TPC]
        othr = xT[:, (1 - h) * c.TPC:(2 - h) * c.TPC]
        m["xT3"] = np.ascontiguousarray(
            np.concatenate([mine, othr], axis=1)).reshape(c.HC, 128, c.TB)
        ak = np.zeros((128, c.NH * c.KC), np.float32)
        p = np.arange(128, dtype=np.float32)
        for hh in range(c.NH):
            sl = slopes[hh]
            for ci in range(c.KC):
                col = hh * c.KC + ci
                if ci < c.KH:
                    kg = h * c.TPC + ci * 128 + p
                    ak[:, col] = sl * kg
                elif h == 1:
                    kg = (ci - c.KH) * 128 + p
                    ak[:, col] = sl * kg
                else:
                    ak[:, col] = DEAD
        m["alik"] = ak
        aq = np.zeros((c.NH, c.TPC), np.float32)
        qg = h * c.TPC + np.arange(c.TPC, dtype=np.float32)
        for hh in range(c.NH):
            aq[hh] = -float(c.D) * slopes[hh] * qg
        m["aliqm"] = np.ascontiguousarray(aq)
        in_maps.append(m)
    return in_maps


class Q:
    def __init__(self, mksem, prefix):
        self.mksem = mksem
        self.prefix = prefix
        self.streams = {}

    def _s(self, stream):
        if stream not in self.streams:
            self.streams[stream] = [self.mksem(f"{self.prefix}_{stream}"), 0]
        return self.streams[stream]

    def dma(self, eng, stream, out, in_, **kw):
        s = self._s(stream)
        eng.dma_start(out=out, in_=in_, **kw).then_inc(s[0], 16)
        s[1] += 16
        return s[1]

    def dmat(self, eng, stream, out, in_):
        s = self._s(stream)
        eng.dma_start(out=out, in_=in_, transpose=True).then_inc(s[0], 16)
        s[1] += 16
        return s[1]

    def tot(self, stream):
        return self._s(stream)

    def w(self, eng, stream):
        s = self._s(stream)
        eng.wait_ge(s[0], s[1])


class E:
    def __init__(self, sem):
        self.sem = sem
        self.n = 0

    def c(self, inst):
        inst.then_inc(self.sem, 1)
        self.n += 1
        return self.n


def chunks(total, w=512):
    return [(i, min(w, total - i)) for i in range(0, total, w)]


def bpieces(lo, hi):
    out = []
    while lo < hi:
        nxt = min(hi, (lo // 512 + 1) * 512)
        out.append((lo, nxt - lo))
        lo = nxt
    return out


def build_program(cfg):
    c = cfg
    nc = bass.Bass("TRN2", target_bir_lowering=False)
    dt = nc.dram_tensor

    xT3 = dt("xT3", [c.HC, 128, c.TB], F32, kind="ExternalInput")
    wq3 = dt("wq3", [c.HC, 128, c.CC * 128], F32, kind="ExternalInput")
    bq = dt("bq", [128, c.CC], F32, kind="ExternalInput")
    aliqm = dt("aliqm", [c.NH, c.TPC], F32, kind="ExternalInput")
    alik = dt("alik", [128, c.NH * c.KC], F32, kind="ExternalInput")
    masku = dt("masku", [128, 128], F32, kind="ExternalInput")
    ones22 = dt("ones22", [2, 128], BF16, kind="ExternalInput")
    onesc = dt("onesc", [128, 1], BF16, kind="ExternalInput")
    onesf = dt("onesf", [128, 1], F32, kind="ExternalInput")
    wp4 = dt("wp4", [c.HC, c.HC, 128, 128], F32, kind="ExternalInput")
    bpc = dt("bpc", [128, c.HC], F32, kind="ExternalInput")
    gcT = dt("gcT", [c.HC, 128, c.I], U8, kind="ExternalInput")
    ucT = dt("ucT", [c.HC, 128, c.I], U8, kind="ExternalInput")
    dcT = dt("dcT", [c.JC, 128, c.DM], U8, kind="ExternalInput")
    gsT = dt("gsT", [c.DM // 64, c.I], F32, kind="ExternalInput")
    usT = dt("usT", [c.DM // 64, c.I], F32, kind="ExternalInput")
    dsT = dt("dsT", [c.I // 64, c.DM], F32, kind="ExternalInput")
    y = dt("y", [c.TPC, c.DM], F32, kind="ExternalOutput")

    xbf_d = dt("xbf_d", [c.HC, 128, c.TB], BF16)
    qkv_sp = dt("qkv_sp", [c.CC * 128, c.TB], BF16)
    q2k2_d = dt("q2k2_d", [2 * c.NH, c.TB], F32)
    frow_d = dt("frow_d", [2, c.TPC], BF16)
    ot_d = dt("ot_d", [c.NH, 128, c.TPC], BF16)
    rrl_d = dt("rrl_d", [2 * c.NH, c.TPC], BF16)
    pres_d = dt("pres_d", [c.HC, 128, c.TPC], F32)
    xlnh_d = dt("xlnh_d", [c.DM, c.TPC], BF16)
    xlnl_d = dt("xlnl_d", [c.DM, c.TPC], BF16)
    abrow_d = dt("abrow_d", [4, c.TPC], BF16)
    wg_deq = dt("wg_deq", [128, c.JC, c.HC, 128], BF16)
    wu_deq = dt("wu_deq", [128, c.JC, c.HC, 128], BF16)
    wd_deq = dt("wd_deq", [c.JC, 128, c.DM], BF16)
    inter_d = dt("inter_d", [c.JC, 128, c.TPC], BF16)
    yd = dt("yd", [c.TPC, c.DM], F32)

    st = contextlib.ExitStack()
    with st:
        def sbin(stack, name, shape, dtt):
            return stack.enter_context(nc.sbuf_tensor(name, shape, dtt))

        def sb(name, shape, dtt):
            return st.enter_context(nc.sbuf_tensor(name, shape, dtt))

        def psum(name, shape, dtt=F32):
            return st.enter_context(nc.psum_tensor(name, shape, dtt))

        def sem(name):
            return st.enter_context(nc.semaphore(name))

        ones22_s = sb("ones22_s", [2, 128], BF16)
        onesc_s = sb("onesc_s", [128, 1], BF16)
        onescf_s = sb("onescf_s", [128, 1], F32)
        masku_s = sb("masku_s", [128, 128], F32)
        rows_s = sb("rows_s", [128, 16], F32)
        bq_s = sb("bq_s", [128, c.CC], F32)
        alik_s = sb("alik_s", [128, c.NH * c.KC], F32)
        aliq_sb = sb("aliq_sb", [c.NH, c.TPC], F32)
        bpc_s = sb("bpc_s", [128, c.HC], F32)
        rwm = sb("rwm", [c.NH, 1], F32)
        rwf = sb("rwf", [c.NH, c.TB], F32)
        rwfh = sb("rwfh", [c.NH, c.TPC], BF16)
        rwfl = sb("rwfl", [c.NH, c.TPC], BF16)
        q2st = sb("q2st", [1, 2 * c.TSUP], F32)
        psA = psum("psA", [128, 1024])
        psB = psum("psB", [128, 1024])
        psC = psum("psC", [128, 1024])
        psD = psum("psD", [128, 1024])

        tS = sem("tS")
        aS = sem("aS")
        vS = sem("vS")

        block = st.enter_context(nc.Block())
        qsy = Q(sem, "sy")
        qsc = Q(sem, "sc")
        qgp = Q(sem, "gp")
        pe = E(tS)
        act = E(aS)
        dve = E(vS)

        SY, SC, GP, VE, TE = (block.sync, block.scalar, block.gpsimd,
                              block.vector, block.tensor)
        rt = {}
        last_mm = [None]

        def MM(t, *a, **kw):
            last_mm[0] = t.matmul(*a, **kw)
            return last_mm[0]

        def pemark():
            return pe.c(last_mm[0])

        DQW = c.DQW
        st_dq = contextlib.ExitStack()
        dq_c = sbin(st_dq, "dq_c", [128, 2 * DQW], F16)
        dq_ca = sbin(st_dq, "dq_ca", [128, DQW], F16)
        dq_cb = sbin(st_dq, "dq_cb", [128, DQW], F16)
        dq_m = sbin(st_dq, "dq_m", [128, DQW], F16)
        dq_x = sbin(st_dq, "dq_x", [128, DQW], F16)
        dq_l7 = sbin(st_dq, "dq_l7", [128, DQW], F16)
        dq_t = sbin(st_dq, "dq_t", [128, DQW], F16)
        dq_s = sbin(st_dq, "dq_s", [128, 2 * DQW], F16)
        dq_w = sbin(st_dq, "dq_w", [128, 2 * DQW], BF16)

        dq_tiles = []
        for colt in range(c.NCGU):
            col = colt * DQW
            w = min(DQW, c.I - col)
            for mat in ("g", "u"):
                for hc in range(c.HC):
                    dq_tiles.append((mat, hc, col, w))
        for colt in range(c.NCD):
            for jc in range(c.JC):
                dq_tiles.append(("d", jc, colt * DQW, DQW))
        NDQ = len(dq_tiles)
        NGU_T = 2 * c.HC * c.NCGU
        dq_vd = {}
        dq_spd = {}
        dq_bl = {}
        dq_bcons = {}
        dq_next = [0]

        def dq_emit_one(i):
            mat, a, col, w = dq_tiles[i]
            cs = i % 2
            cw = slice(cs * DQW, cs * DQW + w)
            ww = slice(0, w)

            def _ld(g):
                if i >= 2:
                    g.wait_ge(vS, dq_vd[i - 2])
                src, srows = {"g": (gcT, gsT), "u": (ucT, usT),
                              "d": (dcT, dsT)}[mat]
                qgp.dma(g, f"dqc{i % 2}", dq_c[:, cw], src[a, :, col:col + w])
                qgp.dma(g, f"dqc{i % 2}", dq_s[0:64, cw],
                        srows[2 * a:2 * a + 1, col:col + w]
                        .partition_broadcast(64))
                qgp.dma(g, f"dqc{i % 2}", dq_s[64:128, cw],
                        srows[2 * a + 1:2 * a + 2, col:col + w]
                        .partition_broadcast(64))
            GP(_ld)

            def _tree(v):
                qgp.w(v, f"dqc{i % 2}")
                if i >= 2:
                    qgp.w(v, f"dqw{i % 2}")
                cc_ = dq_c[:, cw]
                ca_, cb_ = dq_ca[:, ww], dq_cb[:, ww]
                m_, x_, l7_, t_ = (dq_m[:, ww], dq_x[:, ww],
                                   dq_l7[:, ww], dq_t[:, ww])
                dve.c(v.tensor_scalar(ca_, cc_, -DQ_CA, None, ALU.add))
                dve.c(v.tensor_scalar(cb_, cc_, -DQ_CB, None, ALU.add))
                v.wait_ge(vS, dve.n)
                dve.c(v.tensor_scalar(l7_, ca_, NFG[7], NFDA[7],
                                      ALU.mult, ALU.add))
                dve.c(v.tensor_scalar(m_, ca_, NFG[0], NFDA[0],
                                      ALU.mult, ALU.add))
                v.wait_ge(vS, dve.n)
                for i2 in range(1, 7):
                    dve.c(v.tensor_scalar(t_, ca_, NFG[i2], NFDA[i2],
                                          ALU.mult, ALU.add))
                    v.wait_ge(vS, dve.n)
                    dve.c(v.tensor_tensor(m_, m_, t_, ALU.min))
                    v.wait_ge(vS, dve.n)
                dve.c(v.tensor_tensor(m_, m_, l7_, ALU.min))
                dve.c(v.tensor_scalar(x_, cb_, NFG[8], NFDB[8],
                                      ALU.mult, ALU.add))
                v.wait_ge(vS, dve.n)
                for i2 in range(9, 15):
                    dve.c(v.tensor_scalar(t_, cb_, NFG[i2], NFDB[i2],
                                          ALU.mult, ALU.add))
                    v.wait_ge(vS, dve.n)
                    dve.c(v.tensor_tensor(x_, x_, t_, ALU.max))
                    v.wait_ge(vS, dve.n)
                dve.c(v.tensor_scalar(t_, cb_, NFG[7], NFDB[7],
                                      ALU.mult, ALU.add))
                v.wait_ge(vS, dve.n)
                dve.c(v.tensor_tensor(x_, x_, t_, ALU.max))
                v.wait_ge(vS, dve.n)
                dve.c(v.tensor_tensor(m_, m_, x_, ALU.add))
                v.wait_ge(vS, dve.n)
                dve.c(v.tensor_tensor(m_, m_, l7_, ALU.subtract))
                v.wait_ge(vS, dve.n)
                dve.c(v.tensor_tensor(dq_w[:, cw], m_, dq_s[:, cw], ALU.mult))
                dq_vd[i] = dve.n
            VE(_tree)

            def _sp(g):
                g.wait_ge(vS, dq_vd[i])
                if mat in ("g", "u"):
                    dst = wg_deq if mat == "g" else wu_deq
                    tot = qgp.dma(g, f"dqw{i % 2}",
                                  dst[:, col // 128:(col + w) // 128, a, :],
                                  dq_w[:, cs * DQW:cs * DQW + w]
                                  .rearrange("p (j q) -> p j q", q=128))
                else:
                    tot = qgp.dma(g, f"dqw{i % 2}", wd_deq[a, :, col:col + w],
                                  dq_w[:, cw])
                dq_spd[i] = (f"dqw{i % 2}", tot)
            GP(_sp)

        def dq_emit_some(k):
            e = min(dq_next[0] + k, NDQ)
            for i in range(dq_next[0], e):
                dq_emit_one(i)
            dq_next[0] = e

        def dq_wait_spilled(eng, upto_i):
            tots = {}
            for i in range(upto_i + 1):
                s, t = dq_spd[i]
                tots[s] = max(tots.get(s, 0), t)
            for s, t in tots.items():
                eng.wait_ge(qgp.tot(s)[0], t)

        def _p0(g):
            qgp.dma(g, "const", ones22_s[:, :], ones22[:, :])
            qgp.dma(g, "const", onesc_s[:, :], onesc[:, :])
            qgp.dma(g, "const", onescf_s[:, :], onesf[:, :])
            qgp.dma(g, "const", masku_s[:, :], masku[:, :])
            qgp.dma(g, "const", bq_s[:, :], bq[:, :])
            qgp.dma(g, "const", alik_s[:, :], alik[:, :])
            qgp.dma(g, "const", aliq_sb[:, :], aliqm[:, :])
            qgp.dma(g, "const", bpc_s[:, :], bpc[:, :])
        GP(_p0)

        st_p1a = contextlib.ExitStack()
        xsb = [sbin(st_p1a, f"xsb{i}", [128, c.HC * c.TSUP], BF16)
               for i in range(2)]
        wq_sb = sbin(st_p1a, "wq_sb", [128, c.HC * c.CCH * 128], BF16)
        qe_sb = sbin(st_p1a, "qe_sb", [128, 4 * c.TSUP], BF16)
        sqe_sb = sbin(st_p1a, "sqe_sb", [128, 2 * c.TSUP], BF16)
        xc_sb = [sbin(st_p1a, f"xc{i}", [128, c.TB], BF16) for i in range(2)]

        xcast_done = {}
        for hc in range(c.HC):
            def _xc(g, hc=hc):
                sl = hc % 2
                if hc >= 2:
                    qgp.w(g, f"xcb{sl}")
                qgp.dma(g, f"xca{sl}", xc_sb[sl][:, :], xT3[hc, :, :])
                qgp.w(g, f"xca{sl}")
                tot = qgp.dma(g, f"xcb{sl}", xbf_d[hc, :, :], xc_sb[sl][:, :])
                xcast_done[hc] = tot
            GP(_xc)

        x_ld = {}
        p1a_mm = {}
        p1a_ev = {}
        p1a_sq = {}
        p1a_q2mm = {}
        p1a_q2ev = {}
        nsq = [0]

        def p1a_wqload(ps):
            def _wq(g, ps=ps):
                if ps >= 1:
                    g.wait_ge(tS, p1a_mm[(ps * c.NTS - 1, c.CCH - 1)])
                nq = c.HC // 4
                for qtr in range(4):
                    qgp.dma(g, "wqp", wq_sb[:, qtr * nq * c.CCH * 128:
                                            (qtr + 1) * nq * c.CCH * 128]
                            .rearrange("p (h q) -> p h q", q=c.CCH * 128),
                            wq3[qtr * nq:(qtr + 1) * nq, :,
                                ps * c.CCH * 128:(ps + 1) * c.CCH * 128]
                            .rearrange("h p q -> p h q"))
            GP(_wq)

        def p1a_load(ts2):
            def _l(g, ts2=ts2):
                if ts2 == 0:
                    qgp.w(g, "xcb0")
                    qgp.w(g, "xcb1")
                if ts2 >= 2:
                    g.wait_ge(tS, p1a_mm[(ts2 - 2, c.CCH - 1)])
                nq4 = c.HC // 4
                ts = ts2 % c.NTS
                for q4 in range(4):
                    qgp.dma(g, f"xs{ts2 % 2}",
                            xsb[ts2 % 2][:, q4 * nq4 * c.TSUP:
                                         (q4 + 1) * nq4 * c.TSUP]
                            .rearrange("p (h t) -> p h t", t=c.TSUP),
                            xbf_d[q4 * nq4:(q4 + 1) * nq4, :,
                                  ts * c.TSUP:(ts + 1) * c.TSUP]
                            .rearrange("h p t -> p h t"))
            GP(_l)

        p1a_wqload(0)
        p1a_load(0)
        p1a_load(1)
        for ts2 in range(c.NPASS * c.NTS):
            ps = ts2 // c.NTS
            ts = ts2 % c.NTS
            for ccl in range(c.CCH):
                cc = ps * c.CCH + ccl
                idx = ts2 * c.CCH + ccl
                slot = idx % 4
                pp = (psA, psB)[slot // 2]
                col = (slot % 2) * 512

                def _mm(t, ts2=ts2, ccl=ccl, cc=cc, idx=idx, pp=pp, col=col):
                    if ccl == 0:
                        qgp.w(t, f"xs{ts2 % 2}")
                        if ts2 == 0:
                            qgp.w(t, "const")
                        if ts2 % c.NTS == 0:
                            qgp.w(t, "wqp")
                    if idx >= 4:
                        t.wait_ge(aS, p1a_ev[idx - 4])
                    for hc in range(c.HC):
                        MM(t, pp[:, col:col + c.TSUP],
                           wq_sb[:, (hc * c.CCH + ccl) * 128:
                                 (hc * c.CCH + ccl + 1) * 128],
                           xsb[ts2 % 2][:, hc * c.TSUP:(hc + 1) * c.TSUP],
                           start=(hc == 0), stop=(hc == c.HC - 1))
                    pemark()
                    p1a_mm[(ts2, ccl)] = pe.n
                TE(_mm)

                def _ev(s, ts2=ts2, ts=ts, cc=cc, idx=idx, pp=pp, col=col):
                    es = idx % 4
                    s.wait_ge(tS, p1a_mm[(ts2, cc % c.CCH)])
                    if idx >= 4:
                        qsc.w(s, f"sp{idx % 4}")
                    act.c(s.activation(qe_sb[:, es * c.TSUP:(es + 1) * c.TSUP],
                                       pp[:, col:col + c.TSUP],
                                       AF.Identity, bias=bq_s[:, cc:cc + 1]))
                    p1a_ev[idx] = act.n
                    s.wait_ge(aS, p1a_ev[idx])
                    qsc.dma(s, f"sp{idx % 4}",
                            qkv_sp[cc * 128:(cc + 1) * 128,
                                   ts * c.TSUP:(ts + 1) * c.TSUP],
                            qe_sb[:, es * c.TSUP:(es + 1) * c.TSUP])
                SC(_ev)

                if cc % 3 < 2:
                    si = nsq[0]
                    nsq[0] += 1

                    def _sq(s, idx=idx, si=si):
                        ss = si % 2
                        s.wait_ge(aS, p1a_ev[idx])
                        if si >= 2:
                            s.wait_ge(tS, p1a_q2mm[si - 2])
                        act.c(s.activation(
                            sqe_sb[:, ss * c.TSUP:(ss + 1) * c.TSUP],
                            qe_sb[:, (idx % 4) * c.TSUP:
                                  (idx % 4 + 1) * c.TSUP], AF.Square))
                        p1a_sq[si] = act.n
                    SC(_sq)

                    def _q2(t, si=si):
                        ss = si % 2
                        t.wait_ge(aS, p1a_sq[si])
                        if si >= 2:
                            t.wait_ge(vS, p1a_q2ev[si - 2])
                        MM(t, psC[0:1, ss * 512:ss * 512 + c.TSUP],
                           onesc_s[:, :],
                           sqe_sb[:, ss * c.TSUP:(ss + 1) * c.TSUP],
                           start=True, stop=True)
                        pemark()
                        p1a_q2mm[si] = pe.n
                    TE(_q2)

                    def _q2e(v, si=si):
                        ss = si % 2
                        v.wait_ge(tS, p1a_q2mm[si])
                        if si >= 2:
                            qgp.w(v, "q2st")
                        dve.c(v.tensor_copy(
                            q2st[0:1, ss * c.TSUP:(ss + 1) * c.TSUP],
                            psC[0:1, ss * 512:ss * 512 + c.TSUP]))
                        p1a_q2ev[si] = dve.n
                    VE(_q2e)

                    def _q2w(g, ts=ts, cc=cc, si=si):
                        ss = si % 2
                        g.wait_ge(vS, p1a_q2ev[si])
                        g3 = cc % 3
                        hh = cc // 3
                        row = hh if g3 == 0 else c.NH + hh
                        scol = ts * c.TSUP
                        qgp.dma(g, "q2st", q2k2_d[row:row + 1,
                                                  scol:scol + c.TSUP],
                                q2st[0:1, ss * c.TSUP:(ss + 1) * c.TSUP])
                    GP(_q2w)

            if ts2 + 2 < c.NPASS * c.NTS:
                p1a_load(ts2 + 2)
            if ts2 % c.NTS == c.NTS - 1 and ts2 // c.NTS + 1 < c.NPASS:
                p1a_wqload(ts2 // c.NTS + 1)
            dq_emit_some(2)

        if os.environ.get("MK_UPTO") == "p1a":
            def _fin0(sy):
                qsc.w(sy, "sp0")
                qsc.w(sy, "sp1")
                qsc.w(sy, "sp2")
                qsc.w(sy, "sp3")
            SY(_fin0)
            st_p1a.close()
            st_dq.close()
            return nc

        def _rows_gp(g):
            g.wait_ge(vS, p1a_q2ev[nsq[0] - 1])
            qgp.w(g, "q2st")
            qgp.dma(g, "rows", rwf[:, :], q2k2_d[c.NH:2 * c.NH, :])
            rt["rows_k2"] = True
        GP(_rows_gp)

        def _rows_v(v):
            qgp.w(v, "rows")
            dve.c(v.tensor_reduce(rwm[:, :], rwf[:, :], AX, ALU.max))
            rt["rows_max"] = dve.n
        VE(_rows_v)

        def _rows_g2(g):
            g.wait_ge(vS, rt["rows_max"])
            qgp.dma(g, "rows", rwf[:, 0:c.TPC], q2k2_d[0:c.NH, 0:c.TPC])
        GP(_rows_g2)

        def _rows_v1(v):
            qgp.w(v, "rows")
            dve.c(v.tensor_scalar(rwf[:, 0:c.TPC], rwf[:, 0:c.TPC],
                                  rwm[:, 0:1], None, ALU.mult))
            v.wait_ge(vS, dve.n)
            dve.c(v.tensor_scalar(rwf[:, 0:c.TPC], rwf[:, 0:c.TPC], 1e-12,
                                  None, ALU.max))
            rt["rows_mul"] = dve.n
        VE(_rows_v1)

        def _rows_s(s):
            s.wait_ge(vS, rt["rows_mul"])
            act.c(s.activation(rwf[:, 0:c.TPC], rwf[:, 0:c.TPC], AF.Sqrt))
            rt["rows_sqrt"] = act.n
        SC(_rows_s)

        def _rows_v2(v):
            v.wait_ge(aS, rt["rows_sqrt"])
            dve.c(v.scalar_tensor_tensor(rwf[:, 0:c.TPC], rwf[:, 0:c.TPC],
                                         -1.0, aliq_sb[:, :], ALU.mult,
                                         ALU.add))
            v.wait_ge(vS, dve.n)
            dve.c(v.tensor_copy(rwfh[:, :], rwf[:, 0:c.TPC]))
            v.wait_ge(vS, dve.n)
            dve.c(v.tensor_tensor(rwfl[:, :], rwf[:, 0:c.TPC], rwfh[:, :],
                                  ALU.subtract))
            rt["rows_done"] = dve.n
        VE(_rows_v2)

        if os.environ.get("MK_UPTO") == "rows":
            def _fin0(sy):
                sy.wait_ge(vS, rt["rows_done"])
            SY(_fin0)
            st_p1a.close()
            st_dq.close()
            return nc

        st_p1a.close()
        st_p1b = contextlib.ExitStack()
        qt_sb = [sbin(st_p1b, f"qt{i}", [128, c.TPC], BF16) for i in range(2)]
        kt_sb = [sbin(st_p1b, f"kt{i}", [128, c.TB], BF16) for i in range(2)]
        vn_sb = [sbin(st_p1b, f"vn{i}", [128, c.TB], BF16) for i in range(2)]
        pt_sb = sbin(st_p1b, "pt_sb", [128, c.KC * c.TPC], BF16)
        frow_s = sbin(st_p1b, "frow_s", [2, c.TPC], BF16)
        rrow_s = sbin(st_p1b, "rrow_s", [1, 2 * c.TPC], F32)
        rrbf_s = sbin(st_p1b, "rrbf_s", [1, 2 * c.TPC], BF16)
        ot_f = sbin(st_p1b, "ot_f", [128, c.TPC], F32)
        ot_b = sbin(st_p1b, "ot_b", [128, c.TPC], BF16)

        p1b_smm = {}
        p1b_mask = {}
        p1b_exp = {}
        p1b_av = {}
        p1b_otf = {}
        p1b_otb = {}
        p1b_rr = {}
        p1b_done_pe = {}

        def qstart(ci):
            return 128 * ci if ci < c.KH else 0

        for hh in range(c.NH):
            idx = hh
            buf = idx % 2

            def _ld(sy, idx=idx, hh=hh, buf=buf):
                if idx == 0:
                    for kq in range(4):
                        qsc.w(sy, f"sp{kq}")
                    qgp.w(sy, "rows")
                    qgp.w(sy, "xcb0")
                    qgp.w(sy, "xcb1")
                    qgp.w(sy, "q2st")
                if idx >= 2:
                    sy.wait_ge(tS, p1b_done_pe[idx - 2])
                r0 = (3 * hh) * 128
                qsy.dma(sy, f"qk{buf}", qt_sb[buf][:, :],
                        qkv_sp[r0:r0 + 128, 0:c.TPC])
                qsy.dma(sy, f"qk{buf}", kt_sb[buf][:, :],
                        qkv_sp[r0 + 128:r0 + 256, :])
                for kc in range(c.KC):
                    qsy.dmat(sy, f"qk{buf}",
                             vn_sb[buf][:, kc * 128:(kc + 1) * 128],
                             qkv_sp[r0 + 256:r0 + 384,
                                    kc * 128:(kc + 1) * 128])
            SY(_ld)

            def _frow(g, idx=idx, hh=hh):
                if idx == 0:
                    g.wait_ge(vS, rt["rows_done"])
                if idx >= 1:
                    qgp.w(g, "frow")
                qgp.dma(g, "frow", frow_d[0:1, :], rwfh[hh:hh + 1, :])
                qgp.dma(g, "frow", frow_d[1:2, :], rwfl[hh:hh + 1, :])
                qgp.w(g, "frow")
                if idx >= 1:
                    g.wait_ge(tS, p1b_done_pe[idx - 1])
                qgp.dma(g, "frow", frow_s[:, :], frow_d[:, :])
            GP(_frow)

            for kci in range(c.KC):
                qs = qstart(kci)
                wdt = c.TPC - qs

                def _smm1(t, idx=idx, hh=hh, buf=buf, kci=kci, qs=qs,
                          wdt=wdt):
                    if kci == 0:
                        qsy.w(t, f"qk{buf}")
                        qgp.w(t, "frow")
                        if idx >= 1:
                            t.wait_ge(aS, p1b_otf[idx - 1])
                            t.wait_ge(vS, p1b_rr[idx - 1])
                    sp = (psA, psB)[kci % 2]
                    if kci >= 2:
                        t.wait_ge(aS, p1b_exp[(idx, kci - 2)])
                    for p0, pw in bpieces(qs, c.TPC):
                        cs = slice(p0, p0 + pw)
                        MM(t, sp[:, cs],
                           kt_sb[buf][:, kci * 128:(kci + 1) * 128],
                           qt_sb[buf][:, p0:p0 + pw],
                           start=True, stop=False)
                        MM(t, sp[:, cs], ones22_s[:, :],
                           frow_s[:, p0:p0 + pw],
                           start=False, stop=True)
                    pemark()
                    p1b_smm[(idx, kci)] = pe.n
                TE(_smm1)

                if kci < c.KH:
                    def _mask1(v, idx=idx, kci=kci, qs=qs):
                        sp = (psA, psB)[kci % 2]
                        v.wait_ge(tS, p1b_smm[(idx, kci)])
                        dve.c(v.tensor_tensor(sp[:, qs:qs + 128],
                                              sp[:, qs:qs + 128],
                                              masku_s[:, :], ALU.add))
                        p1b_mask[(idx, kci)] = dve.n
                    VE(_mask1)

                def _exp1(s, idx=idx, hh=hh, kci=kci, qs=qs):
                    sp = (psA, psB)[kci % 2]
                    s.wait_ge(tS, p1b_smm[(idx, kci)])
                    if (idx, kci) in p1b_mask:
                        s.wait_ge(vS, p1b_mask[(idx, kci)])
                    if idx >= 1 and kci < 2:
                        s.wait_ge(tS, p1b_av[idx - 1])
                    act.c(s.activation(
                        pt_sb[:, kci * c.TPC + qs:(kci + 1) * c.TPC],
                        sp[:, qs:c.TPC], AF.Exp,
                        bias=alik_s[:, hh * c.KC + kci:hh * c.KC + kci + 1],
                        scale=c.SCALE))
                    p1b_exp[(idx, kci)] = act.n
                SC(_exp1)

            def _rsav(t, idx=idx, buf=buf):
                for kci in range(c.KC):
                    qs = qstart(kci)
                    t.wait_ge(aS, p1b_exp[(idx, kci)])
                    first = kci == 0
                    last = kci == c.KC - 1
                    for p0, pw in bpieces(qs, c.TPC):
                        pts = pt_sb[:, kci * c.TPC + p0:
                                    kci * c.TPC + p0 + pw]
                        cs = slice(p0, p0 + pw)
                        MM(t, psD[0:1, cs], onesc_s[:, :], pts,
                           start=first, stop=last)
                        MM(t, psC[:, cs],
                           vn_sb[buf][:, kci * 128:(kci + 1) * 128],
                           pts, start=first, stop=last)
                pemark()
                p1b_av[idx] = pe.n
            TE(_rsav)

            def _rr(v, idx=idx):
                v.wait_ge(tS, p1b_av[idx])
                if idx >= 1:
                    qsc.w(v, "otw")
                dve.c(v.reciprocal(rrow_s[0:1, 0:c.TPC], psD[0:1, 0:c.TPC]))
                v.wait_ge(vS, dve.n)
                dve.c(v.tensor_copy(rrbf_s[0:1, 0:c.TPC],
                                    rrow_s[0:1, 0:c.TPC]))
                v.wait_ge(vS, dve.n)
                dve.c(v.tensor_tensor(rrow_s[0:1, c.TPC:2 * c.TPC],
                                      rrow_s[0:1, 0:c.TPC],
                                      rrbf_s[0:1, 0:c.TPC], ALU.subtract))
                v.wait_ge(vS, dve.n)
                dve.c(v.tensor_copy(rrbf_s[0:1, c.TPC:2 * c.TPC],
                                    rrow_s[0:1, c.TPC:2 * c.TPC]))
                p1b_rr[idx] = dve.n
            VE(_rr)

            def _otf(s, idx=idx):
                s.wait_ge(tS, p1b_av[idx])
                if idx >= 1:
                    s.wait_ge(vS, p1b_otb[idx - 1])
                act.c(s.copy(ot_f[:, :], psC[:, 0:c.TPC]))
                p1b_otf[idx] = act.n
            SC(_otf)

            def _otb(v, idx=idx):
                v.wait_ge(aS, p1b_otf[idx])
                if idx >= 1:
                    qsc.w(v, "otw")
                dve.c(v.tensor_copy(ot_b[:, :], ot_f[:, :]))
                p1b_otb[idx] = dve.n
            VE(_otb)

            def _otw(s, idx=idx, hh=hh):
                s.wait_ge(vS, p1b_otb[idx])
                qsc.dma(s, "otw", ot_d[hh, :, :], ot_b[:, :])
                s.wait_ge(vS, p1b_rr[idx])
                qsc.dma(s, "otw", rrl_d[2 * hh:2 * hh + 1, :],
                        rrbf_s[0:1, 0:c.TPC])
                qsc.dma(s, "otw", rrl_d[2 * hh + 1:2 * hh + 2, :],
                        rrbf_s[0:1, c.TPC:2 * c.TPC])
            SC(_otw)

            p1b_done_pe[idx] = pe.n
            dq_emit_some(2)

        if os.environ.get("MK_UPTO") == "p1b":
            def _fin0(sy):
                qsc.w(sy, "otw")
            SY(_fin0)
            st_p1b.close()
            st_dq.close()
            return nc

        st_p1b.close()
        st_p2 = contextlib.ExitStack()
        otm_sb = sbin(st_p2, "otm_sb", [128, c.NH * c.TPC], BF16)
        rr2_sb = sbin(st_p2, "rr2_sb", [2, 2 * c.TPC], BF16)
        wpl_sb = [sbin(st_p2, f"wpl{i}", [128, c.HC * 128], BF16)
                  for i in range(2)]
        xrt_sb = [sbin(st_p2, f"xrt{i}", [128, c.TPC], F32) for i in range(2)]
        rch_sb = sbin(st_p2, "rch_sb", [128, 2 * c.TPC], F32)
        sqp_sb = sbin(st_p2, "sqp_sb", [128, c.TPC], F32)
        ab_sb = sbin(st_p2, "ab_sb", [2, 2 * c.TPC], BF16)
        xlh_sb = sbin(st_p2, "xlh_sb", [128, 2 * c.TPC], BF16)
        xll_sb = sbin(st_p2, "xll_sb", [128, 2 * c.TPC], BF16)

        p2_norm_mm = {}
        p2_norm = {}
        p2_mm = {}
        p2_ev = {}
        p2_row = {}

        def _p2_ld(g):
            qsc.w(g, "otw")
            g.wait_ge(tS, p1b_done_pe[c.NH - 1])
            qsy.w(g, "qk0")
            qsy.w(g, "qk1")
            qgp.w(g, "frow")
            nq4 = c.NH // 4
            for q4 in range(4):
                qgp.dma(g, "p2ld",
                        otm_sb[:, q4 * nq4 * c.TPC:(q4 + 1) * nq4 * c.TPC]
                        .rearrange("p (h t) -> p h t", t=c.TPC),
                        ot_d[q4 * nq4:(q4 + 1) * nq4, :, :]
                        .rearrange("h p t -> p h t"))
        GP(_p2_ld)

        for hc in range(c.NH):
            def _nld(g, hc=hc):
                if hc == 0:
                    qgp.w(g, "p2ld")
                if hc >= 2:
                    g.wait_ge(tS, p2_norm_mm[hc - 2])
                qgp.dma(g, f"rr{hc % 2}",
                        rr2_sb[:, (hc % 2) * c.TPC:(hc % 2 + 1) * c.TPC],
                        rrl_d[2 * hc:2 * hc + 2, :])
            GP(_nld)

            def _nmm(t, hc=hc):
                qgp.w(t, f"rr{hc % 2}")
                pp = (psA, psB)[hc % 2]
                if hc >= 2:
                    t.wait_ge(vS, p2_norm[hc - 2])
                for h0, hw in chunks(c.TPC):
                    MM(t, pp[:, h0:h0 + hw], ones22_s[:, :],
                       rr2_sb[:, (hc % 2) * c.TPC + h0:
                              (hc % 2) * c.TPC + h0 + hw],
                       start=True, stop=True)
                pemark()
                p2_norm_mm[hc] = pe.n
            TE(_nmm)

            def _nmul(v, hc=hc):
                v.wait_ge(tS, p2_norm_mm[hc])
                pp = (psA, psB)[hc % 2]
                dve.c(v.tensor_tensor(otm_sb[:, hc * c.TPC:(hc + 1) * c.TPC],
                                      otm_sb[:, hc * c.TPC:(hc + 1) * c.TPC],
                                      pp[:, 0:c.TPC], ALU.mult))
                p2_norm[hc] = dve.n
            VE(_nmul)
            dq_emit_some(1)

        def p2_wld(oc):
            def _w(g, oc=oc):
                if oc >= 2:
                    g.wait_ge(tS, p2_mm[oc - 2])
                qgp.dma(g, f"wp{oc % 2}", wpl_sb[oc % 2][:, :]
                        .rearrange("p (h q) -> p h q", q=128),
                        wp4[oc, :, :, :].rearrange("h p q -> p h q"))
            GP(_w)

        def p2_xld(oc):
            def _x(sy, oc=oc):
                if oc < 2:
                    sy.wait_ge(tS, p1b_done_pe[c.NH - 1])
                    sy.wait_ge(aS, p1b_otf[c.NH - 1])
                    sy.wait_ge(vS, p1b_rr[c.NH - 1])
                    sy.wait_ge(vS, p1b_otb[c.NH - 1])
                    qsc.w(sy, "otw")
                if oc >= 2:
                    sy.wait_ge(vS, p2_ev[(oc - 2, "res")])
                qsy.dma(sy, f"xr{oc % 2}", xrt_sb[oc % 2][:, :],
                        xT3[oc, :, 0:c.TPC])
            SY(_x)

        p2_wld(0)
        p2_wld(1)
        p2_xld(0)
        p2_xld(1)
        for oc in range(c.HC):
            def _pmm(t, oc=oc):
                pp = (psA, psB)[oc % 2]
                qgp.w(t, f"wp{oc % 2}")
                if oc == 0:
                    t.wait_ge(vS, p2_norm[c.NH - 1])
                if oc >= 2:
                    t.wait_ge(aS, p2_ev[(oc - 2, "copy")])
                for hc in range(c.NH):
                    for h0, hw in chunks(c.TPC):
                        MM(t, pp[:, h0:h0 + hw],
                           wpl_sb[oc % 2][:, hc * 128:(hc + 1) * 128],
                           otm_sb[:, hc * c.TPC + h0:hc * c.TPC + h0 + hw],
                           start=(hc == 0), stop=(hc == c.NH - 1))
                pemark()
                p2_mm[oc] = pe.n
            TE(_pmm)
            if oc + 2 < c.HC:
                p2_wld(oc + 2)

            def _pev(s, oc=oc):
                pp = (psA, psB)[oc % 2]
                es = oc % 2
                s.wait_ge(tS, p2_mm[oc])
                if oc >= 2:
                    qsc.w(s, f"prest{oc % 2}")
                act.c(s.activation(rch_sb[:, es * c.TPC:(es + 1) * c.TPC],
                                   pp[:, 0:c.TPC], AF.Identity,
                                   bias=bpc_s[:, oc:oc + 1]))
                p2_ev[(oc, "copy")] = act.n
            SC(_pev)

            def _pres(v, oc=oc):
                es = oc % 2
                rr = rch_sb[:, es * c.TPC:(es + 1) * c.TPC]
                v.wait_ge(aS, p2_ev[(oc, "copy")])
                qsy.w(v, f"xr{oc % 2}")
                dve.c(v.tensor_tensor(rr, rr, xrt_sb[oc % 2][:, :], ALU.add))
                p2_ev[(oc, "res")] = dve.n
            VE(_pres)
            if oc + 2 < c.HC:
                p2_xld(oc + 2)

            def _psq(s, oc=oc):
                es = oc % 2
                s.wait_ge(vS, p2_ev[(oc, "res")])
                act.c(s.activation(sqp_sb[:, :],
                                   rch_sb[:, es * c.TPC:(es + 1) * c.TPC],
                                   AF.Square))
                p2_ev[(oc, "sq")] = act.n
            SC(_psq)

            def _prow(t, oc=oc):
                es = oc % 2
                t.wait_ge(vS, p2_ev[(oc, "res")])
                for h0, hw in chunks(c.TPC):
                    MM(t, psC[0:1, h0:h0 + hw], onescf_s[:, :],
                       rch_sb[:, es * c.TPC + h0:es * c.TPC + h0 + hw],
                       start=(oc == 0), stop=(oc == c.HC - 1))
                t.wait_ge(aS, p2_ev[(oc, "sq")])
                for h0, hw in chunks(c.TPC):
                    MM(t, psD[0:1, h0:h0 + hw], onescf_s[:, :],
                       sqp_sb[:, h0:h0 + hw],
                       start=(oc == 0), stop=(oc == c.HC - 1))
                pemark()
                p2_row[oc] = pe.n
            TE(_prow)

            def _pst(s, oc=oc):
                es = oc % 2
                s.wait_ge(tS, p2_row[oc])
                qsc.dma(s, f"prest{oc % 2}", pres_d[oc, :, :],
                        rch_sb[:, es * c.TPC:(es + 1) * c.TPC])
            SC(_pst)

            dq_emit_some(1)

        lnr_s = sbin(st_p2, "lnr_s", [1, 8 * c.TPC], F32)

        def _l1r(v):
            v.wait_ge(tS, p2_row[c.HC - 1])
            dve.c(v.tensor_scalar(lnr_s[0:1, 0:c.TPC], psC[0:1, 0:c.TPC],
                                  1.0 / c.DM, None, ALU.mult))
            dve.c(v.tensor_scalar(lnr_s[0:1, c.TPC:2 * c.TPC],
                                  psD[0:1, 0:c.TPC], 1.0 / c.DM, None,
                                  ALU.mult))
            v.wait_ge(vS, dve.n)
            dve.c(v.tensor_tensor(lnr_s[0:1, 2 * c.TPC:3 * c.TPC],
                                  lnr_s[0:1, 0:c.TPC], lnr_s[0:1, 0:c.TPC],
                                  ALU.mult))
            v.wait_ge(vS, dve.n)
            dve.c(v.tensor_tensor(lnr_s[0:1, c.TPC:2 * c.TPC],
                                  lnr_s[0:1, c.TPC:2 * c.TPC],
                                  lnr_s[0:1, 2 * c.TPC:3 * c.TPC],
                                  ALU.subtract))
            v.wait_ge(vS, dve.n)
            dve.c(v.tensor_scalar(lnr_s[0:1, c.TPC:2 * c.TPC],
                                  lnr_s[0:1, c.TPC:2 * c.TPC], 1e-5, None,
                                  ALU.add))
            rt["l1var"] = dve.n
        VE(_l1r)

        def _l1s(s):
            s.wait_ge(vS, rt["l1var"])
            act.c(s.activation(lnr_s[0:1, 3 * c.TPC:4 * c.TPC],
                               lnr_s[0:1, c.TPC:2 * c.TPC], AF.Sqrt))
            rt["l1std"] = act.n
        SC(_l1s)

        def _l1v2(v):
            v.wait_ge(aS, rt["l1std"])
            dve.c(v.reciprocal(lnr_s[0:1, 4 * c.TPC:5 * c.TPC],
                               lnr_s[0:1, 3 * c.TPC:4 * c.TPC]))
            v.wait_ge(vS, dve.n)
            dve.c(v.tensor_tensor(lnr_s[0:1, 5 * c.TPC:6 * c.TPC],
                                  lnr_s[0:1, 0:c.TPC],
                                  lnr_s[0:1, 4 * c.TPC:5 * c.TPC], ALU.mult))
            v.wait_ge(vS, dve.n)
            dve.c(v.tensor_scalar(lnr_s[0:1, 5 * c.TPC:6 * c.TPC],
                                  lnr_s[0:1, 5 * c.TPC:6 * c.TPC], -1.0,
                                  None, ALU.mult))
            rt["l1ab"] = dve.n
        VE(_l1v2)

        def _l1b1(g):
            g.wait_ge(vS, rt["l1ab"])
            qgp.dma(g, "l1b", abrow_d[0:1, :], lnr_s[0:1, 4 * c.TPC:5 * c.TPC])
            qgp.dma(g, "l1b", abrow_d[2:3, :], lnr_s[0:1, 5 * c.TPC:6 * c.TPC])
            qgp.w(g, "l1b")
            qgp.dma(g, "l1b", ab_sb[0:1, 0:c.TPC], abrow_d[0:1, :])
            qgp.dma(g, "l1b", ab_sb[0:1, c.TPC:2 * c.TPC], abrow_d[2:3, :])
        GP(_l1b1)

        def _l1lo(v):
            qgp.w(v, "l1b")
            dve.c(v.tensor_tensor(lnr_s[0:1, 6 * c.TPC:7 * c.TPC],
                                  lnr_s[0:1, 4 * c.TPC:5 * c.TPC],
                                  ab_sb[0:1, 0:c.TPC], ALU.subtract))
            dve.c(v.tensor_tensor(lnr_s[0:1, 7 * c.TPC:8 * c.TPC],
                                  lnr_s[0:1, 5 * c.TPC:6 * c.TPC],
                                  ab_sb[0:1, c.TPC:2 * c.TPC], ALU.subtract))
            rt["l1lo"] = dve.n
        VE(_l1lo)

        def _l1b2(g):
            g.wait_ge(vS, rt["l1lo"])
            qgp.dma(g, "l1b", abrow_d[1:2, :], lnr_s[0:1, 6 * c.TPC:7 * c.TPC])
            qgp.dma(g, "l1b", abrow_d[3:4, :], lnr_s[0:1, 7 * c.TPC:8 * c.TPC])
            qgp.w(g, "l1b")
            qgp.dma(g, "l1b", ab_sb[:, 0:c.TPC], abrow_d[0:2, :])
            qgp.dma(g, "l1b", ab_sb[:, c.TPC:2 * c.TPC], abrow_d[2:4, :])
        GP(_l1b2)

        def _l1bc(t):
            qgp.w(t, "l1b")
            for h0, hw in chunks(c.TPC):
                MM(t, psA[:, h0:h0 + hw], ones22_s[:, :],
                   ab_sb[:, h0:h0 + hw], start=True, stop=True)
                MM(t, psB[:, h0:h0 + hw], ones22_s[:, :],
                   ab_sb[:, c.TPC + h0:c.TPC + h0 + hw],
                   start=True, stop=True)
            pemark()
            rt["l1bc"] = pe.n
        TE(_l1bc)

        p2b = {}
        for oc in range(c.HC):
            def _n2ld(sy, oc=oc):
                if oc < 2:
                    qsc.w(sy, f"prest{oc % 2}")
                else:
                    sy.wait_ge(vS, p2b[(oc - 2, "lo")])
                qsy.dma(sy, f"prl{oc % 2}",
                        rch_sb[:, (oc % 2) * c.TPC:(oc % 2 + 1) * c.TPC],
                        pres_d[oc, :, :])
            SY(_n2ld)

            def _n2v(v, oc=oc):
                es = oc % 2
                rr = rch_sb[:, es * c.TPC:(es + 1) * c.TPC]
                qsy.w(v, f"prl{oc % 2}")
                if oc == 0:
                    v.wait_ge(tS, rt["l1bc"])
                if oc >= 2:
                    qsc.w(v, f"xst{oc % 2}")
                dve.c(v.tensor_tensor(rr, rr, psA[:, 0:c.TPC], ALU.mult))
                v.wait_ge(vS, dve.n)
                dve.c(v.tensor_tensor(rr, rr, psB[:, 0:c.TPC], ALU.add))
                v.wait_ge(vS, dve.n)
                xh = xlh_sb[:, es * c.TPC:(es + 1) * c.TPC]
                xl = xll_sb[:, es * c.TPC:(es + 1) * c.TPC]
                dve.c(v.tensor_copy(xh, rr))
                v.wait_ge(vS, dve.n)
                p2b[(oc, "hi")] = dve.n
                dve.c(v.tensor_tensor(xl, rr, xh, ALU.subtract))
                p2b[(oc, "lo")] = dve.n
            VE(_n2v)

            def _n2st(s, oc=oc):
                es = oc % 2
                xh = xlh_sb[:, es * c.TPC:(es + 1) * c.TPC]
                xl = xll_sb[:, es * c.TPC:(es + 1) * c.TPC]
                s.wait_ge(vS, p2b[(oc, "lo")])
                qsc.dma(s, f"xst{oc % 2}", xlnh_d[oc * 128:(oc + 1) * 128, :],
                        xh)
                qsc.dma(s, f"xst{oc % 2}", xlnl_d[oc * 128:(oc + 1) * 128, :],
                        xl)
            SC(_n2st)
            dq_emit_some(1)

        if os.environ.get("MK_UPTO") == "p2":
            def _fin0(sy):
                qsc.w(sy, "xst0")
                qsc.w(sy, "xst1")
            SY(_fin0)
            st_p2.close()
            st_dq.close()
            return nc

        st_p2.close()
        st_p3a = contextlib.ExitStack()
        xf_sb = sbin(st_p3a, "xf_sb", [128, c.HC * c.TPC], BF16)
        wgl_sb = [sbin(st_p3a, f"wgl{i}", [128, c.HC * 128], BF16)
                  for i in range(2)]
        wul_sb = [sbin(st_p3a, f"wul{i}", [128, c.HC * 128], BF16)
                  for i in range(2)]
        gu_sb = sbin(st_p3a, "gu_sb", [128, c.TPC], BF16)
        int_sb = [sbin(st_p3a, f"int{i}", [128, c.TPC], BF16)
                  for i in range(2)]

        xlnv = xlnh_d.rearrange("(h p) t -> h p t", p=128)
        p3_gu = {}
        p3_int = {}
        pgS = sem("pgS")

        def _pgate(sy):
            qsc.w(sy, "xst0")
            qsc.w(sy, "xst1")
            qsc.w(sy, "prest0")
            qsc.w(sy, "prest1")
            qsy.w(sy, "prl0")
            qsy.w(sy, "prl1")
            qsy.w(sy, "xr0")
            qsy.w(sy, "xr1")
            qgp.w(sy, "l1b")
            qgp.w(sy, "p2ld")
            qgp.w(sy, "rr0")
            qgp.w(sy, "rr1")
            qgp.w(sy, "wp0")
            qgp.w(sy, "wp1")
            sy.wait_ge(vS, p2b[(c.HC - 1, "lo")])
            sy.wait_ge(aS, rt["l1std"])
            sy.wait_ge(tS, rt["l1bc"])
            sy.nop().then_inc(pgS, 1)
        SY(_pgate)

        def _p3x(sy):
            sy.wait_ge(pgS, 1)
            nq4 = c.HC // 4
            for q4 in range(4):
                qsy.dma(sy, "p3x", xf_sb[:, q4 * nq4 * c.TPC:
                                         (q4 + 1) * nq4 * c.TPC]
                        .rearrange("p (h t) -> p h t", t=c.TPC),
                        xlnv[q4 * nq4:(q4 + 1) * nq4, :, :]
                        .rearrange("h p t -> p h t"))
        SY(_p3x)

        def p3_guload(j):
            def _w(s, j=j):
                sl = j % 2
                if j < 2:
                    s.wait_ge(pgS, 1)
                if j >= 2:
                    s.wait_ge(tS, p3_gu[j - 2])
                dq_wait_spilled(s, (min(j // 8, c.NCGU - 1) + 1)
                                * 2 * c.HC - 1)
                nq4 = c.HC // 4
                for q4 in range(4):
                    qsc.dma(s, f"wgu{sl}", wgl_sb[sl][:, q4 * nq4 * 128:
                                                      (q4 + 1) * nq4 * 128]
                            .rearrange("p (h q) -> p h q", q=128),
                            wg_deq[:, j, q4 * nq4:(q4 + 1) * nq4, :])
                    qsc.dma(s, f"wgu{sl}", wul_sb[sl][:, q4 * nq4 * 128:
                                                      (q4 + 1) * nq4 * 128]
                            .rearrange("p (h q) -> p h q", q=128),
                            wu_deq[:, j, q4 * nq4:(q4 + 1) * nq4, :])
            SC(_w)

        p3_guload(0)
        p3_guload(1)
        for j in range(c.JC):
            def _gu(t, j=j):
                sl = j % 2
                psG = (psA, psB)[j % 2]
                psU = (psC, psD)[j % 2]
                qsc.w(t, f"wgu{sl}")
                if j == 0:
                    qsy.w(t, "p3x")
                    t.wait_ge(tS, rt["l1bc"])
                    t.wait_ge(vS, p2b[(c.HC - 1, "lo")])
                if j >= 2:
                    t.wait_ge(vS, p3_int[j - 2])
                for hc in range(c.HC):
                    lg = wgl_sb[sl][:, hc * 128:(hc + 1) * 128]
                    lu = wul_sb[sl][:, hc * 128:(hc + 1) * 128]
                    for h0, hw in chunks(c.TPC):
                        xr_ = xf_sb[:, hc * c.TPC + h0:hc * c.TPC + h0 + hw]
                        MM(t, psG[:, h0:h0 + hw], lg, xr_,
                           start=(hc == 0), stop=(hc == c.HC - 1))
                        MM(t, psU[:, h0:h0 + hw], lu, xr_,
                           start=(hc == 0), stop=(hc == c.HC - 1))
                pemark()
                p3_gu[j] = pe.n
            TE(_gu)
            if j + 2 < c.JC:
                p3_guload(j + 2)

            def _silu(s, j=j):
                psG = (psA, psB)[j % 2]
                s.wait_ge(tS, p3_gu[j])
                if j >= 1:
                    s.wait_ge(vS, p3_int[j - 1])
                act.c(s.activation(gu_sb[:, :], psG[:, 0:c.TPC], AF.Silu))
                p3_gu[(j, "silu")] = act.n
            SC(_silu)

            def _mul(v, j=j):
                psU = (psC, psD)[j % 2]
                v.wait_ge(aS, p3_gu[(j, "silu")])
                if j >= 2:
                    qsy.w(v, f"ist{j % 2}")
                dve.c(v.tensor_tensor(int_sb[j % 2][:, :], gu_sb[:, :],
                                      psU[:, 0:c.TPC], ALU.mult))
                p3_int[j] = dve.n
            VE(_mul)

            def _ist(sy, j=j):
                sy.wait_ge(vS, p3_int[j])
                qsy.dma(sy, f"ist{j % 2}", inter_d[j, :, :],
                        int_sb[j % 2][:, :])
            SY(_ist)
            dq_emit_some(12)

        if os.environ.get("MK_UPTO") == "p3a":
            def _fin0(sy):
                qsy.w(sy, "ist0")
                qsy.w(sy, "ist1")
            SY(_fin0)
            st_p3a.close()
            st_dq.close()
            return nc

        st_p3a.close()
        st_p3b = contextlib.ExitStack()
        JH = c.JC // 2
        wdl_sb = [sbin(st_p3b, f"wdl{i}", [128, JH * 512], BF16)
                  for i in range(2)]
        it_sb = [sbin(st_p3b, f"it{i}", [128, c.JC * 128], BF16)
                 for i in range(2)]
        yev_sb = [sbin(st_p3b, f"yev{i}", [128, 512], F32) for i in range(2)]
        l2a_sb = sbin(st_p3b, "l2a_sb", [128, c.DM], F32)
        DH = c.DM // 2
        l2h_sb = sbin(st_p3b, "l2h_sb", [128, DH], BF16)

        NDB = c.DM // 512
        p3_dn = {}
        p3_ev = {}
        p3_ln = {}

        def p3_wdload(db, half):
            def _w(s, db=db, half=half):
                sl = half
                if db >= 1:
                    s.wait_ge(tS, p3_dn[(db - 1, c.CPT - 1)])
                if db == 0:
                    qsy.w(s, "ist0")
                    qsy.w(s, "ist1")
                dq_wait_spilled(s, NGU_T + (db // 2 + 1) * c.JC - 1)
                j0 = half * JH
                qsc.dma(s, f"wdq{sl}", wdl_sb[sl][:, :]
                        .rearrange("p (j q) -> p j q", q=512),
                        wd_deq[j0:j0 + JH, :, db * 512:(db + 1) * 512]
                        .rearrange("j p q -> p j q"))
            SC(_w)

        def p3_itload(db, tc):
            def _i(sy, db=db, tc=tc):
                idx = db * c.CPT + tc
                if idx >= 2:
                    pdb, ptc = (idx - 2) // c.CPT, (idx - 2) % c.CPT
                    sy.wait_ge(tS, p3_dn[(pdb, ptc)])
                if idx == 0:
                    qsy.w(sy, "ist0")
                    qsy.w(sy, "ist1")
                qsy.dma(sy, f"it{idx % 2}", it_sb[idx % 2][:, :]
                        .rearrange("p (j q) -> p j q", q=128),
                        inter_d[:, :, tc * 128:(tc + 1) * 128]
                        .rearrange("j p q -> p j q"))
            SY(_i)

        p3_wdload(0, 0)
        p3_wdload(0, 1)
        for pre in range(min(2, NDB * c.CPT)):
            p3_itload(pre // c.CPT, pre % c.CPT)
        for db in range(NDB):
            for tc in range(c.CPT):
                idx = db * c.CPT + tc
                slot = idx % 4
                pp = (psA, psB)[slot // 2]
                pcol = (slot % 2) * 512

                def _dn(t, db=db, tc=tc, idx=idx, pp=pp, pcol=pcol):
                    if tc == 0:
                        qsc.w(t, "wdq0")
                        qsc.w(t, "wdq1")
                    qsy.w(t, f"it{idx % 2}")
                    if idx >= 4:
                        t.wait_ge(aS, p3_ev[((idx - 4) // c.CPT,
                                            (idx - 4) % c.CPT)])
                    for j in range(c.JC):
                        half, jl = divmod(j, JH)
                        lh = it_sb[idx % 2][:, j * 128:(j + 1) * 128]
                        MM(t, pp[:, pcol:pcol + 512], lh,
                           wdl_sb[half][:, jl * 512:(jl + 1) * 512],
                           start=(j == 0), stop=(j == c.JC - 1))
                    pemark()
                    p3_dn[(db, tc)] = pe.n
                TE(_dn)
                if idx + 2 < NDB * c.CPT:
                    p3_itload((idx + 2) // c.CPT, (idx + 2) % c.CPT)

                def _ev(s, db=db, tc=tc, idx=idx, pp=pp, pcol=pcol):
                    es = idx % 2
                    s.wait_ge(tS, p3_dn[(db, tc)])
                    if idx >= 2:
                        qsc.w(s, f"ydw{idx % 2}")
                    act.c(s.copy(yev_sb[es][:, :], pp[:, pcol:pcol + 512]))
                    p3_ev[(db, tc)] = act.n
                    s.wait_ge(aS, p3_ev[(db, tc)])
                    qsc.dma(s, f"ydw{idx % 2}",
                            yd[tc * 128:(tc + 1) * 128,
                               db * 512:(db + 1) * 512], yev_sb[es][:, :])
                SC(_ev)
                dq_emit_some(6)
            if db + 1 < NDB:
                p3_wdload(db + 1, 0)
                p3_wdload(db + 1, 1)

        def emit_ln2(w2):
            def _l2ld(sy, w2=w2):
                if w2 == 0:
                    qsc.w(sy, "ydw0")
                    qsc.w(sy, "ydw1")
                if w2 >= 1:
                    qsy.w(sy, "yst")
                qsy.dma(sy, "l2ld", l2a_sb[:, :],
                        yd[w2 * 128:(w2 + 1) * 128, :])
                qsy.dmat(sy, "l2ld", l2h_sb[:, :],
                         xlnh_d[0:DH, w2 * 128:(w2 + 1) * 128])
            SY(_l2ld)

            rsrc = [(xlnh_d, 1), (xlnl_d, 0), (xlnl_d, 1)]
            for rnd in range(4):
                def _l2a0(v, w2=w2, rnd=rnd):
                    qsy.w(v, "l2ld")
                    hf = rnd % 2
                    dve.c(v.tensor_tensor(l2a_sb[:, hf * DH:(hf + 1) * DH],
                                          l2a_sb[:, hf * DH:(hf + 1) * DH],
                                          l2h_sb[:, :], ALU.add))
                    p3_ln[(w2, "add", rnd)] = dve.n
                VE(_l2a0)
                if rnd < 3:
                    def _l2ld2(sy, w2=w2, rnd=rnd):
                        sy.wait_ge(vS, p3_ln[(w2, "add", rnd)])
                        src, hf = rsrc[rnd]
                        qsy.dmat(sy, "l2ld", l2h_sb[:, :],
                                 src[hf * DH:(hf + 1) * DH,
                                     w2 * 128:(w2 + 1) * 128])
                    SY(_l2ld2)

            def _l2a(v, w2=w2):
                v.wait_ge(vS, p3_ln[(w2, "add", 3)])
                dve.c(v.tensor_reduce(rows_s[:, 0:1], l2a_sb[:, :], AX,
                                      ALU.add))
                p3_ln[(w2, "sum")] = dve.n
            VE(_l2a)

            def _l2sq(s, w2=w2):
                s.wait_ge(vS, p3_ln[(w2, "sum")])
                act.c(s.activation(l2h_sb[:, :], l2a_sb[:, 0:DH], AF.Square,
                                   accum_out=rows_s[:, 1:2]))
                s.wait_ge(aS, act.n)
                act.c(s.activation(l2h_sb[:, :], l2a_sb[:, DH:c.DM],
                                   AF.Square, accum_out=rows_s[:, 9:10]))
                p3_ln[(w2, "sq")] = act.n
            SC(_l2sq)

            def _l2v0(v, w2=w2):
                v.wait_ge(aS, p3_ln[(w2, "sq")])
                dve.c(v.tensor_tensor(rows_s[:, 1:2], rows_s[:, 1:2],
                                      rows_s[:, 9:10], ALU.add))
                v.wait_ge(vS, dve.n)
                dve.c(v.tensor_scalar(rows_s[:, 2:3], rows_s[:, 0:1],
                                      1.0 / c.DM, None, ALU.mult))
                v.wait_ge(vS, dve.n)
                dve.c(v.tensor_tensor(rows_s[:, 3:4], rows_s[:, 2:3],
                                      rows_s[:, 2:3], ALU.mult))
                v.wait_ge(vS, dve.n)
                dve.c(v.tensor_scalar(rows_s[:, 4:5], rows_s[:, 1:2],
                                      1.0 / c.DM, None, ALU.mult))
                v.wait_ge(vS, dve.n)
                dve.c(v.tensor_tensor(rows_s[:, 5:6], rows_s[:, 4:5],
                                      rows_s[:, 3:4], ALU.subtract))
                v.wait_ge(vS, dve.n)
                dve.c(v.tensor_scalar(rows_s[:, 5:6], rows_s[:, 5:6], 1e-5,
                                      None, ALU.add))
                p3_ln[(w2, "var")] = dve.n
            VE(_l2v0)

            def _l2s(s, w2=w2):
                s.wait_ge(vS, p3_ln[(w2, "var")])
                act.c(s.activation(rows_s[:, 6:7], rows_s[:, 5:6], AF.Sqrt))
                p3_ln[(w2, "std")] = act.n
            SC(_l2s)

            def _l2ap(v, w2=w2):
                v.wait_ge(aS, p3_ln[(w2, "std")])
                dve.c(v.reciprocal(rows_s[:, 7:8], rows_s[:, 6:7]))
                v.wait_ge(vS, dve.n)
                dve.c(v.tensor_scalar(l2a_sb[:, :], l2a_sb[:, :],
                                      rows_s[:, 2:3], rows_s[:, 7:8],
                                      ALU.subtract, ALU.mult))
                p3_ln[(w2, "apply")] = dve.n
            VE(_l2ap)

            def _l2y(sy, w2=w2):
                sy.wait_ge(vS, p3_ln[(w2, "apply")])
                qsy.dma(sy, "yst", y[w2 * 128:(w2 + 1) * 128, :],
                        l2a_sb[:, :])
            SY(_l2y)

        for w2 in range(c.CPT):
            emit_ln2(w2)

        def _fin(sy):
            qsy.w(sy, "yst")
        SY(_fin)
        st_p3b.close()
        st_dq.close()

    return nc


_CACHE = {}


def kernel(L0_attn_hidden_states, L0_attn_w_attn, L0_attn_b_attn,
           L0_attn_w_proj, L0_attn_b_proj, L0_ffn_gate_codes,
           L0_ffn_gate_scales, L0_ffn_up_codes, L0_ffn_up_scales,
           L0_ffn_down_codes, L0_ffn_down_scales):
    cfg = Cfg()
    in_maps = stage_inputs(
        cfg, L0_attn_hidden_states, L0_attn_w_attn, L0_attn_b_attn,
        L0_attn_w_proj, L0_attn_b_proj, L0_ffn_gate_codes, L0_ffn_gate_scales,
        L0_ffn_up_codes, L0_ffn_up_scales, L0_ffn_down_codes,
        L0_ffn_down_scales)
    if "nc" not in _CACHE:
        _CACHE["nc"] = build_program(cfg)
    nc = _CACHE["nc"]
    res = run_bass_kernel_spmd(nc, in_maps, core_ids=list(range(cfg.NC)))
    out = np.concatenate([r["y"] for r in res.results], axis=0)
    return out.reshape(cfg.B, cfg.S, cfg.DM).astype(np.float32)


# version 10
# speedup vs baseline: 1.1994x; 1.0040x over previous
import contextlib
import math
import os

import numpy as np
import ml_dtypes

import concourse.bass as bass
import concourse.mybir as mybir
from concourse.bass_utils import run_bass_kernel_spmd

F32 = mybir.dt.float32
BF16 = mybir.dt.bfloat16
F16 = mybir.dt.float16
I32 = mybir.dt.int32
U8 = mybir.dt.uint8
ALU = mybir.AluOpType
AF = mybir.ActivationFunctionType
AX = mybir.AxisListType.X
NPBF = ml_dtypes.bfloat16

NF4 = np.array([
    -1.0, -0.6961928009986877, -0.5250730514526367, -0.39491748809814453,
    -0.28444138169288635, -0.18477343022823334, -0.09105003625154495, 0.0,
    0.07958029955625534, 0.16093020141124725, 0.24611230194568634,
    0.33791524171829224, 0.44070982933044434, 0.5626170039176941,
    0.7229568362236023, 1.0], dtype=np.float64)

DQ_CA = 4.5
DQ_CB = 10.5
NFG = (NF4[1:] - NF4[:-1])
NFDA = (NF4[:-1] - NFG * (np.arange(15) - DQ_CA))
NFDB = (NF4[:-1] - NFG * (np.arange(15) - DQ_CB))
NFG = [float(v) for v in NFG]
NFDA = [float(v) for v in NFDA]
NFDB = [float(v) for v in NFDB]

DEAD = -60000.0


def alibi_slopes(n):
    cp2 = 2 ** math.floor(math.log2(n))
    base = 2 ** (-(2 ** (-(math.log2(cp2) - 3))))
    slopes = base ** np.arange(1, cp2 + 1, dtype=np.float32)
    if cp2 != n:
        eb = 2 ** (-(2 ** (-(math.log2(2 * cp2) - 3))))
        extra = eb ** np.arange(1, 1 + 2 * min(cp2, n - cp2), 2,
                                dtype=np.float32)
        slopes = np.concatenate([slopes, extra])
    return slopes.astype(np.float32)


class Cfg:
    def __init__(self, NH=32, B=4, S=2048, I=11008):
        self.NC = 8
        self.D = 128
        self.NH = NH
        self.DM = NH * self.D
        self.HC = self.DM // 128
        self.B = B
        self.S = S
        self.TB = S
        self.TPC = S // 2
        self.KC = S // 128
        self.KH = self.KC // 2
        self.CC = 3 * NH
        self.CCH = 6
        self.NPASS = self.CC // self.CCH
        self.TSUP = min(512, S)
        self.NTS = self.TB // self.TSUP
        self.I = I
        self.JC = I // 128
        self.SCALE = 1.0 / self.D
        self.DQW = 1024
        self.NCGU = -(-I // self.DQW)
        self.NCD = self.DM // self.DQW
        self.CPT = self.TPC // 128


def stage_inputs(cfg, x, w_attn, b_attn, w_proj, b_proj,
                 g_codes, g_scales, u_codes, u_scales, d_codes, d_scales):
    c = cfg
    slopes = alibi_slopes(c.NH)
    x = np.asarray(x, dtype=np.float32)
    w_attn = np.asarray(w_attn, dtype=np.float32)
    b_attn = np.asarray(b_attn, dtype=np.float32)
    w_proj = np.asarray(w_proj, dtype=np.float32)
    b_proj = np.asarray(b_proj, dtype=np.float32)

    wq = np.concatenate(
        [w_attn[:, (g * c.DM + h * 128):(g * c.DM + (h + 1) * 128)]
         for h in range(c.NH) for g in range(3)], axis=1)
    wq3 = np.ascontiguousarray(wq).reshape(c.HC, 128, c.CC * 128)
    bqcols = np.stack(
        [b_attn[g * c.DM + h * 128:g * c.DM + (h + 1) * 128]
         for h in range(c.NH) for g in range(3)], axis=1)
    bq = np.ascontiguousarray(bqcols.astype(np.float32))

    wp4 = np.ascontiguousarray(
        w_proj.reshape(c.HC, 128, c.HC, 128).transpose(2, 0, 1, 3))
    bpc = np.ascontiguousarray(b_proj.reshape(c.HC, 128).T)

    gT = np.ascontiguousarray(np.asarray(g_codes).T.astype(np.uint8))
    uT = np.ascontiguousarray(np.asarray(u_codes).T.astype(np.uint8))
    dTr = np.ascontiguousarray(np.asarray(d_codes).T.astype(np.uint8))
    gcT = gT.reshape(c.HC, 128, c.I)
    ucT = uT.reshape(c.HC, 128, c.I)
    dcT = dTr.reshape(c.JC, 128, c.DM)
    gsT = np.ascontiguousarray(np.asarray(g_scales).T)
    usT = np.ascontiguousarray(np.asarray(u_scales).T)
    dsT = np.ascontiguousarray(np.asarray(d_scales).T)

    common = {
        "wq3": wq3, "bq": bq, "wp4": wp4, "bpc": bpc,
        "ones22": np.ones((2, 128), dtype=NPBF),
        "onesc": np.ones((128, 1), dtype=NPBF),
        "onesf": np.ones((128, 1), dtype=np.float32),
        "masku": np.ascontiguousarray(
            np.where(np.arange(128)[:, None] > np.arange(128)[None, :],
                     np.float32(-1e30), np.float32(0.0)).astype(np.float32)),
        "gcT": gcT, "ucT": ucT, "dcT": dcT,
        "gsT": gsT, "usT": usT, "dsT": dsT,
    }

    in_maps = []
    for r in range(c.NC):
        b, h = r // 2, r % 2
        m = dict(common)
        xb = x[b]
        xT = np.ascontiguousarray(xb.T)
        mine = xT[:, h * c.TPC:(h + 1) * c.TPC]
        othr = xT[:, (1 - h) * c.TPC:(2 - h) * c.TPC]
        m["xT3"] = np.ascontiguousarray(
            np.concatenate([mine, othr], axis=1)).reshape(c.HC, 128, c.TB)
        ak = np.zeros((128, c.NH * c.KC), np.float32)
        p = np.arange(128, dtype=np.float32)
        for hh in range(c.NH):
            sl = slopes[hh]
            for ci in range(c.KC):
                col = hh * c.KC + ci
                if ci < c.KH:
                    kg = h * c.TPC + ci * 128 + p
                    ak[:, col] = sl * kg
                elif h == 1:
                    kg = (ci - c.KH) * 128 + p
                    ak[:, col] = sl * kg
                else:
                    ak[:, col] = DEAD
        m["alik"] = ak
        aq = np.zeros((c.NH, c.TPC), np.float32)
        qg = h * c.TPC + np.arange(c.TPC, dtype=np.float32)
        for hh in range(c.NH):
            aq[hh] = -float(c.D) * slopes[hh] * qg
        m["aliqm"] = np.ascontiguousarray(aq)
        in_maps.append(m)
    return in_maps


class Q:
    def __init__(self, mksem, prefix):
        self.mksem = mksem
        self.prefix = prefix
        self.streams = {}

    def _s(self, stream):
        if stream not in self.streams:
            self.streams[stream] = [self.mksem(f"{self.prefix}_{stream}"), 0]
        return self.streams[stream]

    def dma(self, eng, stream, out, in_, **kw):
        s = self._s(stream)
        eng.dma_start(out=out, in_=in_, **kw).then_inc(s[0], 16)
        s[1] += 16
        return s[1]

    def dmat(self, eng, stream, out, in_):
        s = self._s(stream)
        eng.dma_start(out=out, in_=in_, transpose=True).then_inc(s[0], 16)
        s[1] += 16
        return s[1]

    def tot(self, stream):
        return self._s(stream)

    def w(self, eng, stream):
        s = self._s(stream)
        eng.wait_ge(s[0], s[1])


class E:
    def __init__(self, sem):
        self.sem = sem
        self.n = 0

    def c(self, inst):
        inst.then_inc(self.sem, 1)
        self.n += 1
        return self.n


def chunks(total, w=512):
    return [(i, min(w, total - i)) for i in range(0, total, w)]


def bpieces(lo, hi):
    out = []
    while lo < hi:
        nxt = min(hi, (lo // 512 + 1) * 512)
        out.append((lo, nxt - lo))
        lo = nxt
    return out


def build_program(cfg):
    c = cfg
    nc = bass.Bass("TRN2", target_bir_lowering=False)
    dt = nc.dram_tensor

    xT3 = dt("xT3", [c.HC, 128, c.TB], F32, kind="ExternalInput")
    wq3 = dt("wq3", [c.HC, 128, c.CC * 128], F32, kind="ExternalInput")
    bq = dt("bq", [128, c.CC], F32, kind="ExternalInput")
    aliqm = dt("aliqm", [c.NH, c.TPC], F32, kind="ExternalInput")
    alik = dt("alik", [128, c.NH * c.KC], F32, kind="ExternalInput")
    masku = dt("masku", [128, 128], F32, kind="ExternalInput")
    ones22 = dt("ones22", [2, 128], BF16, kind="ExternalInput")
    onesc = dt("onesc", [128, 1], BF16, kind="ExternalInput")
    onesf = dt("onesf", [128, 1], F32, kind="ExternalInput")
    wp4 = dt("wp4", [c.HC, c.HC, 128, 128], F32, kind="ExternalInput")
    bpc = dt("bpc", [128, c.HC], F32, kind="ExternalInput")
    gcT = dt("gcT", [c.HC, 128, c.I], U8, kind="ExternalInput")
    ucT = dt("ucT", [c.HC, 128, c.I], U8, kind="ExternalInput")
    dcT = dt("dcT", [c.JC, 128, c.DM], U8, kind="ExternalInput")
    gsT = dt("gsT", [c.DM // 64, c.I], F32, kind="ExternalInput")
    usT = dt("usT", [c.DM // 64, c.I], F32, kind="ExternalInput")
    dsT = dt("dsT", [c.I // 64, c.DM], F32, kind="ExternalInput")
    y = dt("y", [c.TPC, c.DM], F32, kind="ExternalOutput")

    xbf_d = dt("xbf_d", [c.HC, 128, c.TB], BF16)
    qkv_sp = dt("qkv_sp", [c.CC * 128, c.TB], BF16)
    q2k2_d = dt("q2k2_d", [2 * c.NH, c.TB], F32)
    frow_d = dt("frow_d", [2, c.TPC], BF16)
    ot_d = dt("ot_d", [c.NH, 128, c.TPC], BF16)
    rrl_d = dt("rrl_d", [2 * c.NH, c.TPC], BF16)
    pres_d = dt("pres_d", [c.HC, 128, c.TPC], F32)
    xlnh_d = dt("xlnh_d", [c.DM, c.TPC], BF16)
    xlnl_d = dt("xlnl_d", [c.DM, c.TPC], BF16)
    abrow_d = dt("abrow_d", [4, c.TPC], BF16)
    wg_deq = dt("wg_deq", [128, c.JC, c.HC, 128], BF16)
    wu_deq = dt("wu_deq", [128, c.JC, c.HC, 128], BF16)
    wd_deq = dt("wd_deq", [c.JC, 128, c.DM], BF16)
    inter_d = dt("inter_d", [c.JC, 128, c.TPC], BF16)
    yd = dt("yd", [c.TPC, c.DM], F32)

    st = contextlib.ExitStack()
    with st:
        def sbin(stack, name, shape, dtt):
            return stack.enter_context(nc.sbuf_tensor(name, shape, dtt))

        def sb(name, shape, dtt):
            return st.enter_context(nc.sbuf_tensor(name, shape, dtt))

        def psum(name, shape, dtt=F32):
            return st.enter_context(nc.psum_tensor(name, shape, dtt))

        def sem(name):
            return st.enter_context(nc.semaphore(name))

        ones22_s = sb("ones22_s", [2, 128], BF16)
        onesc_s = sb("onesc_s", [128, 1], BF16)
        onescf_s = sb("onescf_s", [128, 1], F32)
        masku_s = sb("masku_s", [128, 128], F32)
        rows_s = sb("rows_s", [128, 16], F32)
        bq_s = sb("bq_s", [128, c.CC], F32)
        alik_s = sb("alik_s", [128, c.NH * c.KC], F32)
        aliq_sb = sb("aliq_sb", [c.NH, c.TPC], F32)
        bpc_s = sb("bpc_s", [128, c.HC], F32)
        rwm = sb("rwm", [c.NH, 1], F32)
        rwf = sb("rwf", [c.NH, c.TB], F32)
        rwfh = sb("rwfh", [c.NH, c.TPC], BF16)
        rwfl = sb("rwfl", [c.NH, c.TPC], BF16)
        q2st = sb("q2st", [1, 2 * c.TSUP], F32)
        psA = psum("psA", [128, 1024])
        psB = psum("psB", [128, 1024])
        psC = psum("psC", [128, 1024])
        psD = psum("psD", [128, 1024])

        tS = sem("tS")
        aS = sem("aS")
        vS = sem("vS")

        block = st.enter_context(nc.Block())
        qsy = Q(sem, "sy")
        qsc = Q(sem, "sc")
        qgp = Q(sem, "gp")
        pe = E(tS)
        act = E(aS)
        dve = E(vS)

        SY, SC, GP, VE, TE = (block.sync, block.scalar, block.gpsimd,
                              block.vector, block.tensor)
        rt = {}
        last_mm = [None]

        def MM(t, *a, **kw):
            last_mm[0] = t.matmul(*a, **kw)
            return last_mm[0]

        def pemark():
            return pe.c(last_mm[0])

        DQW = c.DQW
        st_dq = contextlib.ExitStack()
        dq_c = sbin(st_dq, "dq_c", [128, 2 * DQW], F16)
        dq_ca = sbin(st_dq, "dq_ca", [128, DQW], F16)
        dq_cb = sbin(st_dq, "dq_cb", [128, DQW], F16)
        dq_m = sbin(st_dq, "dq_m", [128, DQW], F16)
        dq_x = sbin(st_dq, "dq_x", [128, DQW], F16)
        dq_l7 = sbin(st_dq, "dq_l7", [128, DQW], F16)
        dq_t = sbin(st_dq, "dq_t", [128, DQW], F16)
        dq_s = sbin(st_dq, "dq_s", [128, 2 * DQW], F16)
        dq_w = sbin(st_dq, "dq_w", [128, 2 * DQW], BF16)

        dq_tiles = []
        for colt in range(c.NCGU):
            col = colt * DQW
            w = min(DQW, c.I - col)
            for mat in ("g", "u"):
                for hc in range(c.HC):
                    dq_tiles.append((mat, hc, col, w))
        for colt in range(c.NCD):
            for jc in range(c.JC):
                dq_tiles.append(("d", jc, colt * DQW, DQW))
        NDQ = len(dq_tiles)
        NGU_T = 2 * c.HC * c.NCGU
        dq_vd = {}
        dq_spd = {}
        dq_bl = {}
        dq_bcons = {}
        dq_next = [0]

        def dq_emit_one(i):
            mat, a, col, w = dq_tiles[i]
            cs = i % 2
            cw = slice(cs * DQW, cs * DQW + w)
            ww = slice(0, w)

            def _ld(g):
                if i >= 2:
                    g.wait_ge(vS, dq_vd[i - 2])
                src, srows = {"g": (gcT, gsT), "u": (ucT, usT),
                              "d": (dcT, dsT)}[mat]
                qgp.dma(g, f"dqc{i % 2}", dq_c[:, cw], src[a, :, col:col + w])
                qgp.dma(g, f"dqc{i % 2}", dq_s[0:64, cw],
                        srows[2 * a:2 * a + 1, col:col + w]
                        .partition_broadcast(64))
                qgp.dma(g, f"dqc{i % 2}", dq_s[64:128, cw],
                        srows[2 * a + 1:2 * a + 2, col:col + w]
                        .partition_broadcast(64))
            GP(_ld)

            def _tree(v):
                qgp.w(v, f"dqc{i % 2}")
                if i >= 2:
                    qgp.w(v, f"dqw{i % 2}")
                cc_ = dq_c[:, cw]
                ca_, cb_ = dq_ca[:, ww], dq_cb[:, ww]
                m_, x_, l7_, t_ = (dq_m[:, ww], dq_x[:, ww],
                                   dq_l7[:, ww], dq_t[:, ww])
                dve.c(v.tensor_scalar(ca_, cc_, -DQ_CA, None, ALU.add))
                dve.c(v.tensor_scalar(cb_, cc_, -DQ_CB, None, ALU.add))
                v.wait_ge(vS, dve.n)
                dve.c(v.tensor_scalar(l7_, ca_, NFG[7], NFDA[7],
                                      ALU.mult, ALU.add))
                dve.c(v.tensor_scalar(m_, ca_, NFG[0], NFDA[0],
                                      ALU.mult, ALU.add))
                v.wait_ge(vS, dve.n)
                for i2 in range(1, 7):
                    dve.c(v.tensor_scalar(t_, ca_, NFG[i2], NFDA[i2],
                                          ALU.mult, ALU.add))
                    v.wait_ge(vS, dve.n)
                    dve.c(v.tensor_tensor(m_, m_, t_, ALU.min))
                    v.wait_ge(vS, dve.n)
                dve.c(v.tensor_tensor(m_, m_, l7_, ALU.min))
                dve.c(v.tensor_scalar(x_, cb_, NFG[8], NFDB[8],
                                      ALU.mult, ALU.add))
                v.wait_ge(vS, dve.n)
                for i2 in range(9, 15):
                    dve.c(v.tensor_scalar(t_, cb_, NFG[i2], NFDB[i2],
                                          ALU.mult, ALU.add))
                    v.wait_ge(vS, dve.n)
                    dve.c(v.tensor_tensor(x_, x_, t_, ALU.max))
                    v.wait_ge(vS, dve.n)
                dve.c(v.tensor_scalar(t_, cb_, NFG[7], NFDB[7],
                                      ALU.mult, ALU.add))
                v.wait_ge(vS, dve.n)
                dve.c(v.tensor_tensor(x_, x_, t_, ALU.max))
                v.wait_ge(vS, dve.n)
                dve.c(v.tensor_tensor(m_, m_, x_, ALU.add))
                v.wait_ge(vS, dve.n)
                dve.c(v.tensor_tensor(m_, m_, l7_, ALU.subtract))
                v.wait_ge(vS, dve.n)
                dve.c(v.tensor_tensor(dq_w[:, cw], m_, dq_s[:, cw], ALU.mult))
                dq_vd[i] = dve.n
            VE(_tree)

            def _sp(g):
                g.wait_ge(vS, dq_vd[i])
                if mat in ("g", "u"):
                    dst = wg_deq if mat == "g" else wu_deq
                    tot = qgp.dma(g, f"dqw{i % 2}",
                                  dst[:, col // 128:(col + w) // 128, a, :],
                                  dq_w[:, cs * DQW:cs * DQW + w]
                                  .rearrange("p (j q) -> p j q", q=128))
                else:
                    tot = qgp.dma(g, f"dqw{i % 2}", wd_deq[a, :, col:col + w],
                                  dq_w[:, cw])
                dq_spd[i] = (f"dqw{i % 2}", tot)
            GP(_sp)

        def dq_emit_some(k):
            e = min(dq_next[0] + k, NDQ)
            for i in range(dq_next[0], e):
                dq_emit_one(i)
            dq_next[0] = e

        def dq_wait_spilled(eng, upto_i):
            tots = {}
            for i in range(upto_i + 1):
                s, t = dq_spd[i]
                tots[s] = max(tots.get(s, 0), t)
            for s, t in tots.items():
                eng.wait_ge(qgp.tot(s)[0], t)

        def _p0(g):
            qgp.dma(g, "const", ones22_s[:, :], ones22[:, :])
            qgp.dma(g, "const", onesc_s[:, :], onesc[:, :])
            qgp.dma(g, "const", onescf_s[:, :], onesf[:, :])
            qgp.dma(g, "const", masku_s[:, :], masku[:, :])
            qgp.dma(g, "const", bq_s[:, :], bq[:, :])
            qgp.dma(g, "const", alik_s[:, :], alik[:, :])
            qgp.dma(g, "const", aliq_sb[:, :], aliqm[:, :])
            qgp.dma(g, "const", bpc_s[:, :], bpc[:, :])
        GP(_p0)

        _sc1 = nc.enter_named_scope("p1a", False)
        st_p1a = contextlib.ExitStack()
        xsb = [sbin(st_p1a, f"xsb{i}", [128, c.HC * c.TSUP], BF16)
               for i in range(2)]
        wq_sb = sbin(st_p1a, "wq_sb", [128, c.HC * c.CCH * 128], BF16)
        qe_sb = sbin(st_p1a, "qe_sb", [128, 4 * c.TSUP], BF16)
        sqe_sb = sbin(st_p1a, "sqe_sb", [128, 2 * c.TSUP], BF16)
        xc_sb = [sbin(st_p1a, f"xc{i}", [128, c.TB], BF16) for i in range(2)]

        xcast_done = {}
        for hc in range(c.HC):
            def _xc(g, hc=hc):
                sl = hc % 2
                if hc >= 2:
                    qgp.w(g, f"xcb{sl}")
                qgp.dma(g, f"xca{sl}", xc_sb[sl][:, :], xT3[hc, :, :])
                qgp.w(g, f"xca{sl}")
                tot = qgp.dma(g, f"xcb{sl}", xbf_d[hc, :, :], xc_sb[sl][:, :])
                xcast_done[hc] = tot
            GP(_xc)

        x_ld = {}
        p1a_mm = {}
        p1a_ev = {}
        p1a_sq = {}
        p1a_q2mm = {}
        p1a_q2ev = {}
        nsq = [0]

        def p1a_wqload(ps):
            def _wq(g, ps=ps):
                if ps >= 1:
                    g.wait_ge(tS, p1a_mm[(ps * c.NTS - 1, c.CCH - 1)])
                nq = c.HC // 4
                for qtr in range(4):
                    qgp.dma(g, "wqp", wq_sb[:, qtr * nq * c.CCH * 128:
                                            (qtr + 1) * nq * c.CCH * 128]
                            .rearrange("p (h q) -> p h q", q=c.CCH * 128),
                            wq3[qtr * nq:(qtr + 1) * nq, :,
                                ps * c.CCH * 128:(ps + 1) * c.CCH * 128]
                            .rearrange("h p q -> p h q"))
            GP(_wq)

        def p1a_load(ts2):
            def _l(g, ts2=ts2):
                if ts2 == 0:
                    qgp.w(g, "xcb0")
                    qgp.w(g, "xcb1")
                if ts2 >= 2:
                    g.wait_ge(tS, p1a_mm[(ts2 - 2, c.CCH - 1)])
                nq4 = c.HC // 4
                ts = ts2 % c.NTS
                for q4 in range(4):
                    qgp.dma(g, f"xs{ts2 % 2}",
                            xsb[ts2 % 2][:, q4 * nq4 * c.TSUP:
                                         (q4 + 1) * nq4 * c.TSUP]
                            .rearrange("p (h t) -> p h t", t=c.TSUP),
                            xbf_d[q4 * nq4:(q4 + 1) * nq4, :,
                                  ts * c.TSUP:(ts + 1) * c.TSUP]
                            .rearrange("h p t -> p h t"))
            GP(_l)

        p1a_wqload(0)
        p1a_load(0)
        p1a_load(1)
        for ts2 in range(c.NPASS * c.NTS):
            ps = ts2 // c.NTS
            ts = ts2 % c.NTS
            for ccl in range(c.CCH):
                cc = ps * c.CCH + ccl
                idx = ts2 * c.CCH + ccl
                slot = idx % 4
                pp = (psA, psB)[slot // 2]
                col = (slot % 2) * 512

                def _mm(t, ts2=ts2, ccl=ccl, cc=cc, idx=idx, pp=pp, col=col):
                    if ccl == 0:
                        qgp.w(t, f"xs{ts2 % 2}")
                        if ts2 == 0:
                            qgp.w(t, "const")
                        if ts2 % c.NTS == 0:
                            qgp.w(t, "wqp")
                    if idx >= 4:
                        t.wait_ge(aS, p1a_ev[idx - 4])
                    for hc in range(c.HC):
                        MM(t, pp[:, col:col + c.TSUP],
                           wq_sb[:, (hc * c.CCH + ccl) * 128:
                                 (hc * c.CCH + ccl + 1) * 128],
                           xsb[ts2 % 2][:, hc * c.TSUP:(hc + 1) * c.TSUP],
                           start=(hc == 0), stop=(hc == c.HC - 1))
                    pemark()
                    p1a_mm[(ts2, ccl)] = pe.n
                TE(_mm)

                def _ev(s, ts2=ts2, ts=ts, cc=cc, idx=idx, pp=pp, col=col):
                    es = idx % 4
                    s.wait_ge(tS, p1a_mm[(ts2, cc % c.CCH)])
                    if idx >= 4:
                        qsc.w(s, f"sp{idx % 4}")
                    act.c(s.activation(qe_sb[:, es * c.TSUP:(es + 1) * c.TSUP],
                                       pp[:, col:col + c.TSUP],
                                       AF.Identity, bias=bq_s[:, cc:cc + 1]))
                    p1a_ev[idx] = act.n
                    s.wait_ge(aS, p1a_ev[idx])
                    qsc.dma(s, f"sp{idx % 4}",
                            qkv_sp[cc * 128:(cc + 1) * 128,
                                   ts * c.TSUP:(ts + 1) * c.TSUP],
                            qe_sb[:, es * c.TSUP:(es + 1) * c.TSUP])
                SC(_ev)

                if cc % 3 < 2:
                    si = nsq[0]
                    nsq[0] += 1

                    def _sq(s, idx=idx, si=si):
                        ss = si % 2
                        s.wait_ge(aS, p1a_ev[idx])
                        if si >= 2:
                            s.wait_ge(tS, p1a_q2mm[si - 2])
                        act.c(s.activation(
                            sqe_sb[:, ss * c.TSUP:(ss + 1) * c.TSUP],
                            qe_sb[:, (idx % 4) * c.TSUP:
                                  (idx % 4 + 1) * c.TSUP], AF.Square))
                        p1a_sq[si] = act.n
                    SC(_sq)

                    def _q2(t, si=si):
                        ss = si % 2
                        t.wait_ge(aS, p1a_sq[si])
                        if si >= 2:
                            t.wait_ge(vS, p1a_q2ev[si - 2])
                        MM(t, psC[0:1, ss * 512:ss * 512 + c.TSUP],
                           onesc_s[:, :],
                           sqe_sb[:, ss * c.TSUP:(ss + 1) * c.TSUP],
                           start=True, stop=True)
                        pemark()
                        p1a_q2mm[si] = pe.n
                    TE(_q2)

                    def _q2e(v, si=si):
                        ss = si % 2
                        v.wait_ge(tS, p1a_q2mm[si])
                        if si >= 2:
                            qgp.w(v, "q2st")
                        dve.c(v.tensor_copy(
                            q2st[0:1, ss * c.TSUP:(ss + 1) * c.TSUP],
                            psC[0:1, ss * 512:ss * 512 + c.TSUP]))
                        p1a_q2ev[si] = dve.n
                    VE(_q2e)

                    def _q2w(g, ts=ts, cc=cc, si=si):
                        ss = si % 2
                        g.wait_ge(vS, p1a_q2ev[si])
                        g3 = cc % 3
                        hh = cc // 3
                        row = hh if g3 == 0 else c.NH + hh
                        scol = ts * c.TSUP
                        qgp.dma(g, "q2st", q2k2_d[row:row + 1,
                                                  scol:scol + c.TSUP],
                                q2st[0:1, ss * c.TSUP:(ss + 1) * c.TSUP])
                    GP(_q2w)

            if ts2 + 2 < c.NPASS * c.NTS:
                p1a_load(ts2 + 2)
            if ts2 % c.NTS == c.NTS - 1 and ts2 // c.NTS + 1 < c.NPASS:
                p1a_wqload(ts2 // c.NTS + 1)
            dq_emit_some(2)

        if os.environ.get("MK_UPTO") == "p1a":
            def _fin0(sy):
                qsc.w(sy, "sp0")
                qsc.w(sy, "sp1")
                qsc.w(sy, "sp2")
                qsc.w(sy, "sp3")
            SY(_fin0)
            st_p1a.close()
            st_dq.close()
            return nc

        nc.leave_named_scope("p1a", _sc1[0], False)
        _sc2 = nc.enter_named_scope("rows", False)
        def _rows_gp(g):
            g.wait_ge(vS, p1a_q2ev[nsq[0] - 1])
            qgp.w(g, "q2st")
            qgp.dma(g, "rows", rwf[:, :], q2k2_d[c.NH:2 * c.NH, :])
            rt["rows_k2"] = True
        GP(_rows_gp)

        def _rows_v(v):
            qgp.w(v, "rows")
            dve.c(v.tensor_reduce(rwm[:, :], rwf[:, :], AX, ALU.max))
            rt["rows_max"] = dve.n
        VE(_rows_v)

        def _rows_g2(g):
            g.wait_ge(vS, rt["rows_max"])
            qgp.dma(g, "rows", rwf[:, 0:c.TPC], q2k2_d[0:c.NH, 0:c.TPC])
        GP(_rows_g2)

        def _rows_v1(v):
            qgp.w(v, "rows")
            dve.c(v.tensor_scalar(rwf[:, 0:c.TPC], rwf[:, 0:c.TPC],
                                  rwm[:, 0:1], None, ALU.mult))
            v.wait_ge(vS, dve.n)
            dve.c(v.tensor_scalar(rwf[:, 0:c.TPC], rwf[:, 0:c.TPC], 1e-12,
                                  None, ALU.max))
            rt["rows_mul"] = dve.n
        VE(_rows_v1)

        def _rows_s(s):
            s.wait_ge(vS, rt["rows_mul"])
            act.c(s.activation(rwf[:, 0:c.TPC], rwf[:, 0:c.TPC], AF.Sqrt))
            rt["rows_sqrt"] = act.n
        SC(_rows_s)

        def _rows_v2(v):
            v.wait_ge(aS, rt["rows_sqrt"])
            dve.c(v.scalar_tensor_tensor(rwf[:, 0:c.TPC], rwf[:, 0:c.TPC],
                                         -1.0, aliq_sb[:, :], ALU.mult,
                                         ALU.add))
            v.wait_ge(vS, dve.n)
            dve.c(v.tensor_copy(rwfh[:, :], rwf[:, 0:c.TPC]))
            v.wait_ge(vS, dve.n)
            dve.c(v.tensor_tensor(rwfl[:, :], rwf[:, 0:c.TPC], rwfh[:, :],
                                  ALU.subtract))
            rt["rows_done"] = dve.n
        VE(_rows_v2)

        if os.environ.get("MK_UPTO") == "rows":
            def _fin0(sy):
                sy.wait_ge(vS, rt["rows_done"])
            SY(_fin0)
            st_p1a.close()
            st_dq.close()
            return nc

        nc.leave_named_scope("rows", _sc2[0], False)
        _sc3 = nc.enter_named_scope("p1b", False)
        st_p1a.close()
        st_p1b = contextlib.ExitStack()
        qt_sb = [sbin(st_p1b, f"qt{i}", [128, c.TPC], BF16) for i in range(2)]
        kt_sb = [sbin(st_p1b, f"kt{i}", [128, c.TB], BF16) for i in range(2)]
        vn_sb = [sbin(st_p1b, f"vn{i}", [128, c.TB], BF16) for i in range(2)]
        pt_sb = sbin(st_p1b, "pt_sb", [128, c.KC * c.TPC], BF16)
        frow_s = sbin(st_p1b, "frow_s", [2, c.TPC], BF16)
        rrow_s = sbin(st_p1b, "rrow_s", [1, 2 * c.TPC], F32)
        rrbf_s = sbin(st_p1b, "rrbf_s", [1, 2 * c.TPC], BF16)
        ot_f = sbin(st_p1b, "ot_f", [128, c.TPC], F32)
        ot_b = sbin(st_p1b, "ot_b", [128, c.TPC], BF16)

        p1b_smm = {}
        p1b_mask = {}
        p1b_exp = {}
        p1b_av = {}
        p1b_otf = {}
        p1b_otb = {}
        p1b_rr = {}
        p1b_done_pe = {}

        def qstart(ci):
            return 128 * ci if ci < c.KH else 0

        for hh in range(c.NH):
            idx = hh
            buf = idx % 2

            def _ld(sy, idx=idx, hh=hh, buf=buf):
                if idx == 0:
                    for kq in range(4):
                        qsc.w(sy, f"sp{kq}")
                    qgp.w(sy, "rows")
                    qgp.w(sy, "xcb0")
                    qgp.w(sy, "xcb1")
                    qgp.w(sy, "q2st")
                if idx >= 2:
                    sy.wait_ge(tS, p1b_done_pe[idx - 2])
                r0 = (3 * hh) * 128
                qsy.dma(sy, f"qk{buf}", qt_sb[buf][:, :],
                        qkv_sp[r0:r0 + 128, 0:c.TPC])
                qsy.dma(sy, f"qk{buf}", kt_sb[buf][:, :],
                        qkv_sp[r0 + 128:r0 + 256, :])
                for kc in range(c.KC):
                    qsy.dmat(sy, f"qk{buf}",
                             vn_sb[buf][:, kc * 128:(kc + 1) * 128],
                             qkv_sp[r0 + 256:r0 + 384,
                                    kc * 128:(kc + 1) * 128])
            SY(_ld)

            def _frow(g, idx=idx, hh=hh):
                if idx == 0:
                    g.wait_ge(vS, rt["rows_done"])
                if idx >= 1:
                    qgp.w(g, "frow")
                qgp.dma(g, "frow", frow_d[0:1, :], rwfh[hh:hh + 1, :])
                qgp.dma(g, "frow", frow_d[1:2, :], rwfl[hh:hh + 1, :])
                qgp.w(g, "frow")
                if idx >= 1:
                    g.wait_ge(tS, p1b_done_pe[idx - 1])
                qgp.dma(g, "frow", frow_s[:, :], frow_d[:, :])
            GP(_frow)

            for kci in range(c.KC):
                qs = qstart(kci)
                wdt = c.TPC - qs

                def _smm1(t, idx=idx, hh=hh, buf=buf, kci=kci, qs=qs,
                          wdt=wdt):
                    if kci == 0:
                        qsy.w(t, f"qk{buf}")
                        qgp.w(t, "frow")
                        if idx >= 1:
                            t.wait_ge(aS, p1b_otf[idx - 1])
                            t.wait_ge(vS, p1b_rr[idx - 1])
                    sp = (psA, psB)[kci % 2]
                    if kci >= 2:
                        t.wait_ge(aS, p1b_exp[(idx, kci - 2)])
                    for p0, pw in bpieces(qs, c.TPC):
                        cs = slice(p0, p0 + pw)
                        MM(t, sp[:, cs],
                           kt_sb[buf][:, kci * 128:(kci + 1) * 128],
                           qt_sb[buf][:, p0:p0 + pw],
                           start=True, stop=False)
                        MM(t, sp[:, cs], ones22_s[:, :],
                           frow_s[:, p0:p0 + pw],
                           start=False, stop=True)
                    pemark()
                    p1b_smm[(idx, kci)] = pe.n
                TE(_smm1)

                if kci < c.KH:
                    def _mask1(v, idx=idx, kci=kci, qs=qs):
                        sp = (psA, psB)[kci % 2]
                        v.wait_ge(tS, p1b_smm[(idx, kci)])
                        dve.c(v.tensor_tensor(sp[:, qs:qs + 128],
                                              sp[:, qs:qs + 128],
                                              masku_s[:, :], ALU.add))
                        p1b_mask[(idx, kci)] = dve.n
                    VE(_mask1)

                def _exp1(s, idx=idx, hh=hh, kci=kci, qs=qs):
                    sp = (psA, psB)[kci % 2]
                    s.wait_ge(tS, p1b_smm[(idx, kci)])
                    if (idx, kci) in p1b_mask:
                        s.wait_ge(vS, p1b_mask[(idx, kci)])
                    if idx >= 1 and kci < 2:
                        s.wait_ge(tS, p1b_av[idx - 1])
                    act.c(s.activation(
                        pt_sb[:, kci * c.TPC + qs:(kci + 1) * c.TPC],
                        sp[:, qs:c.TPC], AF.Exp,
                        bias=alik_s[:, hh * c.KC + kci:hh * c.KC + kci + 1],
                        scale=c.SCALE))
                    p1b_exp[(idx, kci)] = act.n
                SC(_exp1)

            def _rsav(t, idx=idx, buf=buf):
                for kci in range(c.KC):
                    qs = qstart(kci)
                    t.wait_ge(aS, p1b_exp[(idx, kci)])
                    first = kci == 0
                    last = kci == c.KC - 1
                    for p0, pw in bpieces(qs, c.TPC):
                        pts = pt_sb[:, kci * c.TPC + p0:
                                    kci * c.TPC + p0 + pw]
                        cs = slice(p0, p0 + pw)
                        MM(t, psD[0:1, cs], onesc_s[:, :], pts,
                           start=first, stop=last)
                        MM(t, psC[:, cs],
                           vn_sb[buf][:, kci * 128:(kci + 1) * 128],
                           pts, start=first, stop=last)
                pemark()
                p1b_av[idx] = pe.n
            TE(_rsav)

            def _rr(v, idx=idx):
                v.wait_ge(tS, p1b_av[idx])
                if idx >= 1:
                    qsc.w(v, "otw")
                dve.c(v.reciprocal(rrow_s[0:1, 0:c.TPC], psD[0:1, 0:c.TPC]))
                v.wait_ge(vS, dve.n)
                dve.c(v.tensor_copy(rrbf_s[0:1, 0:c.TPC],
                                    rrow_s[0:1, 0:c.TPC]))
                v.wait_ge(vS, dve.n)
                dve.c(v.tensor_tensor(rrow_s[0:1, c.TPC:2 * c.TPC],
                                      rrow_s[0:1, 0:c.TPC],
                                      rrbf_s[0:1, 0:c.TPC], ALU.subtract))
                v.wait_ge(vS, dve.n)
                dve.c(v.tensor_copy(rrbf_s[0:1, c.TPC:2 * c.TPC],
                                    rrow_s[0:1, c.TPC:2 * c.TPC]))
                p1b_rr[idx] = dve.n
            VE(_rr)

            def _otf(s, idx=idx):
                s.wait_ge(tS, p1b_av[idx])
                if idx >= 1:
                    s.wait_ge(vS, p1b_otb[idx - 1])
                act.c(s.copy(ot_f[:, :], psC[:, 0:c.TPC]))
                p1b_otf[idx] = act.n
            SC(_otf)

            def _otb(v, idx=idx):
                v.wait_ge(aS, p1b_otf[idx])
                if idx >= 1:
                    qsc.w(v, "otw")
                dve.c(v.tensor_copy(ot_b[:, :], ot_f[:, :]))
                p1b_otb[idx] = dve.n
            VE(_otb)

            def _otw(s, idx=idx, hh=hh):
                s.wait_ge(vS, p1b_otb[idx])
                qsc.dma(s, "otw", ot_d[hh, :, :], ot_b[:, :])
                s.wait_ge(vS, p1b_rr[idx])
                qsc.dma(s, "otw", rrl_d[2 * hh:2 * hh + 1, :],
                        rrbf_s[0:1, 0:c.TPC])
                qsc.dma(s, "otw", rrl_d[2 * hh + 1:2 * hh + 2, :],
                        rrbf_s[0:1, c.TPC:2 * c.TPC])
            SC(_otw)

            p1b_done_pe[idx] = pe.n
            dq_emit_some(2)

        if os.environ.get("MK_UPTO") == "p1b":
            def _fin0(sy):
                qsc.w(sy, "otw")
            SY(_fin0)
            st_p1b.close()
            st_dq.close()
            return nc

        nc.leave_named_scope("p1b", _sc3[0], False)
        _sc4 = nc.enter_named_scope("p2", False)
        st_p1b.close()
        st_p2 = contextlib.ExitStack()
        otm_sb = sbin(st_p2, "otm_sb", [128, c.NH * c.TPC], BF16)
        rr2_sb = sbin(st_p2, "rr2_sb", [2, 2 * c.TPC], BF16)
        wpl_sb = [sbin(st_p2, f"wpl{i}", [128, c.HC * 128], BF16)
                  for i in range(2)]
        xrt_sb = [sbin(st_p2, f"xrt{i}", [128, c.TPC], F32) for i in range(2)]
        rch_sb = sbin(st_p2, "rch_sb", [128, 2 * c.TPC], F32)
        sqp_sb = sbin(st_p2, "sqp_sb", [128, c.TPC], F32)
        ab_sb = sbin(st_p2, "ab_sb", [2, 2 * c.TPC], BF16)
        xlh_sb = sbin(st_p2, "xlh_sb", [128, 2 * c.TPC], BF16)
        xll_sb = sbin(st_p2, "xll_sb", [128, 2 * c.TPC], BF16)

        p2_norm_mm = {}
        p2_norm = {}
        p2_mm = {}
        p2_ev = {}
        p2_row = {}

        def _p2_ld(g):
            qsc.w(g, "otw")
            g.wait_ge(tS, p1b_done_pe[c.NH - 1])
            qsy.w(g, "qk0")
            qsy.w(g, "qk1")
            qgp.w(g, "frow")
            nq4 = c.NH // 4
            for q4 in range(4):
                qgp.dma(g, "p2ld",
                        otm_sb[:, q4 * nq4 * c.TPC:(q4 + 1) * nq4 * c.TPC]
                        .rearrange("p (h t) -> p h t", t=c.TPC),
                        ot_d[q4 * nq4:(q4 + 1) * nq4, :, :]
                        .rearrange("h p t -> p h t"))
        GP(_p2_ld)

        for hc in range(c.NH):
            def _nld(g, hc=hc):
                if hc == 0:
                    qgp.w(g, "p2ld")
                if hc >= 2:
                    g.wait_ge(tS, p2_norm_mm[hc - 2])
                qgp.dma(g, f"rr{hc % 2}",
                        rr2_sb[:, (hc % 2) * c.TPC:(hc % 2 + 1) * c.TPC],
                        rrl_d[2 * hc:2 * hc + 2, :])
            GP(_nld)

            def _nmm(t, hc=hc):
                qgp.w(t, f"rr{hc % 2}")
                pp = (psA, psB)[hc % 2]
                if hc >= 2:
                    t.wait_ge(vS, p2_norm[hc - 2])
                for h0, hw in chunks(c.TPC):
                    MM(t, pp[:, h0:h0 + hw], ones22_s[:, :],
                       rr2_sb[:, (hc % 2) * c.TPC + h0:
                              (hc % 2) * c.TPC + h0 + hw],
                       start=True, stop=True)
                pemark()
                p2_norm_mm[hc] = pe.n
            TE(_nmm)

            def _nmul(v, hc=hc):
                v.wait_ge(tS, p2_norm_mm[hc])
                pp = (psA, psB)[hc % 2]
                dve.c(v.tensor_tensor(otm_sb[:, hc * c.TPC:(hc + 1) * c.TPC],
                                      otm_sb[:, hc * c.TPC:(hc + 1) * c.TPC],
                                      pp[:, 0:c.TPC], ALU.mult))
                p2_norm[hc] = dve.n
            VE(_nmul)
            dq_emit_some(1)

        def p2_wld(oc):
            def _w(g, oc=oc):
                if oc >= 2:
                    g.wait_ge(tS, p2_mm[oc - 2])
                qgp.dma(g, f"wp{oc % 2}", wpl_sb[oc % 2][:, :]
                        .rearrange("p (h q) -> p h q", q=128),
                        wp4[oc, :, :, :].rearrange("h p q -> p h q"))
            GP(_w)

        def p2_xld(oc):
            def _x(sy, oc=oc):
                if oc < 2:
                    sy.wait_ge(tS, p1b_done_pe[c.NH - 1])
                    sy.wait_ge(aS, p1b_otf[c.NH - 1])
                    sy.wait_ge(vS, p1b_rr[c.NH - 1])
                    sy.wait_ge(vS, p1b_otb[c.NH - 1])
                    qsc.w(sy, "otw")
                if oc >= 2:
                    sy.wait_ge(vS, p2_ev[(oc - 2, "res")])
                qsy.dma(sy, f"xr{oc % 2}", xrt_sb[oc % 2][:, :],
                        xT3[oc, :, 0:c.TPC])
            SY(_x)

        p2_wld(0)
        p2_wld(1)
        p2_xld(0)
        p2_xld(1)
        for oc in range(c.HC):
            def _pmm(t, oc=oc):
                pp = (psA, psB)[oc % 2]
                qgp.w(t, f"wp{oc % 2}")
                if oc == 0:
                    t.wait_ge(vS, p2_norm[c.NH - 1])
                if oc >= 2:
                    t.wait_ge(aS, p2_ev[(oc - 2, "copy")])
                for hc in range(c.NH):
                    for h0, hw in chunks(c.TPC):
                        MM(t, pp[:, h0:h0 + hw],
                           wpl_sb[oc % 2][:, hc * 128:(hc + 1) * 128],
                           otm_sb[:, hc * c.TPC + h0:hc * c.TPC + h0 + hw],
                           start=(hc == 0), stop=(hc == c.NH - 1))
                pemark()
                p2_mm[oc] = pe.n
            TE(_pmm)
            if oc + 2 < c.HC:
                p2_wld(oc + 2)

            def _pev(s, oc=oc):
                pp = (psA, psB)[oc % 2]
                es = oc % 2
                s.wait_ge(tS, p2_mm[oc])
                if oc >= 2:
                    qsc.w(s, f"prest{oc % 2}")
                act.c(s.activation(rch_sb[:, es * c.TPC:(es + 1) * c.TPC],
                                   pp[:, 0:c.TPC], AF.Identity,
                                   bias=bpc_s[:, oc:oc + 1]))
                p2_ev[(oc, "copy")] = act.n
            SC(_pev)

            def _pres(v, oc=oc):
                es = oc % 2
                rr = rch_sb[:, es * c.TPC:(es + 1) * c.TPC]
                v.wait_ge(aS, p2_ev[(oc, "copy")])
                qsy.w(v, f"xr{oc % 2}")
                dve.c(v.tensor_tensor(rr, rr, xrt_sb[oc % 2][:, :], ALU.add))
                p2_ev[(oc, "res")] = dve.n
            VE(_pres)
            if oc + 2 < c.HC:
                p2_xld(oc + 2)

            def _psq(s, oc=oc):
                es = oc % 2
                s.wait_ge(vS, p2_ev[(oc, "res")])
                act.c(s.activation(sqp_sb[:, :],
                                   rch_sb[:, es * c.TPC:(es + 1) * c.TPC],
                                   AF.Square))
                p2_ev[(oc, "sq")] = act.n
            SC(_psq)

            def _prow(t, oc=oc):
                es = oc % 2
                t.wait_ge(vS, p2_ev[(oc, "res")])
                for h0, hw in chunks(c.TPC):
                    MM(t, psC[0:1, h0:h0 + hw], onescf_s[:, :],
                       rch_sb[:, es * c.TPC + h0:es * c.TPC + h0 + hw],
                       start=(oc == 0), stop=(oc == c.HC - 1))
                t.wait_ge(aS, p2_ev[(oc, "sq")])
                for h0, hw in chunks(c.TPC):
                    MM(t, psD[0:1, h0:h0 + hw], onescf_s[:, :],
                       sqp_sb[:, h0:h0 + hw],
                       start=(oc == 0), stop=(oc == c.HC - 1))
                pemark()
                p2_row[oc] = pe.n
            TE(_prow)

            def _pst(s, oc=oc):
                es = oc % 2
                s.wait_ge(tS, p2_row[oc])
                qsc.dma(s, f"prest{oc % 2}", pres_d[oc, :, :],
                        rch_sb[:, es * c.TPC:(es + 1) * c.TPC])
            SC(_pst)

            dq_emit_some(1)

        lnr_s = sbin(st_p2, "lnr_s", [1, 8 * c.TPC], F32)

        def _l1r(v):
            v.wait_ge(tS, p2_row[c.HC - 1])
            dve.c(v.tensor_scalar(lnr_s[0:1, 0:c.TPC], psC[0:1, 0:c.TPC],
                                  1.0 / c.DM, None, ALU.mult))
            dve.c(v.tensor_scalar(lnr_s[0:1, c.TPC:2 * c.TPC],
                                  psD[0:1, 0:c.TPC], 1.0 / c.DM, None,
                                  ALU.mult))
            v.wait_ge(vS, dve.n)
            dve.c(v.tensor_tensor(lnr_s[0:1, 2 * c.TPC:3 * c.TPC],
                                  lnr_s[0:1, 0:c.TPC], lnr_s[0:1, 0:c.TPC],
                                  ALU.mult))
            v.wait_ge(vS, dve.n)
            dve.c(v.tensor_tensor(lnr_s[0:1, c.TPC:2 * c.TPC],
                                  lnr_s[0:1, c.TPC:2 * c.TPC],
                                  lnr_s[0:1, 2 * c.TPC:3 * c.TPC],
                                  ALU.subtract))
            v.wait_ge(vS, dve.n)
            dve.c(v.tensor_scalar(lnr_s[0:1, c.TPC:2 * c.TPC],
                                  lnr_s[0:1, c.TPC:2 * c.TPC], 1e-5, None,
                                  ALU.add))
            rt["l1var"] = dve.n
        VE(_l1r)

        def _l1s(s):
            s.wait_ge(vS, rt["l1var"])
            act.c(s.activation(lnr_s[0:1, 3 * c.TPC:4 * c.TPC],
                               lnr_s[0:1, c.TPC:2 * c.TPC], AF.Sqrt))
            rt["l1std"] = act.n
        SC(_l1s)

        def _l1v2(v):
            v.wait_ge(aS, rt["l1std"])
            dve.c(v.reciprocal(lnr_s[0:1, 4 * c.TPC:5 * c.TPC],
                               lnr_s[0:1, 3 * c.TPC:4 * c.TPC]))
            v.wait_ge(vS, dve.n)
            dve.c(v.tensor_tensor(lnr_s[0:1, 5 * c.TPC:6 * c.TPC],
                                  lnr_s[0:1, 0:c.TPC],
                                  lnr_s[0:1, 4 * c.TPC:5 * c.TPC], ALU.mult))
            v.wait_ge(vS, dve.n)
            dve.c(v.tensor_scalar(lnr_s[0:1, 5 * c.TPC:6 * c.TPC],
                                  lnr_s[0:1, 5 * c.TPC:6 * c.TPC], -1.0,
                                  None, ALU.mult))
            rt["l1ab"] = dve.n
        VE(_l1v2)

        def _l1b1(g):
            g.wait_ge(vS, rt["l1ab"])
            qgp.dma(g, "l1b", abrow_d[0:1, :], lnr_s[0:1, 4 * c.TPC:5 * c.TPC])
            qgp.dma(g, "l1b", abrow_d[2:3, :], lnr_s[0:1, 5 * c.TPC:6 * c.TPC])
            qgp.w(g, "l1b")
            qgp.dma(g, "l1b", ab_sb[0:1, 0:c.TPC], abrow_d[0:1, :])
            qgp.dma(g, "l1b", ab_sb[0:1, c.TPC:2 * c.TPC], abrow_d[2:3, :])
        GP(_l1b1)

        def _l1lo(v):
            qgp.w(v, "l1b")
            dve.c(v.tensor_tensor(lnr_s[0:1, 6 * c.TPC:7 * c.TPC],
                                  lnr_s[0:1, 4 * c.TPC:5 * c.TPC],
                                  ab_sb[0:1, 0:c.TPC], ALU.subtract))
            dve.c(v.tensor_tensor(lnr_s[0:1, 7 * c.TPC:8 * c.TPC],
                                  lnr_s[0:1, 5 * c.TPC:6 * c.TPC],
                                  ab_sb[0:1, c.TPC:2 * c.TPC], ALU.subtract))
            rt["l1lo"] = dve.n
        VE(_l1lo)

        def _l1b2(g):
            g.wait_ge(vS, rt["l1lo"])
            qgp.dma(g, "l1b", abrow_d[1:2, :], lnr_s[0:1, 6 * c.TPC:7 * c.TPC])
            qgp.dma(g, "l1b", abrow_d[3:4, :], lnr_s[0:1, 7 * c.TPC:8 * c.TPC])
            qgp.w(g, "l1b")
            qgp.dma(g, "l1b", ab_sb[:, 0:c.TPC], abrow_d[0:2, :])
            qgp.dma(g, "l1b", ab_sb[:, c.TPC:2 * c.TPC], abrow_d[2:4, :])
        GP(_l1b2)

        def _l1bc(t):
            qgp.w(t, "l1b")
            for h0, hw in chunks(c.TPC):
                MM(t, psA[:, h0:h0 + hw], ones22_s[:, :],
                   ab_sb[:, h0:h0 + hw], start=True, stop=True)
                MM(t, psB[:, h0:h0 + hw], ones22_s[:, :],
                   ab_sb[:, c.TPC + h0:c.TPC + h0 + hw],
                   start=True, stop=True)
            pemark()
            rt["l1bc"] = pe.n
        TE(_l1bc)

        p2b = {}
        for oc in range(c.HC):
            def _n2ld(sy, oc=oc):
                if oc < 2:
                    qsc.w(sy, f"prest{oc % 2}")
                else:
                    sy.wait_ge(vS, p2b[(oc - 2, "lo")])
                qsy.dma(sy, f"prl{oc % 2}",
                        rch_sb[:, (oc % 2) * c.TPC:(oc % 2 + 1) * c.TPC],
                        pres_d[oc, :, :])
            SY(_n2ld)

            def _n2v(v, oc=oc):
                es = oc % 2
                rr = rch_sb[:, es * c.TPC:(es + 1) * c.TPC]
                qsy.w(v, f"prl{oc % 2}")
                if oc == 0:
                    v.wait_ge(tS, rt["l1bc"])
                if oc >= 2:
                    qsc.w(v, f"xst{oc % 2}")
                dve.c(v.tensor_tensor(rr, rr, psA[:, 0:c.TPC], ALU.mult))
                v.wait_ge(vS, dve.n)
                dve.c(v.tensor_tensor(rr, rr, psB[:, 0:c.TPC], ALU.add))
                v.wait_ge(vS, dve.n)
                xh = xlh_sb[:, es * c.TPC:(es + 1) * c.TPC]
                xl = xll_sb[:, es * c.TPC:(es + 1) * c.TPC]
                dve.c(v.tensor_copy(xh, rr))
                v.wait_ge(vS, dve.n)
                p2b[(oc, "hi")] = dve.n
                dve.c(v.tensor_tensor(xl, rr, xh, ALU.subtract))
                p2b[(oc, "lo")] = dve.n
            VE(_n2v)

            def _n2st(s, oc=oc):
                es = oc % 2
                xh = xlh_sb[:, es * c.TPC:(es + 1) * c.TPC]
                xl = xll_sb[:, es * c.TPC:(es + 1) * c.TPC]
                s.wait_ge(vS, p2b[(oc, "lo")])
                qsc.dma(s, f"xst{oc % 2}", xlnh_d[oc * 128:(oc + 1) * 128, :],
                        xh)
                qsc.dma(s, f"xst{oc % 2}", xlnl_d[oc * 128:(oc + 1) * 128, :],
                        xl)
            SC(_n2st)
            dq_emit_some(1)

        if os.environ.get("MK_UPTO") == "p2":
            def _fin0(sy):
                qsc.w(sy, "xst0")
                qsc.w(sy, "xst1")
            SY(_fin0)
            st_p2.close()
            st_dq.close()
            return nc

        nc.leave_named_scope("p2", _sc4[0], False)
        _sc5 = nc.enter_named_scope("p3a", False)
        st_p2.close()
        st_p3a = contextlib.ExitStack()
        xf_sb = sbin(st_p3a, "xf_sb", [128, c.HC * c.TPC], BF16)
        wgl_sb = [sbin(st_p3a, f"wgl{i}", [128, c.HC * 128], BF16)
                  for i in range(2)]
        wul_sb = [sbin(st_p3a, f"wul{i}", [128, c.HC * 128], BF16)
                  for i in range(2)]
        gu_sb = sbin(st_p3a, "gu_sb", [128, c.TPC], BF16)
        int_sb = [sbin(st_p3a, f"int{i}", [128, c.TPC], BF16)
                  for i in range(2)]

        xlnv = xlnh_d.rearrange("(h p) t -> h p t", p=128)
        p3_gu = {}
        p3_int = {}
        pgS = sem("pgS")

        def _pgate(sy):
            qsc.w(sy, "xst0")
            qsc.w(sy, "xst1")
            qsc.w(sy, "prest0")
            qsc.w(sy, "prest1")
            qsy.w(sy, "prl0")
            qsy.w(sy, "prl1")
            qsy.w(sy, "xr0")
            qsy.w(sy, "xr1")
            qgp.w(sy, "l1b")
            qgp.w(sy, "p2ld")
            qgp.w(sy, "rr0")
            qgp.w(sy, "rr1")
            qgp.w(sy, "wp0")
            qgp.w(sy, "wp1")
            sy.wait_ge(vS, p2b[(c.HC - 1, "lo")])
            sy.wait_ge(aS, rt["l1std"])
            sy.wait_ge(tS, rt["l1bc"])
            sy.nop().then_inc(pgS, 1)
        SY(_pgate)

        def _p3x(sy):
            sy.wait_ge(pgS, 1)
            nq4 = c.HC // 4
            for q4 in range(4):
                qsy.dma(sy, "p3x", xf_sb[:, q4 * nq4 * c.TPC:
                                         (q4 + 1) * nq4 * c.TPC]
                        .rearrange("p (h t) -> p h t", t=c.TPC),
                        xlnv[q4 * nq4:(q4 + 1) * nq4, :, :]
                        .rearrange("h p t -> p h t"))
        SY(_p3x)

        def p3_guload(j):
            def _w(s, j=j):
                sl = j % 2
                if j < 2:
                    s.wait_ge(pgS, 1)
                if j >= 2:
                    s.wait_ge(tS, p3_gu[j - 2])
                dq_wait_spilled(s, (min(j // 8, c.NCGU - 1) + 1)
                                * 2 * c.HC - 1)
                nq4 = c.HC // 4
                for q4 in range(4):
                    qsc.dma(s, f"wgu{sl}", wgl_sb[sl][:, q4 * nq4 * 128:
                                                      (q4 + 1) * nq4 * 128]
                            .rearrange("p (h q) -> p h q", q=128),
                            wg_deq[:, j, q4 * nq4:(q4 + 1) * nq4, :])
                    qsc.dma(s, f"wgu{sl}", wul_sb[sl][:, q4 * nq4 * 128:
                                                      (q4 + 1) * nq4 * 128]
                            .rearrange("p (h q) -> p h q", q=128),
                            wu_deq[:, j, q4 * nq4:(q4 + 1) * nq4, :])
            SC(_w)

        p3_guload(0)
        p3_guload(1)
        for j in range(c.JC):
            def _gu(t, j=j):
                sl = j % 2
                psG = (psA, psB)[j % 2]
                psU = (psC, psD)[j % 2]
                qsc.w(t, f"wgu{sl}")
                if j == 0:
                    qsy.w(t, "p3x")
                    t.wait_ge(tS, rt["l1bc"])
                    t.wait_ge(vS, p2b[(c.HC - 1, "lo")])
                if j >= 2:
                    t.wait_ge(vS, p3_int[j - 2])
                for hc in range(c.HC):
                    lg = wgl_sb[sl][:, hc * 128:(hc + 1) * 128]
                    lu = wul_sb[sl][:, hc * 128:(hc + 1) * 128]
                    for h0, hw in chunks(c.TPC):
                        xr_ = xf_sb[:, hc * c.TPC + h0:hc * c.TPC + h0 + hw]
                        MM(t, psG[:, h0:h0 + hw], lg, xr_,
                           start=(hc == 0), stop=(hc == c.HC - 1))
                        MM(t, psU[:, h0:h0 + hw], lu, xr_,
                           start=(hc == 0), stop=(hc == c.HC - 1))
                pemark()
                p3_gu[j] = pe.n
            TE(_gu)
            if j + 2 < c.JC:
                p3_guload(j + 2)

            def _silu(s, j=j):
                psG = (psA, psB)[j % 2]
                s.wait_ge(tS, p3_gu[j])
                if j >= 1:
                    s.wait_ge(vS, p3_int[j - 1])
                act.c(s.activation(gu_sb[:, :], psG[:, 0:c.TPC], AF.Silu))
                p3_gu[(j, "silu")] = act.n
            SC(_silu)

            def _mul(v, j=j):
                psU = (psC, psD)[j % 2]
                v.wait_ge(aS, p3_gu[(j, "silu")])
                if j >= 2:
                    qsy.w(v, f"ist{j % 2}")
                dve.c(v.tensor_tensor(int_sb[j % 2][:, :], gu_sb[:, :],
                                      psU[:, 0:c.TPC], ALU.mult))
                p3_int[j] = dve.n
            VE(_mul)

            def _ist(sy, j=j):
                sy.wait_ge(vS, p3_int[j])
                qsy.dma(sy, f"ist{j % 2}", inter_d[j, :, :],
                        int_sb[j % 2][:, :])
            SY(_ist)
            dq_emit_some(12)

        if os.environ.get("MK_UPTO") == "p3a":
            def _fin0(sy):
                qsy.w(sy, "ist0")
                qsy.w(sy, "ist1")
            SY(_fin0)
            st_p3a.close()
            st_dq.close()
            return nc

        nc.leave_named_scope("p3a", _sc5[0], False)
        _sc6 = nc.enter_named_scope("p3b", False)
        st_p3a.close()
        st_p3b = contextlib.ExitStack()
        JH = c.JC // 2
        wdl_sb = [sbin(st_p3b, f"wdl{i}", [128, JH * 512], BF16)
                  for i in range(2)]
        it_sb = [sbin(st_p3b, f"it{i}", [128, c.JC * 128], BF16)
                 for i in range(2)]
        yev_sb = [sbin(st_p3b, f"yev{i}", [128, 512], F32) for i in range(2)]
        l2a_sb = sbin(st_p3b, "l2a_sb", [128, c.DM], F32)
        DH = c.DM // 2
        l2h_sb = sbin(st_p3b, "l2h_sb", [128, DH], BF16)

        NDB = c.DM // 512
        p3_dn = {}
        p3_ev = {}
        p3_ln = {}

        def p3_wdload(db, half):
            def _w(s, db=db, half=half):
                sl = half
                if db >= 1:
                    s.wait_ge(tS, p3_dn[(db - 1, c.CPT - 1)])
                if db == 0:
                    qsy.w(s, "ist0")
                    qsy.w(s, "ist1")
                dq_wait_spilled(s, NGU_T + (db // 2 + 1) * c.JC - 1)
                j0 = half * JH
                qsc.dma(s, f"wdq{sl}", wdl_sb[sl][:, :]
                        .rearrange("p (j q) -> p j q", q=512),
                        wd_deq[j0:j0 + JH, :, db * 512:(db + 1) * 512]
                        .rearrange("j p q -> p j q"))
            SC(_w)

        def p3_itload(db, tc):
            def _i(sy, db=db, tc=tc):
                idx = db * c.CPT + tc
                if idx >= 2:
                    pdb, ptc = (idx - 2) // c.CPT, (idx - 2) % c.CPT
                    sy.wait_ge(tS, p3_dn[(pdb, ptc)])
                if idx == 0:
                    qsy.w(sy, "ist0")
                    qsy.w(sy, "ist1")
                qsy.dma(sy, f"it{idx % 2}", it_sb[idx % 2][:, :]
                        .rearrange("p (j q) -> p j q", q=128),
                        inter_d[:, :, tc * 128:(tc + 1) * 128]
                        .rearrange("j p q -> p j q"))
            SY(_i)

        p3_wdload(0, 0)
        p3_wdload(0, 1)
        for pre in range(min(2, NDB * c.CPT)):
            p3_itload(pre // c.CPT, pre % c.CPT)
        for db in range(NDB):
            for tc in range(c.CPT):
                idx = db * c.CPT + tc
                slot = idx % 4
                pp = (psA, psB)[slot // 2]
                pcol = (slot % 2) * 512

                def _dn(t, db=db, tc=tc, idx=idx, pp=pp, pcol=pcol):
                    if tc == 0:
                        qsc.w(t, "wdq0")
                        qsc.w(t, "wdq1")
                    qsy.w(t, f"it{idx % 2}")
                    if idx >= 4:
                        t.wait_ge(aS, p3_ev[((idx - 4) // c.CPT,
                                            (idx - 4) % c.CPT)])
                    for j in range(c.JC):
                        half, jl = divmod(j, JH)
                        lh = it_sb[idx % 2][:, j * 128:(j + 1) * 128]
                        MM(t, pp[:, pcol:pcol + 512], lh,
                           wdl_sb[half][:, jl * 512:(jl + 1) * 512],
                           start=(j == 0), stop=(j == c.JC - 1))
                    pemark()
                    p3_dn[(db, tc)] = pe.n
                TE(_dn)
                if idx + 2 < NDB * c.CPT:
                    p3_itload((idx + 2) // c.CPT, (idx + 2) % c.CPT)

                def _ev(s, db=db, tc=tc, idx=idx, pp=pp, pcol=pcol):
                    es = idx % 2
                    s.wait_ge(tS, p3_dn[(db, tc)])
                    if idx >= 2:
                        qsc.w(s, f"ydw{idx % 2}")
                    act.c(s.copy(yev_sb[es][:, :], pp[:, pcol:pcol + 512]))
                    p3_ev[(db, tc)] = act.n
                    s.wait_ge(aS, p3_ev[(db, tc)])
                    qsc.dma(s, f"ydw{idx % 2}",
                            yd[tc * 128:(tc + 1) * 128,
                               db * 512:(db + 1) * 512], yev_sb[es][:, :])
                SC(_ev)
                dq_emit_some(6)
            if db + 1 < NDB:
                p3_wdload(db + 1, 0)
                p3_wdload(db + 1, 1)

        def emit_ln2(w2):
            def _l2ld(sy, w2=w2):
                if w2 == 0:
                    qsc.w(sy, "ydw0")
                    qsc.w(sy, "ydw1")
                if w2 >= 1:
                    qsy.w(sy, "yst")
                qsy.dma(sy, "l2ld", l2a_sb[:, :],
                        yd[w2 * 128:(w2 + 1) * 128, :])
                qsy.dmat(sy, "l2ld", l2h_sb[:, :],
                         xlnh_d[0:DH, w2 * 128:(w2 + 1) * 128])
            SY(_l2ld)

            rsrc = [(xlnh_d, 1), (xlnl_d, 0), (xlnl_d, 1)]
            for rnd in range(4):
                def _l2a0(v, w2=w2, rnd=rnd):
                    qsy.w(v, "l2ld")
                    hf = rnd % 2
                    dve.c(v.tensor_tensor(l2a_sb[:, hf * DH:(hf + 1) * DH],
                                          l2a_sb[:, hf * DH:(hf + 1) * DH],
                                          l2h_sb[:, :], ALU.add))
                    p3_ln[(w2, "add", rnd)] = dve.n
                VE(_l2a0)
                if rnd < 3:
                    def _l2ld2(sy, w2=w2, rnd=rnd):
                        sy.wait_ge(vS, p3_ln[(w2, "add", rnd)])
                        src, hf = rsrc[rnd]
                        qsy.dmat(sy, "l2ld", l2h_sb[:, :],
                                 src[hf * DH:(hf + 1) * DH,
                                     w2 * 128:(w2 + 1) * 128])
                    SY(_l2ld2)

            def _l2a(v, w2=w2):
                v.wait_ge(vS, p3_ln[(w2, "add", 3)])
                dve.c(v.tensor_reduce(rows_s[:, 0:1], l2a_sb[:, :], AX,
                                      ALU.add))
                p3_ln[(w2, "sum")] = dve.n
            VE(_l2a)

            def _l2sq(s, w2=w2):
                s.wait_ge(vS, p3_ln[(w2, "sum")])
                act.c(s.activation(l2h_sb[:, :], l2a_sb[:, 0:DH], AF.Square,
                                   accum_out=rows_s[:, 1:2]))
                s.wait_ge(aS, act.n)
                act.c(s.activation(l2h_sb[:, :], l2a_sb[:, DH:c.DM],
                                   AF.Square, accum_out=rows_s[:, 9:10]))
                p3_ln[(w2, "sq")] = act.n
            SC(_l2sq)

            def _l2v0(v, w2=w2):
                v.wait_ge(aS, p3_ln[(w2, "sq")])
                dve.c(v.tensor_tensor(rows_s[:, 1:2], rows_s[:, 1:2],
                                      rows_s[:, 9:10], ALU.add))
                v.wait_ge(vS, dve.n)
                dve.c(v.tensor_scalar(rows_s[:, 2:3], rows_s[:, 0:1],
                                      1.0 / c.DM, None, ALU.mult))
                v.wait_ge(vS, dve.n)
                dve.c(v.tensor_tensor(rows_s[:, 3:4], rows_s[:, 2:3],
                                      rows_s[:, 2:3], ALU.mult))
                v.wait_ge(vS, dve.n)
                dve.c(v.tensor_scalar(rows_s[:, 4:5], rows_s[:, 1:2],
                                      1.0 / c.DM, None, ALU.mult))
                v.wait_ge(vS, dve.n)
                dve.c(v.tensor_tensor(rows_s[:, 5:6], rows_s[:, 4:5],
                                      rows_s[:, 3:4], ALU.subtract))
                v.wait_ge(vS, dve.n)
                dve.c(v.tensor_scalar(rows_s[:, 5:6], rows_s[:, 5:6], 1e-5,
                                      None, ALU.add))
                p3_ln[(w2, "var")] = dve.n
            VE(_l2v0)

            def _l2s(s, w2=w2):
                s.wait_ge(vS, p3_ln[(w2, "var")])
                act.c(s.activation(rows_s[:, 6:7], rows_s[:, 5:6], AF.Sqrt))
                p3_ln[(w2, "std")] = act.n
            SC(_l2s)

            def _l2ap(v, w2=w2):
                v.wait_ge(aS, p3_ln[(w2, "std")])
                dve.c(v.reciprocal(rows_s[:, 7:8], rows_s[:, 6:7]))
                v.wait_ge(vS, dve.n)
                dve.c(v.tensor_scalar(l2a_sb[:, :], l2a_sb[:, :],
                                      rows_s[:, 2:3], rows_s[:, 7:8],
                                      ALU.subtract, ALU.mult))
                p3_ln[(w2, "apply")] = dve.n
            VE(_l2ap)

            def _l2y(sy, w2=w2):
                sy.wait_ge(vS, p3_ln[(w2, "apply")])
                qsy.dma(sy, "yst", y[w2 * 128:(w2 + 1) * 128, :],
                        l2a_sb[:, :])
            SY(_l2y)

        for w2 in range(c.CPT):
            emit_ln2(w2)

        nc.leave_named_scope("p3b", _sc6[0], False)
        def _fin(sy):
            qsy.w(sy, "yst")
        SY(_fin)
        st_p3b.close()
        st_dq.close()

    return nc


_CACHE = {}


def kernel(L0_attn_hidden_states, L0_attn_w_attn, L0_attn_b_attn,
           L0_attn_w_proj, L0_attn_b_proj, L0_ffn_gate_codes,
           L0_ffn_gate_scales, L0_ffn_up_codes, L0_ffn_up_scales,
           L0_ffn_down_codes, L0_ffn_down_scales):
    cfg = Cfg()
    in_maps = stage_inputs(
        cfg, L0_attn_hidden_states, L0_attn_w_attn, L0_attn_b_attn,
        L0_attn_w_proj, L0_attn_b_proj, L0_ffn_gate_codes, L0_ffn_gate_scales,
        L0_ffn_up_codes, L0_ffn_up_scales, L0_ffn_down_codes,
        L0_ffn_down_scales)
    if "nc" not in _CACHE:
        _CACHE["nc"] = build_program(cfg)
    nc = _CACHE["nc"]
    res = run_bass_kernel_spmd(nc, in_maps, core_ids=list(range(cfg.NC)))
    out = np.concatenate([r["y"] for r in res.results], axis=0)
    return out.reshape(cfg.B, cfg.S, cfg.DM).astype(np.float32)


# version 11
# speedup vs baseline: 1.3355x; 1.1134x over previous
import contextlib
import math
import os

import numpy as np
import ml_dtypes

import concourse.bass as bass
import concourse.mybir as mybir
from concourse.bass_utils import run_bass_kernel_spmd

F32 = mybir.dt.float32
BF16 = mybir.dt.bfloat16
F16 = mybir.dt.float16
I32 = mybir.dt.int32
U8 = mybir.dt.uint8
ALU = mybir.AluOpType
AF = mybir.ActivationFunctionType
AX = mybir.AxisListType.X
NPBF = ml_dtypes.bfloat16

NF4 = np.array([
    -1.0, -0.6961928009986877, -0.5250730514526367, -0.39491748809814453,
    -0.28444138169288635, -0.18477343022823334, -0.09105003625154495, 0.0,
    0.07958029955625534, 0.16093020141124725, 0.24611230194568634,
    0.33791524171829224, 0.44070982933044434, 0.5626170039176941,
    0.7229568362236023, 1.0], dtype=np.float64)

DQ_CA = 4.5
DQ_CB = 10.5
NFG = (NF4[1:] - NF4[:-1])
NFDA = (NF4[:-1] - NFG * (np.arange(15) - DQ_CA))
NFDB = (NF4[:-1] - NFG * (np.arange(15) - DQ_CB))
NFG = [float(v) for v in NFG]
NFDA = [float(v) for v in NFDA]
NFDB = [float(v) for v in NFDB]

DEAD = -60000.0


def alibi_slopes(n):
    cp2 = 2 ** math.floor(math.log2(n))
    base = 2 ** (-(2 ** (-(math.log2(cp2) - 3))))
    slopes = base ** np.arange(1, cp2 + 1, dtype=np.float32)
    if cp2 != n:
        eb = 2 ** (-(2 ** (-(math.log2(2 * cp2) - 3))))
        extra = eb ** np.arange(1, 1 + 2 * min(cp2, n - cp2), 2,
                                dtype=np.float32)
        slopes = np.concatenate([slopes, extra])
    return slopes.astype(np.float32)


class Cfg:
    def __init__(self, NH=32, B=4, S=2048, I=11008):
        self.NC = 8
        self.D = 128
        self.NH = NH
        self.DM = NH * self.D
        self.HC = self.DM // 128
        self.B = B
        self.S = S
        self.TB = S
        self.TPC = S // 2
        self.KC = S // 128
        self.KH = self.KC // 2
        self.CC = 3 * NH
        self.CCH = 6
        self.NPASS = self.CC // self.CCH
        self.TSUP = min(512, S)
        self.NTS = self.TB // self.TSUP
        self.I = I
        self.JC = I // 128
        self.SCALE = 1.0 / self.D
        self.DQW = 1024
        self.NCGU = -(-I // self.DQW)
        self.NCD = self.DM // self.DQW
        self.CPT = self.TPC // 128


def stage_inputs(cfg, x, w_attn, b_attn, w_proj, b_proj,
                 g_codes, g_scales, u_codes, u_scales, d_codes, d_scales):
    c = cfg
    slopes = alibi_slopes(c.NH)
    x = np.asarray(x, dtype=np.float32)
    w_attn = np.asarray(w_attn, dtype=np.float32)
    b_attn = np.asarray(b_attn, dtype=np.float32)
    w_proj = np.asarray(w_proj, dtype=np.float32)
    b_proj = np.asarray(b_proj, dtype=np.float32)

    wq = np.concatenate(
        [w_attn[:, (g * c.DM + h * 128):(g * c.DM + (h + 1) * 128)]
         for h in range(c.NH) for g in range(3)], axis=1)
    wq3 = np.ascontiguousarray(wq).reshape(c.HC, 128, c.CC * 128)
    bqcols = np.stack(
        [b_attn[g * c.DM + h * 128:g * c.DM + (h + 1) * 128]
         for h in range(c.NH) for g in range(3)], axis=1)
    bq = np.ascontiguousarray(bqcols.astype(np.float32))

    wp4 = np.ascontiguousarray(
        w_proj.reshape(c.HC, 128, c.HC, 128).transpose(2, 0, 1, 3))
    bpc = np.ascontiguousarray(b_proj.reshape(c.HC, 128).T)

    gT = np.ascontiguousarray(np.asarray(g_codes).T.astype(np.uint8))
    uT = np.ascontiguousarray(np.asarray(u_codes).T.astype(np.uint8))
    dTr = np.ascontiguousarray(np.asarray(d_codes).T.astype(np.uint8))
    gcT = gT.reshape(c.HC, 128, c.I)
    ucT = uT.reshape(c.HC, 128, c.I)
    dcT = dTr.reshape(c.JC, 128, c.DM)
    gsT = np.ascontiguousarray(np.asarray(g_scales).T)
    usT = np.ascontiguousarray(np.asarray(u_scales).T)
    dsT = np.ascontiguousarray(np.asarray(d_scales).T)

    common = {
        "wq3": wq3, "bq": bq, "wp4": wp4, "bpc": bpc,
        "ones22": np.ones((2, 128), dtype=NPBF),
        "onesc": np.ones((128, 1), dtype=NPBF),
        "onesf": np.ones((128, 1), dtype=np.float32),
        "masku": np.ascontiguousarray(
            np.where(np.arange(128)[:, None] > np.arange(128)[None, :],
                     np.float32(-1e30), np.float32(0.0)).astype(np.float32)),
        "gcT": gcT, "ucT": ucT, "dcT": dcT,
        "gsT": gsT, "usT": usT, "dsT": dsT,
    }

    in_maps = []
    for r in range(c.NC):
        b, h = r // 2, r % 2
        m = dict(common)
        xb = x[b]
        xT = np.ascontiguousarray(xb.T)
        mine = xT[:, h * c.TPC:(h + 1) * c.TPC]
        othr = xT[:, (1 - h) * c.TPC:(2 - h) * c.TPC]
        m["xT3"] = np.ascontiguousarray(
            np.concatenate([mine, othr], axis=1)).reshape(c.HC, 128, c.TB)
        ak = np.zeros((128, c.NH * c.KC), np.float32)
        p = np.arange(128, dtype=np.float32)
        for hh in range(c.NH):
            sl = slopes[hh]
            for ci in range(c.KC):
                col = hh * c.KC + ci
                if ci < c.KH:
                    kg = h * c.TPC + ci * 128 + p
                    ak[:, col] = sl * kg
                elif h == 1:
                    kg = (ci - c.KH) * 128 + p
                    ak[:, col] = sl * kg
                else:
                    ak[:, col] = DEAD
        m["alik"] = ak
        aq = np.zeros((c.NH, c.TPC), np.float32)
        qg = h * c.TPC + np.arange(c.TPC, dtype=np.float32)
        for hh in range(c.NH):
            aq[hh] = -float(c.D) * slopes[hh] * qg
        m["aliqm"] = np.ascontiguousarray(aq)
        in_maps.append(m)
    return in_maps


class Q:
    def __init__(self, mksem, prefix):
        self.mksem = mksem
        self.prefix = prefix
        self.streams = {}

    def _s(self, stream):
        if stream not in self.streams:
            self.streams[stream] = [self.mksem(f"{self.prefix}_{stream}"), 0]
        return self.streams[stream]

    def dma(self, eng, stream, out, in_, **kw):
        s = self._s(stream)
        eng.dma_start(out=out, in_=in_, **kw).then_inc(s[0], 16)
        s[1] += 16
        return s[1]

    def dmat(self, eng, stream, out, in_):
        s = self._s(stream)
        eng.dma_start(out=out, in_=in_, transpose=True).then_inc(s[0], 16)
        s[1] += 16
        return s[1]

    def tot(self, stream):
        return self._s(stream)

    def w(self, eng, stream):
        s = self._s(stream)
        eng.wait_ge(s[0], s[1])


class E:
    def __init__(self, sem):
        self.sem = sem
        self.n = 0

    def c(self, inst):
        inst.then_inc(self.sem, 1)
        self.n += 1
        return self.n


def chunks(total, w=512):
    return [(i, min(w, total - i)) for i in range(0, total, w)]


def bpieces(lo, hi):
    out = []
    while lo < hi:
        nxt = min(hi, (lo // 512 + 1) * 512)
        out.append((lo, nxt - lo))
        lo = nxt
    return out


def build_program(cfg):
    c = cfg
    nc = bass.Bass("TRN2", target_bir_lowering=False)
    dt = nc.dram_tensor

    xT3 = dt("xT3", [c.HC, 128, c.TB], F32, kind="ExternalInput")
    wq3 = dt("wq3", [c.HC, 128, c.CC * 128], F32, kind="ExternalInput")
    bq = dt("bq", [128, c.CC], F32, kind="ExternalInput")
    aliqm = dt("aliqm", [c.NH, c.TPC], F32, kind="ExternalInput")
    alik = dt("alik", [128, c.NH * c.KC], F32, kind="ExternalInput")
    masku = dt("masku", [128, 128], F32, kind="ExternalInput")
    ones22 = dt("ones22", [2, 128], BF16, kind="ExternalInput")
    onesc = dt("onesc", [128, 1], BF16, kind="ExternalInput")
    onesf = dt("onesf", [128, 1], F32, kind="ExternalInput")
    wp4 = dt("wp4", [c.HC, c.HC, 128, 128], F32, kind="ExternalInput")
    bpc = dt("bpc", [128, c.HC], F32, kind="ExternalInput")
    gcT = dt("gcT", [c.HC, 128, c.I], U8, kind="ExternalInput")
    ucT = dt("ucT", [c.HC, 128, c.I], U8, kind="ExternalInput")
    dcT = dt("dcT", [c.JC, 128, c.DM], U8, kind="ExternalInput")
    gsT = dt("gsT", [c.DM // 64, c.I], F32, kind="ExternalInput")
    usT = dt("usT", [c.DM // 64, c.I], F32, kind="ExternalInput")
    dsT = dt("dsT", [c.I // 64, c.DM], F32, kind="ExternalInput")
    y = dt("y", [c.TPC, c.DM], F32, kind="ExternalOutput")

    xbf_d = dt("xbf_d", [c.HC, 128, c.TB], BF16)
    qkv_sp = dt("qkv_sp", [c.CC * 128, c.TB], BF16)
    q2k2_d = dt("q2k2_d", [2 * c.NH, c.TB], F32)
    frow_d = dt("frow_d", [2, c.TPC], BF16)
    ot_d = dt("ot_d", [c.NH, 128, c.TPC], BF16)
    rrl_d = dt("rrl_d", [2 * c.NH, c.TPC], BF16)
    pres_d = dt("pres_d", [c.HC, 128, c.TPC], F32)
    xlnh_d = dt("xlnh_d", [c.DM, c.TPC], BF16)
    xlnl_d = dt("xlnl_d", [c.DM, c.TPC], BF16)
    abrow_d = dt("abrow_d", [4, c.TPC], BF16)
    wg_deq = dt("wg_deq", [128, c.JC, c.HC, 128], BF16)
    wu_deq = dt("wu_deq", [128, c.JC, c.HC, 128], BF16)
    wd_deq = dt("wd_deq", [c.JC, 128, c.DM], BF16)
    inter_d = dt("inter_d", [c.JC, 128, c.TPC], BF16)
    yd = dt("yd", [c.TPC, c.DM], F32)

    st = contextlib.ExitStack()
    with st:
        def sbin(stack, name, shape, dtt):
            return stack.enter_context(nc.sbuf_tensor(name, shape, dtt))

        def sb(name, shape, dtt):
            return st.enter_context(nc.sbuf_tensor(name, shape, dtt))

        def psum(name, shape, dtt=F32):
            return st.enter_context(nc.psum_tensor(name, shape, dtt))

        def sem(name):
            return st.enter_context(nc.semaphore(name))

        ones22_s = sb("ones22_s", [2, 128], BF16)
        onesc_s = sb("onesc_s", [128, 1], BF16)
        onescf_s = sb("onescf_s", [128, 1], F32)
        masku_s = sb("masku_s", [128, 128], F32)
        rows_s = sb("rows_s", [128, 16], F32)
        bq_s = sb("bq_s", [128, c.CC], F32)
        alik_s = sb("alik_s", [128, c.NH * c.KC], F32)
        aliq_sb = sb("aliq_sb", [c.NH, c.TPC], F32)
        bpc_s = sb("bpc_s", [128, c.HC], F32)
        rwm = sb("rwm", [c.NH, 1], F32)
        rwf = sb("rwf", [c.NH, c.TB], F32)
        rwfh = sb("rwfh", [c.NH, c.TPC], BF16)
        rwfl = sb("rwfl", [c.NH, c.TPC], BF16)
        q2st = sb("q2st", [1, 2 * c.TSUP], F32)
        psA = psum("psA", [128, 1024])
        psB = psum("psB", [128, 1024])
        psC = psum("psC", [128, 1024])
        psD = psum("psD", [128, 1024])

        tS = sem("tS")
        aS = sem("aS")
        vS = sem("vS")

        block = st.enter_context(nc.Block())
        qsy = Q(sem, "sy")
        qsc = Q(sem, "sc")
        qgp = Q(sem, "gp")
        pe = E(tS)
        act = E(aS)
        dve = E(vS)

        SY, SC, GP, VE, TE = (block.sync, block.scalar, block.gpsimd,
                              block.vector, block.tensor)
        rt = {}
        last_mm = [None]

        def MM(t, *a, **kw):
            last_mm[0] = t.matmul(*a, **kw)
            return last_mm[0]

        def pemark():
            return pe.c(last_mm[0])

        DQW = c.DQW
        st_dq = contextlib.ExitStack()
        dq_c = sbin(st_dq, "dq_c", [128, 2 * DQW], F16)
        dq_ca = sbin(st_dq, "dq_ca", [128, DQW], F16)
        dq_cb = sbin(st_dq, "dq_cb", [128, DQW], F16)
        dq_m = sbin(st_dq, "dq_m", [128, DQW], F16)
        dq_x = sbin(st_dq, "dq_x", [128, DQW], F16)
        dq_l7 = sbin(st_dq, "dq_l7", [128, DQW], F16)
        dq_t = sbin(st_dq, "dq_t", [128, DQW], F16)
        dq_s = sbin(st_dq, "dq_s", [128, 2 * DQW], F16)
        dq_w = sbin(st_dq, "dq_w", [128, 2 * DQW], BF16)

        dq_tiles = []
        for colt in range(c.NCGU):
            col = colt * DQW
            w = min(DQW, c.I - col)
            for mat in ("g", "u"):
                for hc in range(c.HC):
                    dq_tiles.append((mat, hc, col, w))
        for colt in range(c.NCD):
            for jc in range(c.JC):
                dq_tiles.append(("d", jc, colt * DQW, DQW))
        NDQ = len(dq_tiles)
        NGU_T = 2 * c.HC * c.NCGU
        dq_vd = {}
        dq_spd = {}
        dq_bl = {}
        dq_bcons = {}
        dq_next = [0]

        def dq_emit_one(i):
            mat, a, col, w = dq_tiles[i]
            cs = i % 2
            cw = slice(cs * DQW, cs * DQW + w)
            ww = slice(0, w)

            def _ld(g):
                if i >= 2:
                    g.wait_ge(vS, dq_vd[i - 2])
                src, srows = {"g": (gcT, gsT), "u": (ucT, usT),
                              "d": (dcT, dsT)}[mat]
                qgp.dma(g, f"dqc{i % 2}", dq_c[:, cw], src[a, :, col:col + w])
                qgp.dma(g, f"dqc{i % 2}", dq_s[0:64, cw],
                        srows[2 * a:2 * a + 1, col:col + w]
                        .partition_broadcast(64))
                qgp.dma(g, f"dqc{i % 2}", dq_s[64:128, cw],
                        srows[2 * a + 1:2 * a + 2, col:col + w]
                        .partition_broadcast(64))
            GP(_ld)

            def _tree(v):
                qgp.w(v, f"dqc{i % 2}")
                if i >= 2:
                    (qsy, qsc)[i % 2].w(v, f"dqw{i % 2}")
                cc_ = dq_c[:, cw]
                ca_, cb_ = dq_ca[:, ww], dq_cb[:, ww]
                m_, x_, l7_, t_ = (dq_m[:, ww], dq_x[:, ww],
                                   dq_l7[:, ww], dq_t[:, ww])
                dve.c(v.tensor_scalar(ca_, cc_, -DQ_CA, None, ALU.add))
                dve.c(v.tensor_scalar(cb_, cc_, -DQ_CB, None, ALU.add))
                v.wait_ge(vS, dve.n)
                dve.c(v.tensor_scalar(l7_, ca_, NFG[7], NFDA[7],
                                      ALU.mult, ALU.add))
                dve.c(v.tensor_scalar(m_, ca_, NFG[0], NFDA[0],
                                      ALU.mult, ALU.add))
                v.wait_ge(vS, dve.n)
                for i2 in range(1, 7):
                    dve.c(v.tensor_scalar(t_, ca_, NFG[i2], NFDA[i2],
                                          ALU.mult, ALU.add))
                    v.wait_ge(vS, dve.n)
                    dve.c(v.tensor_tensor(m_, m_, t_, ALU.min))
                    v.wait_ge(vS, dve.n)
                dve.c(v.tensor_tensor(m_, m_, l7_, ALU.min))
                dve.c(v.tensor_scalar(x_, cb_, NFG[8], NFDB[8],
                                      ALU.mult, ALU.add))
                v.wait_ge(vS, dve.n)
                for i2 in range(9, 15):
                    dve.c(v.tensor_scalar(t_, cb_, NFG[i2], NFDB[i2],
                                          ALU.mult, ALU.add))
                    v.wait_ge(vS, dve.n)
                    dve.c(v.tensor_tensor(x_, x_, t_, ALU.max))
                    v.wait_ge(vS, dve.n)
                dve.c(v.tensor_scalar(t_, cb_, NFG[7], NFDB[7],
                                      ALU.mult, ALU.add))
                v.wait_ge(vS, dve.n)
                dve.c(v.tensor_tensor(x_, x_, t_, ALU.max))
                v.wait_ge(vS, dve.n)
                dve.c(v.tensor_tensor(m_, m_, x_, ALU.add))
                v.wait_ge(vS, dve.n)
                dve.c(v.tensor_tensor(m_, m_, l7_, ALU.subtract))
                v.wait_ge(vS, dve.n)
                dve.c(v.tensor_tensor(dq_w[:, cw], m_, dq_s[:, cw], ALU.mult))
                dq_vd[i] = dve.n
            VE(_tree)

            def _sp(e):
                qq = (qsy, qsc)[i % 2]
                e.wait_ge(vS, dq_vd[i])
                if mat in ("g", "u"):
                    dst = wg_deq if mat == "g" else wu_deq
                    tot = qq.dma(e, f"dqw{i % 2}",
                                 dst[:, col // 128:(col + w) // 128, a, :],
                                 dq_w[:, cs * DQW:cs * DQW + w]
                                 .rearrange("p (j q) -> p j q", q=128))
                else:
                    tot = qq.dma(e, f"dqw{i % 2}", wd_deq[a, :, col:col + w],
                                 dq_w[:, cw])
                dq_spd[i] = (i % 2, tot)
            (SY, SC)[i % 2](_sp)

        def dq_emit_some(k):
            e = min(dq_next[0] + k, NDQ)
            for i in range(dq_next[0], e):
                dq_emit_one(i)
            dq_next[0] = e

        def dq_wait_spilled(eng, upto_i):
            tots = {}
            for i in range(upto_i + 1):
                par, t = dq_spd[i]
                tots[par] = max(tots.get(par, 0), t)
            for par, t in tots.items():
                qq = (qsy, qsc)[par]
                eng.wait_ge(qq.tot(f"dqw{par}")[0], t)

        def _p0(g):
            qgp.dma(g, "const", ones22_s[:, :], ones22[:, :])
            qgp.dma(g, "const", onesc_s[:, :], onesc[:, :])
            qgp.dma(g, "const", onescf_s[:, :], onesf[:, :])
            qgp.dma(g, "const", masku_s[:, :], masku[:, :])
            qgp.dma(g, "const", bq_s[:, :], bq[:, :])
            qgp.dma(g, "const", alik_s[:, :], alik[:, :])
            qgp.dma(g, "const", aliq_sb[:, :], aliqm[:, :])
            qgp.dma(g, "const", bpc_s[:, :], bpc[:, :])
        GP(_p0)

        _sc1 = nc.enter_named_scope("p1a", False)
        st_p1a = contextlib.ExitStack()
        xsb = [sbin(st_p1a, f"xsb{i}", [128, c.HC * c.TSUP], BF16)
               for i in range(2)]
        wq_sb = sbin(st_p1a, "wq_sb", [128, c.HC * c.CCH * 128], BF16)
        qe_sb = sbin(st_p1a, "qe_sb", [128, 4 * c.TSUP], BF16)
        sqe_sb = sbin(st_p1a, "sqe_sb", [128, 2 * c.TSUP], BF16)
        xc_sb = [sbin(st_p1a, f"xc{i}", [128, c.TB], BF16) for i in range(2)]

        xcast_done = {}
        for hc in range(c.HC):
            def _xc(g, hc=hc):
                sl = hc % 2
                if hc >= 2:
                    qgp.w(g, f"xcb{sl}")
                qgp.dma(g, f"xca{sl}", xc_sb[sl][:, :], xT3[hc, :, :])
                qgp.w(g, f"xca{sl}")
                tot = qgp.dma(g, f"xcb{sl}", xbf_d[hc, :, :], xc_sb[sl][:, :])
                xcast_done[hc] = tot
            GP(_xc)

        x_ld = {}
        p1a_mm = {}
        p1a_ev = {}
        p1a_sq = {}
        p1a_q2mm = {}
        p1a_q2ev = {}
        nsq = [0]

        def p1a_wqload(ps):
            def _wq(g, ps=ps):
                if ps >= 1:
                    g.wait_ge(tS, p1a_mm[(ps * c.NTS - 1, c.CCH - 1)])
                nq = c.HC // 4
                for qtr in range(4):
                    qgp.dma(g, "wqp", wq_sb[:, qtr * nq * c.CCH * 128:
                                            (qtr + 1) * nq * c.CCH * 128]
                            .rearrange("p (h q) -> p h q", q=c.CCH * 128),
                            wq3[qtr * nq:(qtr + 1) * nq, :,
                                ps * c.CCH * 128:(ps + 1) * c.CCH * 128]
                            .rearrange("h p q -> p h q"))
            GP(_wq)

        def p1a_load(ts2):
            def _l(g, ts2=ts2):
                if ts2 == 0:
                    qgp.w(g, "xcb0")
                    qgp.w(g, "xcb1")
                if ts2 >= 2:
                    g.wait_ge(tS, p1a_mm[(ts2 - 2, c.CCH - 1)])
                nq4 = c.HC // 4
                ts = ts2 % c.NTS
                for q4 in range(4):
                    qgp.dma(g, f"xs{ts2 % 2}",
                            xsb[ts2 % 2][:, q4 * nq4 * c.TSUP:
                                         (q4 + 1) * nq4 * c.TSUP]
                            .rearrange("p (h t) -> p h t", t=c.TSUP),
                            xbf_d[q4 * nq4:(q4 + 1) * nq4, :,
                                  ts * c.TSUP:(ts + 1) * c.TSUP]
                            .rearrange("h p t -> p h t"))
            GP(_l)

        p1a_wqload(0)
        p1a_load(0)
        p1a_load(1)
        for ts2 in range(c.NPASS * c.NTS):
            ps = ts2 // c.NTS
            ts = ts2 % c.NTS
            for ccl in range(c.CCH):
                cc = ps * c.CCH + ccl
                idx = ts2 * c.CCH + ccl
                slot = idx % 4
                pp = (psA, psB)[slot // 2]
                col = (slot % 2) * 512

                def _mm(t, ts2=ts2, ccl=ccl, cc=cc, idx=idx, pp=pp, col=col):
                    if ccl == 0:
                        qgp.w(t, f"xs{ts2 % 2}")
                        if ts2 == 0:
                            qgp.w(t, "const")
                        if ts2 % c.NTS == 0:
                            qgp.w(t, "wqp")
                    if idx >= 4:
                        t.wait_ge(aS, p1a_ev[idx - 4])
                    for hc in range(c.HC):
                        MM(t, pp[:, col:col + c.TSUP],
                           wq_sb[:, (hc * c.CCH + ccl) * 128:
                                 (hc * c.CCH + ccl + 1) * 128],
                           xsb[ts2 % 2][:, hc * c.TSUP:(hc + 1) * c.TSUP],
                           start=(hc == 0), stop=(hc == c.HC - 1))
                    pemark()
                    p1a_mm[(ts2, ccl)] = pe.n
                TE(_mm)

                def _ev(s, ts2=ts2, ts=ts, cc=cc, idx=idx, pp=pp, col=col):
                    es = idx % 4
                    s.wait_ge(tS, p1a_mm[(ts2, cc % c.CCH)])
                    if idx >= 4:
                        qsc.w(s, f"sp{idx % 4}")
                    act.c(s.activation(qe_sb[:, es * c.TSUP:(es + 1) * c.TSUP],
                                       pp[:, col:col + c.TSUP],
                                       AF.Identity, bias=bq_s[:, cc:cc + 1]))
                    p1a_ev[idx] = act.n
                    s.wait_ge(aS, p1a_ev[idx])
                    qsc.dma(s, f"sp{idx % 4}",
                            qkv_sp[cc * 128:(cc + 1) * 128,
                                   ts * c.TSUP:(ts + 1) * c.TSUP],
                            qe_sb[:, es * c.TSUP:(es + 1) * c.TSUP])
                SC(_ev)

                if cc % 3 < 2:
                    si = nsq[0]
                    nsq[0] += 1

                    def _sq(s, idx=idx, si=si):
                        ss = si % 2
                        s.wait_ge(aS, p1a_ev[idx])
                        if si >= 2:
                            s.wait_ge(tS, p1a_q2mm[si - 2])
                        act.c(s.activation(
                            sqe_sb[:, ss * c.TSUP:(ss + 1) * c.TSUP],
                            qe_sb[:, (idx % 4) * c.TSUP:
                                  (idx % 4 + 1) * c.TSUP], AF.Square))
                        p1a_sq[si] = act.n
                    SC(_sq)

                    def _q2(t, si=si):
                        ss = si % 2
                        t.wait_ge(aS, p1a_sq[si])
                        if si >= 2:
                            t.wait_ge(vS, p1a_q2ev[si - 2])
                        MM(t, psC[0:1, ss * 512:ss * 512 + c.TSUP],
                           onesc_s[:, :],
                           sqe_sb[:, ss * c.TSUP:(ss + 1) * c.TSUP],
                           start=True, stop=True)
                        pemark()
                        p1a_q2mm[si] = pe.n
                    TE(_q2)

                    def _q2e(v, si=si):
                        ss = si % 2
                        v.wait_ge(tS, p1a_q2mm[si])
                        if si >= 2:
                            qgp.w(v, "q2st")
                        dve.c(v.tensor_copy(
                            q2st[0:1, ss * c.TSUP:(ss + 1) * c.TSUP],
                            psC[0:1, ss * 512:ss * 512 + c.TSUP]))
                        p1a_q2ev[si] = dve.n
                    VE(_q2e)

                    def _q2w(g, ts=ts, cc=cc, si=si):
                        ss = si % 2
                        g.wait_ge(vS, p1a_q2ev[si])
                        g3 = cc % 3
                        hh = cc // 3
                        row = hh if g3 == 0 else c.NH + hh
                        scol = ts * c.TSUP
                        qgp.dma(g, "q2st", q2k2_d[row:row + 1,
                                                  scol:scol + c.TSUP],
                                q2st[0:1, ss * c.TSUP:(ss + 1) * c.TSUP])
                    GP(_q2w)

            if ts2 + 2 < c.NPASS * c.NTS:
                p1a_load(ts2 + 2)
            if ts2 % c.NTS == c.NTS - 1 and ts2 // c.NTS + 1 < c.NPASS:
                p1a_wqload(ts2 // c.NTS + 1)
            dq_emit_some(2)

        if os.environ.get("MK_UPTO") == "p1a":
            def _fin0(sy):
                qsc.w(sy, "sp0")
                qsc.w(sy, "sp1")
                qsc.w(sy, "sp2")
                qsc.w(sy, "sp3")
            SY(_fin0)
            st_p1a.close()
            st_dq.close()
            return nc

        nc.leave_named_scope("p1a", _sc1[0], False)
        _sc2 = nc.enter_named_scope("rows", False)
        def _rows_gp(g):
            g.wait_ge(vS, p1a_q2ev[nsq[0] - 1])
            qgp.w(g, "q2st")
            qgp.dma(g, "rows", rwf[:, :], q2k2_d[c.NH:2 * c.NH, :])
            rt["rows_k2"] = True
        GP(_rows_gp)

        def _rows_v(v):
            qgp.w(v, "rows")
            dve.c(v.tensor_reduce(rwm[:, :], rwf[:, :], AX, ALU.max))
            rt["rows_max"] = dve.n
        VE(_rows_v)

        def _rows_g2(g):
            g.wait_ge(vS, rt["rows_max"])
            qgp.dma(g, "rows", rwf[:, 0:c.TPC], q2k2_d[0:c.NH, 0:c.TPC])
        GP(_rows_g2)

        def _rows_v1(v):
            qgp.w(v, "rows")
            dve.c(v.tensor_scalar(rwf[:, 0:c.TPC], rwf[:, 0:c.TPC],
                                  rwm[:, 0:1], None, ALU.mult))
            v.wait_ge(vS, dve.n)
            dve.c(v.tensor_scalar(rwf[:, 0:c.TPC], rwf[:, 0:c.TPC], 1e-12,
                                  None, ALU.max))
            rt["rows_mul"] = dve.n
        VE(_rows_v1)

        def _rows_s(s):
            s.wait_ge(vS, rt["rows_mul"])
            act.c(s.activation(rwf[:, 0:c.TPC], rwf[:, 0:c.TPC], AF.Sqrt))
            rt["rows_sqrt"] = act.n
        SC(_rows_s)

        def _rows_v2(v):
            v.wait_ge(aS, rt["rows_sqrt"])
            dve.c(v.scalar_tensor_tensor(rwf[:, 0:c.TPC], rwf[:, 0:c.TPC],
                                         -1.0, aliq_sb[:, :], ALU.mult,
                                         ALU.add))
            v.wait_ge(vS, dve.n)
            dve.c(v.tensor_copy(rwfh[:, :], rwf[:, 0:c.TPC]))
            v.wait_ge(vS, dve.n)
            dve.c(v.tensor_tensor(rwfl[:, :], rwf[:, 0:c.TPC], rwfh[:, :],
                                  ALU.subtract))
            rt["rows_done"] = dve.n
        VE(_rows_v2)

        if os.environ.get("MK_UPTO") == "rows":
            def _fin0(sy):
                sy.wait_ge(vS, rt["rows_done"])
            SY(_fin0)
            st_p1a.close()
            st_dq.close()
            return nc

        nc.leave_named_scope("rows", _sc2[0], False)
        _sc3 = nc.enter_named_scope("p1b", False)
        st_p1a.close()
        st_p1b = contextlib.ExitStack()
        qt_sb = [sbin(st_p1b, f"qt{i}", [128, c.TPC], BF16) for i in range(2)]
        kt_sb = [sbin(st_p1b, f"kt{i}", [128, c.TB], BF16) for i in range(2)]
        vn_sb = [sbin(st_p1b, f"vn{i}", [128, c.TB], BF16) for i in range(2)]
        pt_sb = sbin(st_p1b, "pt_sb", [128, c.KC * c.TPC], BF16)
        frow_s = sbin(st_p1b, "frow_s", [2, c.TPC], BF16)
        rrow_s = sbin(st_p1b, "rrow_s", [1, 2 * c.TPC], F32)
        rrbf_s = sbin(st_p1b, "rrbf_s", [1, 2 * c.TPC], BF16)
        ot_f = sbin(st_p1b, "ot_f", [128, c.TPC], F32)
        ot_b = sbin(st_p1b, "ot_b", [128, c.TPC], BF16)

        p1b_smm = {}
        p1b_mask = {}
        p1b_exp = {}
        p1b_av = {}
        p1b_otf = {}
        p1b_otb = {}
        p1b_rr = {}
        p1b_done_pe = {}

        def qstart(ci):
            return 128 * ci if ci < c.KH else 0

        for hh in range(c.NH):
            idx = hh
            buf = idx % 2

            def _ld(sy, idx=idx, hh=hh, buf=buf):
                if idx == 0:
                    for kq in range(4):
                        qsc.w(sy, f"sp{kq}")
                    qgp.w(sy, "rows")
                    qgp.w(sy, "xcb0")
                    qgp.w(sy, "xcb1")
                    qgp.w(sy, "q2st")
                if idx >= 2:
                    sy.wait_ge(tS, p1b_done_pe[idx - 2])
                r0 = (3 * hh) * 128
                qsy.dma(sy, f"qk{buf}", qt_sb[buf][:, :],
                        qkv_sp[r0:r0 + 128, 0:c.TPC])
                qsy.dma(sy, f"qk{buf}", kt_sb[buf][:, :],
                        qkv_sp[r0 + 128:r0 + 256, :])
                for kc in range(c.KC):
                    qsy.dmat(sy, f"qk{buf}",
                             vn_sb[buf][:, kc * 128:(kc + 1) * 128],
                             qkv_sp[r0 + 256:r0 + 384,
                                    kc * 128:(kc + 1) * 128])
            SY(_ld)

            def _frow(g, idx=idx, hh=hh):
                if idx == 0:
                    g.wait_ge(vS, rt["rows_done"])
                if idx >= 1:
                    qgp.w(g, "frow")
                qgp.dma(g, "frow", frow_d[0:1, :], rwfh[hh:hh + 1, :])
                qgp.dma(g, "frow", frow_d[1:2, :], rwfl[hh:hh + 1, :])
                qgp.w(g, "frow")
                if idx >= 1:
                    g.wait_ge(tS, p1b_done_pe[idx - 1])
                qgp.dma(g, "frow", frow_s[:, :], frow_d[:, :])
            GP(_frow)

            for kci in range(c.KC):
                qs = qstart(kci)
                wdt = c.TPC - qs

                def _smm1(t, idx=idx, hh=hh, buf=buf, kci=kci, qs=qs,
                          wdt=wdt):
                    if kci == 0:
                        qsy.w(t, f"qk{buf}")
                        qgp.w(t, "frow")
                        if idx >= 1:
                            t.wait_ge(aS, p1b_otf[idx - 1])
                            t.wait_ge(vS, p1b_rr[idx - 1])
                    sp = (psA, psB)[kci % 2]
                    if kci >= 2:
                        t.wait_ge(aS, p1b_exp[(idx, kci - 2)])
                    for p0, pw in bpieces(qs, c.TPC):
                        cs = slice(p0, p0 + pw)
                        MM(t, sp[:, cs],
                           kt_sb[buf][:, kci * 128:(kci + 1) * 128],
                           qt_sb[buf][:, p0:p0 + pw],
                           start=True, stop=False)
                        MM(t, sp[:, cs], ones22_s[:, :],
                           frow_s[:, p0:p0 + pw],
                           start=False, stop=True)
                    pemark()
                    p1b_smm[(idx, kci)] = pe.n
                TE(_smm1)

                if kci < c.KH:
                    def _mask1(v, idx=idx, kci=kci, qs=qs):
                        sp = (psA, psB)[kci % 2]
                        v.wait_ge(tS, p1b_smm[(idx, kci)])
                        dve.c(v.tensor_tensor(sp[:, qs:qs + 128],
                                              sp[:, qs:qs + 128],
                                              masku_s[:, :], ALU.add))
                        p1b_mask[(idx, kci)] = dve.n
                    VE(_mask1)

                def _exp1(s, idx=idx, hh=hh, kci=kci, qs=qs):
                    sp = (psA, psB)[kci % 2]
                    s.wait_ge(tS, p1b_smm[(idx, kci)])
                    if (idx, kci) in p1b_mask:
                        s.wait_ge(vS, p1b_mask[(idx, kci)])
                    if idx >= 1 and kci < 2:
                        s.wait_ge(tS, p1b_av[idx - 1])
                    act.c(s.activation(
                        pt_sb[:, kci * c.TPC + qs:(kci + 1) * c.TPC],
                        sp[:, qs:c.TPC], AF.Exp,
                        bias=alik_s[:, hh * c.KC + kci:hh * c.KC + kci + 1],
                        scale=c.SCALE))
                    p1b_exp[(idx, kci)] = act.n
                SC(_exp1)

            def _rsav(t, idx=idx, buf=buf):
                for kci in range(c.KC):
                    qs = qstart(kci)
                    t.wait_ge(aS, p1b_exp[(idx, kci)])
                    first = kci == 0
                    last = kci == c.KC - 1
                    for p0, pw in bpieces(qs, c.TPC):
                        pts = pt_sb[:, kci * c.TPC + p0:
                                    kci * c.TPC + p0 + pw]
                        cs = slice(p0, p0 + pw)
                        MM(t, psD[0:1, cs], onesc_s[:, :], pts,
                           start=first, stop=last)
                        MM(t, psC[:, cs],
                           vn_sb[buf][:, kci * 128:(kci + 1) * 128],
                           pts, start=first, stop=last)
                pemark()
                p1b_av[idx] = pe.n
            TE(_rsav)

            def _rr(v, idx=idx):
                v.wait_ge(tS, p1b_av[idx])
                if idx >= 1:
                    qsc.w(v, "otw")
                dve.c(v.reciprocal(rrow_s[0:1, 0:c.TPC], psD[0:1, 0:c.TPC]))
                v.wait_ge(vS, dve.n)
                dve.c(v.tensor_copy(rrbf_s[0:1, 0:c.TPC],
                                    rrow_s[0:1, 0:c.TPC]))
                v.wait_ge(vS, dve.n)
                dve.c(v.tensor_tensor(rrow_s[0:1, c.TPC:2 * c.TPC],
                                      rrow_s[0:1, 0:c.TPC],
                                      rrbf_s[0:1, 0:c.TPC], ALU.subtract))
                v.wait_ge(vS, dve.n)
                dve.c(v.tensor_copy(rrbf_s[0:1, c.TPC:2 * c.TPC],
                                    rrow_s[0:1, c.TPC:2 * c.TPC]))
                p1b_rr[idx] = dve.n
            VE(_rr)

            def _otf(s, idx=idx):
                s.wait_ge(tS, p1b_av[idx])
                if idx >= 1:
                    s.wait_ge(vS, p1b_otb[idx - 1])
                act.c(s.copy(ot_f[:, :], psC[:, 0:c.TPC]))
                p1b_otf[idx] = act.n
            SC(_otf)

            def _otb(v, idx=idx):
                v.wait_ge(aS, p1b_otf[idx])
                if idx >= 1:
                    qsc.w(v, "otw")
                dve.c(v.tensor_copy(ot_b[:, :], ot_f[:, :]))
                p1b_otb[idx] = dve.n
            VE(_otb)

            def _otw(s, idx=idx, hh=hh):
                s.wait_ge(vS, p1b_otb[idx])
                qsc.dma(s, "otw", ot_d[hh, :, :], ot_b[:, :])
                s.wait_ge(vS, p1b_rr[idx])
                qsc.dma(s, "otw", rrl_d[2 * hh:2 * hh + 1, :],
                        rrbf_s[0:1, 0:c.TPC])
                qsc.dma(s, "otw", rrl_d[2 * hh + 1:2 * hh + 2, :],
                        rrbf_s[0:1, c.TPC:2 * c.TPC])
            SC(_otw)

            p1b_done_pe[idx] = pe.n
            dq_emit_some(2)

        if os.environ.get("MK_UPTO") == "p1b":
            def _fin0(sy):
                qsc.w(sy, "otw")
            SY(_fin0)
            st_p1b.close()
            st_dq.close()
            return nc

        nc.leave_named_scope("p1b", _sc3[0], False)
        _sc4 = nc.enter_named_scope("p2", False)
        st_p1b.close()
        st_p2 = contextlib.ExitStack()
        otm_sb = sbin(st_p2, "otm_sb", [128, c.NH * c.TPC], BF16)
        rr2_sb = sbin(st_p2, "rr2_sb", [2, 2 * c.TPC], BF16)
        wpl_sb = [sbin(st_p2, f"wpl{i}", [128, c.HC * 128], BF16)
                  for i in range(2)]
        xrt_sb = [sbin(st_p2, f"xrt{i}", [128, c.TPC], F32) for i in range(2)]
        rch_sb = sbin(st_p2, "rch_sb", [128, 2 * c.TPC], F32)
        sqp_sb = sbin(st_p2, "sqp_sb", [128, c.TPC], F32)
        ab_sb = sbin(st_p2, "ab_sb", [2, 2 * c.TPC], BF16)
        xlh_sb = sbin(st_p2, "xlh_sb", [128, 2 * c.TPC], BF16)
        xll_sb = sbin(st_p2, "xll_sb", [128, 2 * c.TPC], BF16)

        p2_norm_mm = {}
        p2_norm = {}
        p2_mm = {}
        p2_ev = {}
        p2_row = {}

        def _p2_ld(g):
            qsc.w(g, "otw")
            g.wait_ge(tS, p1b_done_pe[c.NH - 1])
            qsy.w(g, "qk0")
            qsy.w(g, "qk1")
            qgp.w(g, "frow")
            nq4 = c.NH // 4
            for q4 in range(4):
                qgp.dma(g, "p2ld",
                        otm_sb[:, q4 * nq4 * c.TPC:(q4 + 1) * nq4 * c.TPC]
                        .rearrange("p (h t) -> p h t", t=c.TPC),
                        ot_d[q4 * nq4:(q4 + 1) * nq4, :, :]
                        .rearrange("h p t -> p h t"))
        GP(_p2_ld)

        for hc in range(c.NH):
            def _nld(g, hc=hc):
                if hc == 0:
                    qgp.w(g, "p2ld")
                if hc >= 2:
                    g.wait_ge(tS, p2_norm_mm[hc - 2])
                qgp.dma(g, f"rr{hc % 2}",
                        rr2_sb[:, (hc % 2) * c.TPC:(hc % 2 + 1) * c.TPC],
                        rrl_d[2 * hc:2 * hc + 2, :])
            GP(_nld)

            def _nmm(t, hc=hc):
                qgp.w(t, f"rr{hc % 2}")
                pp = (psA, psB)[hc % 2]
                if hc >= 2:
                    t.wait_ge(vS, p2_norm[hc - 2])
                for h0, hw in chunks(c.TPC):
                    MM(t, pp[:, h0:h0 + hw], ones22_s[:, :],
                       rr2_sb[:, (hc % 2) * c.TPC + h0:
                              (hc % 2) * c.TPC + h0 + hw],
                       start=True, stop=True)
                pemark()
                p2_norm_mm[hc] = pe.n
            TE(_nmm)

            def _nmul(v, hc=hc):
                v.wait_ge(tS, p2_norm_mm[hc])
                pp = (psA, psB)[hc % 2]
                dve.c(v.tensor_tensor(otm_sb[:, hc * c.TPC:(hc + 1) * c.TPC],
                                      otm_sb[:, hc * c.TPC:(hc + 1) * c.TPC],
                                      pp[:, 0:c.TPC], ALU.mult))
                p2_norm[hc] = dve.n
            VE(_nmul)
            dq_emit_some(1)

        def p2_wld(oc):
            def _w(g, oc=oc):
                if oc >= 2:
                    g.wait_ge(tS, p2_mm[oc - 2])
                qgp.dma(g, f"wp{oc % 2}", wpl_sb[oc % 2][:, :]
                        .rearrange("p (h q) -> p h q", q=128),
                        wp4[oc, :, :, :].rearrange("h p q -> p h q"))
            GP(_w)

        def p2_xld(oc):
            def _x(sy, oc=oc):
                if oc < 2:
                    sy.wait_ge(tS, p1b_done_pe[c.NH - 1])
                    sy.wait_ge(aS, p1b_otf[c.NH - 1])
                    sy.wait_ge(vS, p1b_rr[c.NH - 1])
                    sy.wait_ge(vS, p1b_otb[c.NH - 1])
                    qsc.w(sy, "otw")
                if oc >= 2:
                    sy.wait_ge(vS, p2_ev[(oc - 2, "res")])
                qsy.dma(sy, f"xr{oc % 2}", xrt_sb[oc % 2][:, :],
                        xT3[oc, :, 0:c.TPC])
            SY(_x)

        p2_wld(0)
        p2_wld(1)
        p2_xld(0)
        p2_xld(1)
        for oc in range(c.HC):
            def _pmm(t, oc=oc):
                pp = (psA, psB)[oc % 2]
                qgp.w(t, f"wp{oc % 2}")
                if oc == 0:
                    t.wait_ge(vS, p2_norm[c.NH - 1])
                if oc >= 2:
                    t.wait_ge(aS, p2_ev[(oc - 2, "copy")])
                for hc in range(c.NH):
                    for h0, hw in chunks(c.TPC):
                        MM(t, pp[:, h0:h0 + hw],
                           wpl_sb[oc % 2][:, hc * 128:(hc + 1) * 128],
                           otm_sb[:, hc * c.TPC + h0:hc * c.TPC + h0 + hw],
                           start=(hc == 0), stop=(hc == c.NH - 1))
                pemark()
                p2_mm[oc] = pe.n
            TE(_pmm)
            if oc + 2 < c.HC:
                p2_wld(oc + 2)

            def _pev(s, oc=oc):
                pp = (psA, psB)[oc % 2]
                es = oc % 2
                s.wait_ge(tS, p2_mm[oc])
                if oc >= 2:
                    qsc.w(s, f"prest{oc % 2}")
                act.c(s.activation(rch_sb[:, es * c.TPC:(es + 1) * c.TPC],
                                   pp[:, 0:c.TPC], AF.Identity,
                                   bias=bpc_s[:, oc:oc + 1]))
                p2_ev[(oc, "copy")] = act.n
            SC(_pev)

            def _pres(v, oc=oc):
                es = oc % 2
                rr = rch_sb[:, es * c.TPC:(es + 1) * c.TPC]
                v.wait_ge(aS, p2_ev[(oc, "copy")])
                qsy.w(v, f"xr{oc % 2}")
                dve.c(v.tensor_tensor(rr, rr, xrt_sb[oc % 2][:, :], ALU.add))
                p2_ev[(oc, "res")] = dve.n
            VE(_pres)
            if oc + 2 < c.HC:
                p2_xld(oc + 2)

            def _psq(s, oc=oc):
                es = oc % 2
                s.wait_ge(vS, p2_ev[(oc, "res")])
                act.c(s.activation(sqp_sb[:, :],
                                   rch_sb[:, es * c.TPC:(es + 1) * c.TPC],
                                   AF.Square))
                p2_ev[(oc, "sq")] = act.n
            SC(_psq)

            def _prow(t, oc=oc):
                es = oc % 2
                t.wait_ge(vS, p2_ev[(oc, "res")])
                for h0, hw in chunks(c.TPC):
                    MM(t, psC[0:1, h0:h0 + hw], onescf_s[:, :],
                       rch_sb[:, es * c.TPC + h0:es * c.TPC + h0 + hw],
                       start=(oc == 0), stop=(oc == c.HC - 1))
                t.wait_ge(aS, p2_ev[(oc, "sq")])
                for h0, hw in chunks(c.TPC):
                    MM(t, psD[0:1, h0:h0 + hw], onescf_s[:, :],
                       sqp_sb[:, h0:h0 + hw],
                       start=(oc == 0), stop=(oc == c.HC - 1))
                pemark()
                p2_row[oc] = pe.n
            TE(_prow)

            def _pst(s, oc=oc):
                es = oc % 2
                s.wait_ge(tS, p2_row[oc])
                qsc.dma(s, f"prest{oc % 2}", pres_d[oc, :, :],
                        rch_sb[:, es * c.TPC:(es + 1) * c.TPC])
            SC(_pst)

            dq_emit_some(1)

        lnr_s = sbin(st_p2, "lnr_s", [1, 8 * c.TPC], F32)

        def _l1r(v):
            v.wait_ge(tS, p2_row[c.HC - 1])
            dve.c(v.tensor_scalar(lnr_s[0:1, 0:c.TPC], psC[0:1, 0:c.TPC],
                                  1.0 / c.DM, None, ALU.mult))
            dve.c(v.tensor_scalar(lnr_s[0:1, c.TPC:2 * c.TPC],
                                  psD[0:1, 0:c.TPC], 1.0 / c.DM, None,
                                  ALU.mult))
            v.wait_ge(vS, dve.n)
            dve.c(v.tensor_tensor(lnr_s[0:1, 2 * c.TPC:3 * c.TPC],
                                  lnr_s[0:1, 0:c.TPC], lnr_s[0:1, 0:c.TPC],
                                  ALU.mult))
            v.wait_ge(vS, dve.n)
            dve.c(v.tensor_tensor(lnr_s[0:1, c.TPC:2 * c.TPC],
                                  lnr_s[0:1, c.TPC:2 * c.TPC],
                                  lnr_s[0:1, 2 * c.TPC:3 * c.TPC],
                                  ALU.subtract))
            v.wait_ge(vS, dve.n)
            dve.c(v.tensor_scalar(lnr_s[0:1, c.TPC:2 * c.TPC],
                                  lnr_s[0:1, c.TPC:2 * c.TPC], 1e-5, None,
                                  ALU.add))
            rt["l1var"] = dve.n
        VE(_l1r)

        def _l1s(s):
            s.wait_ge(vS, rt["l1var"])
            act.c(s.activation(lnr_s[0:1, 3 * c.TPC:4 * c.TPC],
                               lnr_s[0:1, c.TPC:2 * c.TPC], AF.Sqrt))
            rt["l1std"] = act.n
        SC(_l1s)

        def _l1v2(v):
            v.wait_ge(aS, rt["l1std"])
            dve.c(v.reciprocal(lnr_s[0:1, 4 * c.TPC:5 * c.TPC],
                               lnr_s[0:1, 3 * c.TPC:4 * c.TPC]))
            v.wait_ge(vS, dve.n)
            dve.c(v.tensor_tensor(lnr_s[0:1, 5 * c.TPC:6 * c.TPC],
                                  lnr_s[0:1, 0:c.TPC],
                                  lnr_s[0:1, 4 * c.TPC:5 * c.TPC], ALU.mult))
            v.wait_ge(vS, dve.n)
            dve.c(v.tensor_scalar(lnr_s[0:1, 5 * c.TPC:6 * c.TPC],
                                  lnr_s[0:1, 5 * c.TPC:6 * c.TPC], -1.0,
                                  None, ALU.mult))
            rt["l1ab"] = dve.n
        VE(_l1v2)

        def _l1b1(g):
            g.wait_ge(vS, rt["l1ab"])
            qgp.dma(g, "l1b", abrow_d[0:1, :], lnr_s[0:1, 4 * c.TPC:5 * c.TPC])
            qgp.dma(g, "l1b", abrow_d[2:3, :], lnr_s[0:1, 5 * c.TPC:6 * c.TPC])
            qgp.w(g, "l1b")
            qgp.dma(g, "l1b", ab_sb[0:1, 0:c.TPC], abrow_d[0:1, :])
            qgp.dma(g, "l1b", ab_sb[0:1, c.TPC:2 * c.TPC], abrow_d[2:3, :])
        GP(_l1b1)

        def _l1lo(v):
            qgp.w(v, "l1b")
            dve.c(v.tensor_tensor(lnr_s[0:1, 6 * c.TPC:7 * c.TPC],
                                  lnr_s[0:1, 4 * c.TPC:5 * c.TPC],
                                  ab_sb[0:1, 0:c.TPC], ALU.subtract))
            dve.c(v.tensor_tensor(lnr_s[0:1, 7 * c.TPC:8 * c.TPC],
                                  lnr_s[0:1, 5 * c.TPC:6 * c.TPC],
                                  ab_sb[0:1, c.TPC:2 * c.TPC], ALU.subtract))
            rt["l1lo"] = dve.n
        VE(_l1lo)

        def _l1b2(g):
            g.wait_ge(vS, rt["l1lo"])
            qgp.dma(g, "l1b", abrow_d[1:2, :], lnr_s[0:1, 6 * c.TPC:7 * c.TPC])
            qgp.dma(g, "l1b", abrow_d[3:4, :], lnr_s[0:1, 7 * c.TPC:8 * c.TPC])
            qgp.w(g, "l1b")
            qgp.dma(g, "l1b", ab_sb[:, 0:c.TPC], abrow_d[0:2, :])
            qgp.dma(g, "l1b", ab_sb[:, c.TPC:2 * c.TPC], abrow_d[2:4, :])
        GP(_l1b2)

        def _l1bc(t):
            qgp.w(t, "l1b")
            for h0, hw in chunks(c.TPC):
                MM(t, psA[:, h0:h0 + hw], ones22_s[:, :],
                   ab_sb[:, h0:h0 + hw], start=True, stop=True)
                MM(t, psB[:, h0:h0 + hw], ones22_s[:, :],
                   ab_sb[:, c.TPC + h0:c.TPC + h0 + hw],
                   start=True, stop=True)
            pemark()
            rt["l1bc"] = pe.n
        TE(_l1bc)

        p2b = {}
        for oc in range(c.HC):
            def _n2ld(sy, oc=oc):
                if oc < 2:
                    qsc.w(sy, f"prest{oc % 2}")
                else:
                    sy.wait_ge(vS, p2b[(oc - 2, "lo")])
                qsy.dma(sy, f"prl{oc % 2}",
                        rch_sb[:, (oc % 2) * c.TPC:(oc % 2 + 1) * c.TPC],
                        pres_d[oc, :, :])
            SY(_n2ld)

            def _n2v(v, oc=oc):
                es = oc % 2
                rr = rch_sb[:, es * c.TPC:(es + 1) * c.TPC]
                qsy.w(v, f"prl{oc % 2}")
                if oc == 0:
                    v.wait_ge(tS, rt["l1bc"])
                if oc >= 2:
                    qsc.w(v, f"xst{oc % 2}")
                dve.c(v.tensor_tensor(rr, rr, psA[:, 0:c.TPC], ALU.mult))
                v.wait_ge(vS, dve.n)
                dve.c(v.tensor_tensor(rr, rr, psB[:, 0:c.TPC], ALU.add))
                v.wait_ge(vS, dve.n)
                xh = xlh_sb[:, es * c.TPC:(es + 1) * c.TPC]
                xl = xll_sb[:, es * c.TPC:(es + 1) * c.TPC]
                dve.c(v.tensor_copy(xh, rr))
                v.wait_ge(vS, dve.n)
                p2b[(oc, "hi")] = dve.n
                dve.c(v.tensor_tensor(xl, rr, xh, ALU.subtract))
                p2b[(oc, "lo")] = dve.n
            VE(_n2v)

            def _n2st(s, oc=oc):
                es = oc % 2
                xh = xlh_sb[:, es * c.TPC:(es + 1) * c.TPC]
                xl = xll_sb[:, es * c.TPC:(es + 1) * c.TPC]
                s.wait_ge(vS, p2b[(oc, "lo")])
                qsc.dma(s, f"xst{oc % 2}", xlnh_d[oc * 128:(oc + 1) * 128, :],
                        xh)
                qsc.dma(s, f"xst{oc % 2}", xlnl_d[oc * 128:(oc + 1) * 128, :],
                        xl)
            SC(_n2st)
            dq_emit_some(1)

        if os.environ.get("MK_UPTO") == "p2":
            def _fin0(sy):
                qsc.w(sy, "xst0")
                qsc.w(sy, "xst1")
            SY(_fin0)
            st_p2.close()
            st_dq.close()
            return nc

        nc.leave_named_scope("p2", _sc4[0], False)
        _sc5 = nc.enter_named_scope("p3a", False)
        st_p2.close()
        st_p3a = contextlib.ExitStack()
        xf_sb = sbin(st_p3a, "xf_sb", [128, c.HC * c.TPC], BF16)
        wgl_sb = [sbin(st_p3a, f"wgl{i}", [128, c.HC * 128], BF16)
                  for i in range(2)]
        wul_sb = [sbin(st_p3a, f"wul{i}", [128, c.HC * 128], BF16)
                  for i in range(2)]
        gu_sb = sbin(st_p3a, "gu_sb", [128, c.TPC], BF16)
        int_sb = [sbin(st_p3a, f"int{i}", [128, c.TPC], BF16)
                  for i in range(2)]

        xlnv = xlnh_d.rearrange("(h p) t -> h p t", p=128)
        p3_gu = {}
        p3_int = {}
        pgS = sem("pgS")

        def _pgate(sy):
            qsc.w(sy, "xst0")
            qsc.w(sy, "xst1")
            qsc.w(sy, "prest0")
            qsc.w(sy, "prest1")
            qsy.w(sy, "prl0")
            qsy.w(sy, "prl1")
            qsy.w(sy, "xr0")
            qsy.w(sy, "xr1")
            qgp.w(sy, "l1b")
            qgp.w(sy, "p2ld")
            qgp.w(sy, "rr0")
            qgp.w(sy, "rr1")
            qgp.w(sy, "wp0")
            qgp.w(sy, "wp1")
            sy.wait_ge(vS, p2b[(c.HC - 1, "lo")])
            sy.wait_ge(aS, rt["l1std"])
            sy.wait_ge(tS, rt["l1bc"])
            sy.nop().then_inc(pgS, 1)
        SY(_pgate)

        def _p3x(sy):
            sy.wait_ge(pgS, 1)
            nq4 = c.HC // 4
            for q4 in range(4):
                qsy.dma(sy, "p3x", xf_sb[:, q4 * nq4 * c.TPC:
                                         (q4 + 1) * nq4 * c.TPC]
                        .rearrange("p (h t) -> p h t", t=c.TPC),
                        xlnv[q4 * nq4:(q4 + 1) * nq4, :, :]
                        .rearrange("h p t -> p h t"))
        SY(_p3x)

        def p3_guload(j):
            def _w(s, j=j):
                sl = j % 2
                if j < 2:
                    s.wait_ge(pgS, 1)
                if j >= 2:
                    s.wait_ge(tS, p3_gu[j - 2])
                dq_wait_spilled(s, (min(j // 8, c.NCGU - 1) + 1)
                                * 2 * c.HC - 1)
                nq4 = c.HC // 4
                for q4 in range(4):
                    qsc.dma(s, f"wgu{sl}", wgl_sb[sl][:, q4 * nq4 * 128:
                                                      (q4 + 1) * nq4 * 128]
                            .rearrange("p (h q) -> p h q", q=128),
                            wg_deq[:, j, q4 * nq4:(q4 + 1) * nq4, :])
                    qsc.dma(s, f"wgu{sl}", wul_sb[sl][:, q4 * nq4 * 128:
                                                      (q4 + 1) * nq4 * 128]
                            .rearrange("p (h q) -> p h q", q=128),
                            wu_deq[:, j, q4 * nq4:(q4 + 1) * nq4, :])
            SC(_w)

        p3_guload(0)
        p3_guload(1)
        for j in range(c.JC):
            def _gu(t, j=j):
                sl = j % 2
                psG = (psA, psB)[j % 2]
                psU = (psC, psD)[j % 2]
                qsc.w(t, f"wgu{sl}")
                if j == 0:
                    qsy.w(t, "p3x")
                    t.wait_ge(tS, rt["l1bc"])
                    t.wait_ge(vS, p2b[(c.HC - 1, "lo")])
                if j >= 2:
                    t.wait_ge(vS, p3_int[j - 2])
                for hc in range(c.HC):
                    lg = wgl_sb[sl][:, hc * 128:(hc + 1) * 128]
                    lu = wul_sb[sl][:, hc * 128:(hc + 1) * 128]
                    for h0, hw in chunks(c.TPC):
                        xr_ = xf_sb[:, hc * c.TPC + h0:hc * c.TPC + h0 + hw]
                        MM(t, psG[:, h0:h0 + hw], lg, xr_,
                           start=(hc == 0), stop=(hc == c.HC - 1))
                        MM(t, psU[:, h0:h0 + hw], lu, xr_,
                           start=(hc == 0), stop=(hc == c.HC - 1))
                pemark()
                p3_gu[j] = pe.n
            TE(_gu)
            if j + 2 < c.JC:
                p3_guload(j + 2)

            def _silu(s, j=j):
                psG = (psA, psB)[j % 2]
                s.wait_ge(tS, p3_gu[j])
                if j >= 1:
                    s.wait_ge(vS, p3_int[j - 1])
                act.c(s.activation(gu_sb[:, :], psG[:, 0:c.TPC], AF.Silu))
                p3_gu[(j, "silu")] = act.n
            SC(_silu)

            def _mul(v, j=j):
                psU = (psC, psD)[j % 2]
                v.wait_ge(aS, p3_gu[(j, "silu")])
                if j >= 2:
                    qsy.w(v, f"ist{j % 2}")
                dve.c(v.tensor_tensor(int_sb[j % 2][:, :], gu_sb[:, :],
                                      psU[:, 0:c.TPC], ALU.mult))
                p3_int[j] = dve.n
            VE(_mul)

            def _ist(sy, j=j):
                sy.wait_ge(vS, p3_int[j])
                qsy.dma(sy, f"ist{j % 2}", inter_d[j, :, :],
                        int_sb[j % 2][:, :])
            SY(_ist)
            dq_emit_some(12)

        if os.environ.get("MK_UPTO") == "p3a":
            def _fin0(sy):
                qsy.w(sy, "ist0")
                qsy.w(sy, "ist1")
            SY(_fin0)
            st_p3a.close()
            st_dq.close()
            return nc

        nc.leave_named_scope("p3a", _sc5[0], False)
        _sc6 = nc.enter_named_scope("p3b", False)
        st_p3a.close()
        st_p3b = contextlib.ExitStack()
        JH = c.JC // 2
        wdl_sb = [sbin(st_p3b, f"wdl{i}", [128, JH * 512], BF16)
                  for i in range(2)]
        it_sb = [sbin(st_p3b, f"it{i}", [128, c.JC * 128], BF16)
                 for i in range(2)]
        yev_sb = [sbin(st_p3b, f"yev{i}", [128, 512], F32) for i in range(2)]
        l2a_sb = sbin(st_p3b, "l2a_sb", [128, c.DM], F32)
        DH = c.DM // 2
        l2h_sb = sbin(st_p3b, "l2h_sb", [128, DH], BF16)

        NDB = c.DM // 512
        p3_dn = {}
        p3_ev = {}
        p3_ln = {}

        def p3_wdload(db, half):
            def _w(s, db=db, half=half):
                sl = half
                if db >= 1:
                    s.wait_ge(tS, p3_dn[(db - 1, c.CPT - 1)])
                if db == 0:
                    qsy.w(s, "ist0")
                    qsy.w(s, "ist1")
                dq_wait_spilled(s, NGU_T + (db // 2 + 1) * c.JC - 1)
                j0 = half * JH
                qsc.dma(s, f"wdq{sl}", wdl_sb[sl][:, :]
                        .rearrange("p (j q) -> p j q", q=512),
                        wd_deq[j0:j0 + JH, :, db * 512:(db + 1) * 512]
                        .rearrange("j p q -> p j q"))
            SC(_w)

        def p3_itload(db, tc):
            def _i(sy, db=db, tc=tc):
                idx = db * c.CPT + tc
                if idx >= 2:
                    pdb, ptc = (idx - 2) // c.CPT, (idx - 2) % c.CPT
                    sy.wait_ge(tS, p3_dn[(pdb, ptc)])
                if idx == 0:
                    qsy.w(sy, "ist0")
                    qsy.w(sy, "ist1")
                qsy.dma(sy, f"it{idx % 2}", it_sb[idx % 2][:, :]
                        .rearrange("p (j q) -> p j q", q=128),
                        inter_d[:, :, tc * 128:(tc + 1) * 128]
                        .rearrange("j p q -> p j q"))
            SY(_i)

        p3_wdload(0, 0)
        p3_wdload(0, 1)
        for pre in range(min(2, NDB * c.CPT)):
            p3_itload(pre // c.CPT, pre % c.CPT)
        for db in range(NDB):
            for tc in range(c.CPT):
                idx = db * c.CPT + tc
                slot = idx % 4
                pp = (psA, psB)[slot // 2]
                pcol = (slot % 2) * 512

                def _dn(t, db=db, tc=tc, idx=idx, pp=pp, pcol=pcol):
                    if tc == 0:
                        qsc.w(t, "wdq0")
                        qsc.w(t, "wdq1")
                    qsy.w(t, f"it{idx % 2}")
                    if idx >= 4:
                        t.wait_ge(aS, p3_ev[((idx - 4) // c.CPT,
                                            (idx - 4) % c.CPT)])
                    for j in range(c.JC):
                        half, jl = divmod(j, JH)
                        lh = it_sb[idx % 2][:, j * 128:(j + 1) * 128]
                        MM(t, pp[:, pcol:pcol + 512], lh,
                           wdl_sb[half][:, jl * 512:(jl + 1) * 512],
                           start=(j == 0), stop=(j == c.JC - 1))
                    pemark()
                    p3_dn[(db, tc)] = pe.n
                TE(_dn)
                if idx + 2 < NDB * c.CPT:
                    p3_itload((idx + 2) // c.CPT, (idx + 2) % c.CPT)

                def _ev(s, db=db, tc=tc, idx=idx, pp=pp, pcol=pcol):
                    es = idx % 2
                    s.wait_ge(tS, p3_dn[(db, tc)])
                    if idx >= 2:
                        qsc.w(s, f"ydw{idx % 2}")
                    act.c(s.copy(yev_sb[es][:, :], pp[:, pcol:pcol + 512]))
                    p3_ev[(db, tc)] = act.n
                    s.wait_ge(aS, p3_ev[(db, tc)])
                    qsc.dma(s, f"ydw{idx % 2}",
                            yd[tc * 128:(tc + 1) * 128,
                               db * 512:(db + 1) * 512], yev_sb[es][:, :])
                SC(_ev)
                dq_emit_some(6)
            if db + 1 < NDB:
                p3_wdload(db + 1, 0)
                p3_wdload(db + 1, 1)

        def emit_ln2(w2):
            def _l2ld(sy, w2=w2):
                if w2 == 0:
                    qsc.w(sy, "ydw0")
                    qsc.w(sy, "ydw1")
                if w2 >= 1:
                    qsy.w(sy, "yst")
                qsy.dma(sy, "l2ld", l2a_sb[:, :],
                        yd[w2 * 128:(w2 + 1) * 128, :])
                qsy.dmat(sy, "l2ld", l2h_sb[:, :],
                         xlnh_d[0:DH, w2 * 128:(w2 + 1) * 128])
            SY(_l2ld)

            rsrc = [(xlnh_d, 1), (xlnl_d, 0), (xlnl_d, 1)]
            for rnd in range(4):
                def _l2a0(v, w2=w2, rnd=rnd):
                    qsy.w(v, "l2ld")
                    hf = rnd % 2
                    dve.c(v.tensor_tensor(l2a_sb[:, hf * DH:(hf + 1) * DH],
                                          l2a_sb[:, hf * DH:(hf + 1) * DH],
                                          l2h_sb[:, :], ALU.add))
                    p3_ln[(w2, "add", rnd)] = dve.n
                VE(_l2a0)
                if rnd < 3:
                    def _l2ld2(sy, w2=w2, rnd=rnd):
                        sy.wait_ge(vS, p3_ln[(w2, "add", rnd)])
                        src, hf = rsrc[rnd]
                        qsy.dmat(sy, "l2ld", l2h_sb[:, :],
                                 src[hf * DH:(hf + 1) * DH,
                                     w2 * 128:(w2 + 1) * 128])
                    SY(_l2ld2)

            def _l2a(v, w2=w2):
                v.wait_ge(vS, p3_ln[(w2, "add", 3)])
                dve.c(v.tensor_reduce(rows_s[:, 0:1], l2a_sb[:, :], AX,
                                      ALU.add))
                p3_ln[(w2, "sum")] = dve.n
            VE(_l2a)

            def _l2sq(s, w2=w2):
                s.wait_ge(vS, p3_ln[(w2, "sum")])
                act.c(s.activation(l2h_sb[:, :], l2a_sb[:, 0:DH], AF.Square,
                                   accum_out=rows_s[:, 1:2]))
                s.wait_ge(aS, act.n)
                act.c(s.activation(l2h_sb[:, :], l2a_sb[:, DH:c.DM],
                                   AF.Square, accum_out=rows_s[:, 9:10]))
                p3_ln[(w2, "sq")] = act.n
            SC(_l2sq)

            def _l2v0(v, w2=w2):
                v.wait_ge(aS, p3_ln[(w2, "sq")])
                dve.c(v.tensor_tensor(rows_s[:, 1:2], rows_s[:, 1:2],
                                      rows_s[:, 9:10], ALU.add))
                v.wait_ge(vS, dve.n)
                dve.c(v.tensor_scalar(rows_s[:, 2:3], rows_s[:, 0:1],
                                      1.0 / c.DM, None, ALU.mult))
                v.wait_ge(vS, dve.n)
                dve.c(v.tensor_tensor(rows_s[:, 3:4], rows_s[:, 2:3],
                                      rows_s[:, 2:3], ALU.mult))
                v.wait_ge(vS, dve.n)
                dve.c(v.tensor_scalar(rows_s[:, 4:5], rows_s[:, 1:2],
                                      1.0 / c.DM, None, ALU.mult))
                v.wait_ge(vS, dve.n)
                dve.c(v.tensor_tensor(rows_s[:, 5:6], rows_s[:, 4:5],
                                      rows_s[:, 3:4], ALU.subtract))
                v.wait_ge(vS, dve.n)
                dve.c(v.tensor_scalar(rows_s[:, 5:6], rows_s[:, 5:6], 1e-5,
                                      None, ALU.add))
                p3_ln[(w2, "var")] = dve.n
            VE(_l2v0)

            def _l2s(s, w2=w2):
                s.wait_ge(vS, p3_ln[(w2, "var")])
                act.c(s.activation(rows_s[:, 6:7], rows_s[:, 5:6], AF.Sqrt))
                p3_ln[(w2, "std")] = act.n
            SC(_l2s)

            def _l2ap(v, w2=w2):
                v.wait_ge(aS, p3_ln[(w2, "std")])
                dve.c(v.reciprocal(rows_s[:, 7:8], rows_s[:, 6:7]))
                v.wait_ge(vS, dve.n)
                dve.c(v.tensor_scalar(l2a_sb[:, :], l2a_sb[:, :],
                                      rows_s[:, 2:3], rows_s[:, 7:8],
                                      ALU.subtract, ALU.mult))
                p3_ln[(w2, "apply")] = dve.n
            VE(_l2ap)

            def _l2y(sy, w2=w2):
                sy.wait_ge(vS, p3_ln[(w2, "apply")])
                qsy.dma(sy, "yst", y[w2 * 128:(w2 + 1) * 128, :],
                        l2a_sb[:, :])
            SY(_l2y)

        for w2 in range(c.CPT):
            emit_ln2(w2)

        nc.leave_named_scope("p3b", _sc6[0], False)
        def _fin(sy):
            qsy.w(sy, "yst")
        SY(_fin)
        st_p3b.close()
        st_dq.close()

    return nc


_CACHE = {}


def kernel(L0_attn_hidden_states, L0_attn_w_attn, L0_attn_b_attn,
           L0_attn_w_proj, L0_attn_b_proj, L0_ffn_gate_codes,
           L0_ffn_gate_scales, L0_ffn_up_codes, L0_ffn_up_scales,
           L0_ffn_down_codes, L0_ffn_down_scales):
    cfg = Cfg()
    in_maps = stage_inputs(
        cfg, L0_attn_hidden_states, L0_attn_w_attn, L0_attn_b_attn,
        L0_attn_w_proj, L0_attn_b_proj, L0_ffn_gate_codes, L0_ffn_gate_scales,
        L0_ffn_up_codes, L0_ffn_up_scales, L0_ffn_down_codes,
        L0_ffn_down_scales)
    if "nc" not in _CACHE:
        _CACHE["nc"] = build_program(cfg)
    nc = _CACHE["nc"]
    res = run_bass_kernel_spmd(nc, in_maps, core_ids=list(range(cfg.NC)))
    out = np.concatenate([r["y"] for r in res.results], axis=0)
    return out.reshape(cfg.B, cfg.S, cfg.DM).astype(np.float32)


# version 12 (final)
# speedup vs baseline: 1.6107x; 1.2061x over previous
import contextlib
import math
import os

import numpy as np
import ml_dtypes

import concourse.bass as bass
import concourse.mybir as mybir
from concourse.bass_utils import run_bass_kernel_spmd

F32 = mybir.dt.float32
BF16 = mybir.dt.bfloat16
F16 = mybir.dt.float16
I32 = mybir.dt.int32
U8 = mybir.dt.uint8
ALU = mybir.AluOpType
AF = mybir.ActivationFunctionType
AX = mybir.AxisListType.X
NPBF = ml_dtypes.bfloat16

NF4 = np.array([
    -1.0, -0.6961928009986877, -0.5250730514526367, -0.39491748809814453,
    -0.28444138169288635, -0.18477343022823334, -0.09105003625154495, 0.0,
    0.07958029955625534, 0.16093020141124725, 0.24611230194568634,
    0.33791524171829224, 0.44070982933044434, 0.5626170039176941,
    0.7229568362236023, 1.0], dtype=np.float64)

DQ_CA = 4.5
DQ_CB = 10.5
NFG = (NF4[1:] - NF4[:-1])
NFDA = (NF4[:-1] - NFG * (np.arange(15) - DQ_CA))
NFDB = (NF4[:-1] - NFG * (np.arange(15) - DQ_CB))
NFG = [float(v) for v in NFG]
NFDA = [float(v) for v in NFDA]
NFDB = [float(v) for v in NFDB]

DEAD = -60000.0


def alibi_slopes(n):
    cp2 = 2 ** math.floor(math.log2(n))
    base = 2 ** (-(2 ** (-(math.log2(cp2) - 3))))
    slopes = base ** np.arange(1, cp2 + 1, dtype=np.float32)
    if cp2 != n:
        eb = 2 ** (-(2 ** (-(math.log2(2 * cp2) - 3))))
        extra = eb ** np.arange(1, 1 + 2 * min(cp2, n - cp2), 2,
                                dtype=np.float32)
        slopes = np.concatenate([slopes, extra])
    return slopes.astype(np.float32)


class Cfg:
    def __init__(self, NH=32, B=4, S=2048, I=11008):
        self.NC = 8
        self.D = 128
        self.NH = NH
        self.DM = NH * self.D
        self.HC = self.DM // 128
        self.B = B
        self.S = S
        self.TB = S
        self.TPC = S // 2
        self.KC = S // 128
        self.KH = self.KC // 2
        self.CC = 3 * NH
        self.CCH = 6
        self.NPASS = self.CC // self.CCH
        self.TSUP = min(512, S)
        self.NTS = self.TB // self.TSUP
        self.I = I
        self.JC = I // 128
        self.SCALE = 1.0 / self.D
        self.DQW = 1024
        self.NCGU = -(-I // self.DQW)
        self.NCD = self.DM // self.DQW
        self.CPT = self.TPC // 128


def stage_inputs(cfg, x, w_attn, b_attn, w_proj, b_proj,
                 g_codes, g_scales, u_codes, u_scales, d_codes, d_scales):
    c = cfg
    slopes = alibi_slopes(c.NH)
    x = np.asarray(x, dtype=np.float32)
    w_attn = np.asarray(w_attn, dtype=np.float32)
    b_attn = np.asarray(b_attn, dtype=np.float32)
    w_proj = np.asarray(w_proj, dtype=np.float32)
    b_proj = np.asarray(b_proj, dtype=np.float32)

    wq = np.concatenate(
        [w_attn[:, (g * c.DM + h * 128):(g * c.DM + (h + 1) * 128)]
         for h in range(c.NH) for g in range(3)], axis=1)
    wq3 = np.ascontiguousarray(wq).reshape(c.HC, 128, c.CC * 128)
    bqcols = np.stack(
        [b_attn[g * c.DM + h * 128:g * c.DM + (h + 1) * 128]
         for h in range(c.NH) for g in range(3)], axis=1)
    bq = np.ascontiguousarray(bqcols.astype(np.float32))

    wp4 = np.ascontiguousarray(
        w_proj.reshape(c.HC, 128, c.HC, 128).transpose(2, 0, 1, 3))
    bpc = np.ascontiguousarray(b_proj.reshape(c.HC, 128).T)

    gT = np.ascontiguousarray(np.asarray(g_codes).T.astype(np.uint8))
    uT = np.ascontiguousarray(np.asarray(u_codes).T.astype(np.uint8))
    dTr = np.ascontiguousarray(np.asarray(d_codes).T.astype(np.uint8))
    gcT = gT.reshape(c.HC, 128, c.I)
    ucT = uT.reshape(c.HC, 128, c.I)
    dcT = dTr.reshape(c.JC, 128, c.DM)
    gsT = np.ascontiguousarray(np.asarray(g_scales).T)
    usT = np.ascontiguousarray(np.asarray(u_scales).T)
    dsT = np.ascontiguousarray(np.asarray(d_scales).T)

    common = {
        "wq3": wq3, "bq": bq, "wp4": wp4, "bpc": bpc,
        "ones22": np.ones((2, 128), dtype=NPBF),
        "onesc": np.ones((128, 1), dtype=NPBF),
        "onesf": np.ones((128, 1), dtype=np.float32),
        "masku": np.ascontiguousarray(
            np.where(np.arange(128)[:, None] > np.arange(128)[None, :],
                     np.float32(-1e30), np.float32(0.0)).astype(np.float32)),
        "gcT": gcT, "ucT": ucT, "dcT": dcT,
        "gsT": gsT, "usT": usT, "dsT": dsT,
    }

    in_maps = []
    for r in range(c.NC):
        b, h = r // 2, r % 2
        m = dict(common)
        xb = x[b]
        xT = np.ascontiguousarray(xb.T)
        mine = xT[:, h * c.TPC:(h + 1) * c.TPC]
        othr = xT[:, (1 - h) * c.TPC:(2 - h) * c.TPC]
        m["xT3"] = np.ascontiguousarray(
            np.concatenate([mine, othr], axis=1)).reshape(c.HC, 128, c.TB)
        ak = np.zeros((128, c.NH * c.KC), np.float32)
        p = np.arange(128, dtype=np.float32)
        for hh in range(c.NH):
            sl = slopes[hh]
            for ci in range(c.KC):
                col = hh * c.KC + ci
                if ci < c.KH:
                    kg = h * c.TPC + ci * 128 + p
                    ak[:, col] = sl * kg
                elif h == 1:
                    kg = (ci - c.KH) * 128 + p
                    ak[:, col] = sl * kg
                else:
                    ak[:, col] = DEAD
        m["alik"] = ak
        aq = np.zeros((c.NH, c.TPC), np.float32)
        qg = h * c.TPC + np.arange(c.TPC, dtype=np.float32)
        for hh in range(c.NH):
            aq[hh] = -float(c.D) * slopes[hh] * qg
        m["aliqm"] = np.ascontiguousarray(aq)
        in_maps.append(m)
    return in_maps


class Q:
    def __init__(self, mksem, prefix):
        self.mksem = mksem
        self.prefix = prefix
        self.streams = {}

    def _s(self, stream):
        if stream not in self.streams:
            self.streams[stream] = [self.mksem(f"{self.prefix}_{stream}"), 0]
        return self.streams[stream]

    def dma(self, eng, stream, out, in_, **kw):
        s = self._s(stream)
        eng.dma_start(out=out, in_=in_, **kw).then_inc(s[0], 16)
        s[1] += 16
        return s[1]

    def dmat(self, eng, stream, out, in_):
        s = self._s(stream)
        eng.dma_start(out=out, in_=in_, transpose=True).then_inc(s[0], 16)
        s[1] += 16
        return s[1]

    def tot(self, stream):
        return self._s(stream)

    def w(self, eng, stream):
        s = self._s(stream)
        eng.wait_ge(s[0], s[1])


class E:
    def __init__(self, sem):
        self.sem = sem
        self.n = 0

    def c(self, inst):
        inst.then_inc(self.sem, 1)
        self.n += 1
        return self.n


def chunks(total, w=512):
    return [(i, min(w, total - i)) for i in range(0, total, w)]


def bpieces(lo, hi):
    out = []
    while lo < hi:
        nxt = min(hi, (lo // 512 + 1) * 512)
        out.append((lo, nxt - lo))
        lo = nxt
    return out


def build_program(cfg):
    c = cfg
    nc = bass.Bass("TRN2", target_bir_lowering=False)
    dt = nc.dram_tensor

    xT3 = dt("xT3", [c.HC, 128, c.TB], F32, kind="ExternalInput")
    wq3 = dt("wq3", [c.HC, 128, c.CC * 128], F32, kind="ExternalInput")
    bq = dt("bq", [128, c.CC], F32, kind="ExternalInput")
    aliqm = dt("aliqm", [c.NH, c.TPC], F32, kind="ExternalInput")
    alik = dt("alik", [128, c.NH * c.KC], F32, kind="ExternalInput")
    masku = dt("masku", [128, 128], F32, kind="ExternalInput")
    ones22 = dt("ones22", [2, 128], BF16, kind="ExternalInput")
    onesc = dt("onesc", [128, 1], BF16, kind="ExternalInput")
    onesf = dt("onesf", [128, 1], F32, kind="ExternalInput")
    wp4 = dt("wp4", [c.HC, c.HC, 128, 128], F32, kind="ExternalInput")
    bpc = dt("bpc", [128, c.HC], F32, kind="ExternalInput")
    gcT = dt("gcT", [c.HC, 128, c.I], U8, kind="ExternalInput")
    ucT = dt("ucT", [c.HC, 128, c.I], U8, kind="ExternalInput")
    dcT = dt("dcT", [c.JC, 128, c.DM], U8, kind="ExternalInput")
    gsT = dt("gsT", [c.DM // 64, c.I], F32, kind="ExternalInput")
    usT = dt("usT", [c.DM // 64, c.I], F32, kind="ExternalInput")
    dsT = dt("dsT", [c.I // 64, c.DM], F32, kind="ExternalInput")
    y = dt("y", [c.TPC, c.DM], F32, kind="ExternalOutput")

    xbf_d = dt("xbf_d", [c.HC, 128, c.TB], BF16)
    qkv_sp = dt("qkv_sp", [c.CC * 128, c.TB], BF16)
    q2k2_d = dt("q2k2_d", [2 * c.NH, c.TB], F32)
    frow_d = dt("frow_d", [2, c.TPC], BF16)
    ot_d = dt("ot_d", [c.NH, 128, c.TPC], BF16)
    rrl_d = dt("rrl_d", [2 * c.NH, c.TPC], BF16)
    pres_d = dt("pres_d", [c.HC, 128, c.TPC], F32)
    xlnh_d = dt("xlnh_d", [c.DM, c.TPC], BF16)
    xlnl_d = dt("xlnl_d", [c.DM, c.TPC], BF16)
    abrow_d = dt("abrow_d", [4, c.TPC], BF16)
    wg_deq = dt("wg_deq", [128, c.JC, c.HC, 128], BF16)
    wu_deq = dt("wu_deq", [128, c.JC, c.HC, 128], BF16)
    wd_deq = dt("wd_deq", [c.JC, 128, c.DM], BF16)
    inter_d = dt("inter_d", [c.JC, 128, c.TPC], BF16)
    yd = dt("yd", [c.TPC, c.DM], F32)

    st = contextlib.ExitStack()
    with st:
        def sbin(stack, name, shape, dtt):
            return stack.enter_context(nc.sbuf_tensor(name, shape, dtt))

        def sb(name, shape, dtt):
            return st.enter_context(nc.sbuf_tensor(name, shape, dtt))

        def psum(name, shape, dtt=F32):
            return st.enter_context(nc.psum_tensor(name, shape, dtt))

        def sem(name):
            return st.enter_context(nc.semaphore(name))

        ones22_s = sb("ones22_s", [2, 128], BF16)
        onesc_s = sb("onesc_s", [128, 1], BF16)
        onescf_s = sb("onescf_s", [128, 1], F32)
        masku_s = sb("masku_s", [128, 128], F32)
        rows_s = sb("rows_s", [128, 16], F32)
        bq_s = sb("bq_s", [128, c.CC], F32)
        alik_s = sb("alik_s", [128, c.NH * c.KC], F32)
        aliq_sb = sb("aliq_sb", [c.NH, c.TPC], F32)
        bpc_s = sb("bpc_s", [128, c.HC], F32)
        rwm = sb("rwm", [c.NH, 1], F32)
        rwf = sb("rwf", [c.NH, c.TB], F32)
        rwfh = sb("rwfh", [c.NH, c.TPC], BF16)
        rwfl = sb("rwfl", [c.NH, c.TPC], BF16)
        q2st = sb("q2st", [1, 2 * c.TSUP], F32)
        psA = psum("psA", [128, 1024])
        psB = psum("psB", [128, 1024])
        psC = psum("psC", [128, 1024])
        psD = psum("psD", [128, 1024])

        tS = sem("tS")
        aS = sem("aS")
        vS = sem("vS")

        block = st.enter_context(nc.Block())
        qsy = Q(sem, "sy")
        qsc = Q(sem, "sc")
        qgp = Q(sem, "gp")
        pe = E(tS)
        act = E(aS)
        dve = E(vS)

        SY, SC, GP, VE, TE = (block.sync, block.scalar, block.gpsimd,
                              block.vector, block.tensor)
        rt = {}
        last_mm = [None]

        def MM(t, *a, **kw):
            last_mm[0] = t.matmul(*a, **kw)
            return last_mm[0]

        def pemark():
            return pe.c(last_mm[0])

        DQW = c.DQW
        st_dq = contextlib.ExitStack()
        dq_c = sbin(st_dq, "dq_c", [128, 2 * DQW], F16)
        dq_ca = sbin(st_dq, "dq_ca", [128, DQW], F16)
        dq_cb = sbin(st_dq, "dq_cb", [128, DQW], F16)
        dq_m = sbin(st_dq, "dq_m", [128, DQW], F16)
        dq_x = sbin(st_dq, "dq_x", [128, DQW], F16)
        dq_l7 = sbin(st_dq, "dq_l7", [128, DQW], F16)
        dq_t = sbin(st_dq, "dq_t", [128, DQW], F16)
        dq_s = sbin(st_dq, "dq_s", [128, 2 * DQW], F16)
        dq_w = sbin(st_dq, "dq_w", [128, 2 * DQW], BF16)

        dq_tiles = []
        for colt in range(c.NCGU):
            col = colt * DQW
            w = min(DQW, c.I - col)
            for mat in ("g", "u"):
                for hc in range(c.HC):
                    dq_tiles.append((mat, hc, col, w))
        for colt in range(c.NCD):
            for jc in range(c.JC):
                dq_tiles.append(("d", jc, colt * DQW, DQW))
        NDQ = len(dq_tiles)
        NGU_T = 2 * c.HC * c.NCGU
        dq_vd = {}
        dq_spd = {}
        dq_bl = {}
        dq_bcons = {}
        dq_next = [0]

        def dq_emit_one(i):
            mat, a, col, w = dq_tiles[i]
            cs = i % 2
            cw = slice(cs * DQW, cs * DQW + w)
            ww = slice(0, w)

            def _ld(g):
                if i >= 2:
                    g.wait_ge(vS, dq_vd[i - 2])
                src, srows = {"g": (gcT, gsT), "u": (ucT, usT),
                              "d": (dcT, dsT)}[mat]
                qgp.dma(g, f"dqc{i % 2}", dq_c[:, cw], src[a, :, col:col + w])
                qgp.dma(g, f"dqc{i % 2}", dq_s[0:64, cw],
                        srows[2 * a:2 * a + 1, col:col + w]
                        .partition_broadcast(64))
                qgp.dma(g, f"dqc{i % 2}", dq_s[64:128, cw],
                        srows[2 * a + 1:2 * a + 2, col:col + w]
                        .partition_broadcast(64))
            GP(_ld)

            def _tree(v):
                qgp.w(v, f"dqc{i % 2}")
                if i >= 2:
                    (qsy, qsc)[i % 2].w(v, f"dqw{i % 2}")
                cc_ = dq_c[:, cw]
                ca_, cb_ = dq_ca[:, ww], dq_cb[:, ww]
                m_, x_ = dq_m[:, ww], dq_x[:, ww]
                tA, tB = dq_t[:, ww], dq_l7[:, ww]
                mk = {}
                mk["ca"] = dve.c(v.tensor_scalar(ca_, cc_, -DQ_CA, None,
                                                 ALU.add))
                mk["cb"] = dve.c(v.tensor_scalar(cb_, cc_, -DQ_CB, None,
                                                 ALU.add))
                v.wait_ge(vS, mk["ca"])
                mk["mn0"] = dve.c(v.tensor_scalar(m_, ca_, NFG[0], NFDA[0],
                                                  ALU.mult, ALU.add))
                v.wait_ge(vS, mk["cb"])
                mk["mx8"] = dve.c(v.tensor_scalar(x_, cb_, NFG[8], NFDB[8],
                                                  ALU.mult, ALU.add))
                mk["mn-1"] = mk["mn0"]
                mk["mx-1"] = mk["mx8"]
                bsegs = (9, 10, 11, 12, 13, 14, 7)
                for z, i2 in enumerate(range(1, 8)):
                    j2 = bsegs[z]
                    ga, da = NFG[i2], NFDA[i2]
                    gb, db_ = NFG[j2], NFDB[j2]
                    if z >= 1:
                        v.wait_ge(vS, mk[f"mn{z - 1}"])
                    mk[f"la{z}"] = dve.c(v.tensor_scalar(tA, ca_, ga, da,
                                                         ALU.mult, ALU.add))
                    if z >= 1:
                        v.wait_ge(vS, mk[f"mx{z - 1}"])
                    mk[f"lb{z}"] = dve.c(v.tensor_scalar(tB, cb_, gb, db_,
                                                         ALU.mult, ALU.add))
                    v.wait_ge(vS, mk[f"la{z}"])
                    mk[f"mn{z}"] = dve.c(v.tensor_tensor(m_, m_, tA, ALU.min))
                    v.wait_ge(vS, mk[f"lb{z}"])
                    mk[f"mx{z}"] = dve.c(v.tensor_tensor(x_, x_, tB, ALU.max))
                v.wait_ge(vS, mk["mn6"])
                mk["l7a"] = dve.c(v.tensor_scalar(tA, ca_, NFG[7], NFDA[7],
                                                  ALU.mult, ALU.add))
                v.wait_ge(vS, mk["mx6"])
                dve.c(v.tensor_tensor(m_, m_, x_, ALU.add))
                v.wait_ge(vS, dve.n)
                v.wait_ge(vS, mk["l7a"])
                dve.c(v.tensor_tensor(m_, m_, tA, ALU.subtract))
                v.wait_ge(vS, dve.n)
                dve.c(v.tensor_tensor(dq_w[:, cw], m_, dq_s[:, cw], ALU.mult))
                dq_vd[i] = dve.n
            VE(_tree)

            def _sp(e):
                qq = (qsy, qsc)[i % 2]
                e.wait_ge(vS, dq_vd[i])
                if mat in ("g", "u"):
                    dst = wg_deq if mat == "g" else wu_deq
                    tot = qq.dma(e, f"dqw{i % 2}",
                                 dst[:, col // 128:(col + w) // 128, a, :],
                                 dq_w[:, cs * DQW:cs * DQW + w]
                                 .rearrange("p (j q) -> p j q", q=128))
                else:
                    tot = qq.dma(e, f"dqw{i % 2}", wd_deq[a, :, col:col + w],
                                 dq_w[:, cw])
                dq_spd[i] = (i % 2, tot)
            (SY, SC)[i % 2](_sp)

        def dq_emit_some(k):
            e = min(dq_next[0] + k, NDQ)
            for i in range(dq_next[0], e):
                dq_emit_one(i)
            dq_next[0] = e

        def dq_wait_spilled(eng, upto_i):
            tots = {}
            for i in range(upto_i + 1):
                par, t = dq_spd[i]
                tots[par] = max(tots.get(par, 0), t)
            for par, t in tots.items():
                qq = (qsy, qsc)[par]
                eng.wait_ge(qq.tot(f"dqw{par}")[0], t)

        def _p0(g):
            qgp.dma(g, "const", ones22_s[:, :], ones22[:, :])
            qgp.dma(g, "const", onesc_s[:, :], onesc[:, :])
            qgp.dma(g, "const", onescf_s[:, :], onesf[:, :])
            qgp.dma(g, "const", masku_s[:, :], masku[:, :])
            qgp.dma(g, "const", bq_s[:, :], bq[:, :])
            qgp.dma(g, "const", alik_s[:, :], alik[:, :])
            qgp.dma(g, "const", aliq_sb[:, :], aliqm[:, :])
            qgp.dma(g, "const", bpc_s[:, :], bpc[:, :])
        GP(_p0)

        _sc1 = nc.enter_named_scope("p1a", False)
        st_p1a = contextlib.ExitStack()
        xsb = [sbin(st_p1a, f"xsb{i}", [128, c.HC * c.TSUP], BF16)
               for i in range(2)]
        wq_sb = sbin(st_p1a, "wq_sb", [128, c.HC * c.CCH * 128], BF16)
        qe_sb = sbin(st_p1a, "qe_sb", [128, 4 * c.TSUP], BF16)
        sqe_sb = sbin(st_p1a, "sqe_sb", [128, 2 * c.TSUP], BF16)
        xc_sb = [sbin(st_p1a, f"xc{i}", [128, c.TB], BF16) for i in range(2)]

        xcast_done = {}
        for hc in range(c.HC):
            def _xc(g, hc=hc):
                sl = hc % 2
                if hc >= 2:
                    qgp.w(g, f"xcb{sl}")
                qgp.dma(g, f"xca{sl}", xc_sb[sl][:, :], xT3[hc, :, :])
                qgp.w(g, f"xca{sl}")
                tot = qgp.dma(g, f"xcb{sl}", xbf_d[hc, :, :], xc_sb[sl][:, :])
                xcast_done[hc] = tot
            GP(_xc)

        x_ld = {}
        p1a_mm = {}
        p1a_ev = {}
        p1a_sq = {}
        p1a_q2mm = {}
        p1a_q2ev = {}
        nsq = [0]

        def p1a_wqload(ps):
            def _wq(g, ps=ps):
                if ps >= 1:
                    g.wait_ge(tS, p1a_mm[(ps * c.NTS - 1, c.CCH - 1)])
                nq = c.HC // 4
                for qtr in range(4):
                    qgp.dma(g, "wqp", wq_sb[:, qtr * nq * c.CCH * 128:
                                            (qtr + 1) * nq * c.CCH * 128]
                            .rearrange("p (h q) -> p h q", q=c.CCH * 128),
                            wq3[qtr * nq:(qtr + 1) * nq, :,
                                ps * c.CCH * 128:(ps + 1) * c.CCH * 128]
                            .rearrange("h p q -> p h q"))
            GP(_wq)

        def p1a_load(ts2):
            def _l(g, ts2=ts2):
                if ts2 == 0:
                    qgp.w(g, "xcb0")
                    qgp.w(g, "xcb1")
                if ts2 >= 2:
                    g.wait_ge(tS, p1a_mm[(ts2 - 2, c.CCH - 1)])
                nq4 = c.HC // 4
                ts = ts2 % c.NTS
                for q4 in range(4):
                    qgp.dma(g, f"xs{ts2 % 2}",
                            xsb[ts2 % 2][:, q4 * nq4 * c.TSUP:
                                         (q4 + 1) * nq4 * c.TSUP]
                            .rearrange("p (h t) -> p h t", t=c.TSUP),
                            xbf_d[q4 * nq4:(q4 + 1) * nq4, :,
                                  ts * c.TSUP:(ts + 1) * c.TSUP]
                            .rearrange("h p t -> p h t"))
            GP(_l)

        p1a_wqload(0)
        p1a_load(0)
        p1a_load(1)
        for ts2 in range(c.NPASS * c.NTS):
            ps = ts2 // c.NTS
            ts = ts2 % c.NTS
            for ccl in range(c.CCH):
                cc = ps * c.CCH + ccl
                idx = ts2 * c.CCH + ccl
                slot = idx % 4
                pp = (psA, psB)[slot // 2]
                col = (slot % 2) * 512

                def _mm(t, ts2=ts2, ccl=ccl, cc=cc, idx=idx, pp=pp, col=col):
                    if ccl == 0:
                        qgp.w(t, f"xs{ts2 % 2}")
                        if ts2 == 0:
                            qgp.w(t, "const")
                        if ts2 % c.NTS == 0:
                            qgp.w(t, "wqp")
                    if idx >= 4:
                        t.wait_ge(aS, p1a_ev[idx - 4])
                    for hc in range(c.HC):
                        MM(t, pp[:, col:col + c.TSUP],
                           wq_sb[:, (hc * c.CCH + ccl) * 128:
                                 (hc * c.CCH + ccl + 1) * 128],
                           xsb[ts2 % 2][:, hc * c.TSUP:(hc + 1) * c.TSUP],
                           start=(hc == 0), stop=(hc == c.HC - 1))
                    pemark()
                    p1a_mm[(ts2, ccl)] = pe.n
                TE(_mm)

                def _ev(s, ts2=ts2, ts=ts, cc=cc, idx=idx, pp=pp, col=col):
                    es = idx % 4
                    s.wait_ge(tS, p1a_mm[(ts2, cc % c.CCH)])
                    if idx >= 4:
                        qsc.w(s, f"sp{idx % 4}")
                    act.c(s.activation(qe_sb[:, es * c.TSUP:(es + 1) * c.TSUP],
                                       pp[:, col:col + c.TSUP],
                                       AF.Identity, bias=bq_s[:, cc:cc + 1]))
                    p1a_ev[idx] = act.n
                    s.wait_ge(aS, p1a_ev[idx])
                    qsc.dma(s, f"sp{idx % 4}",
                            qkv_sp[cc * 128:(cc + 1) * 128,
                                   ts * c.TSUP:(ts + 1) * c.TSUP],
                            qe_sb[:, es * c.TSUP:(es + 1) * c.TSUP])
                SC(_ev)

                if cc % 3 < 2:
                    si = nsq[0]
                    nsq[0] += 1

                    def _sq(s, idx=idx, si=si):
                        ss = si % 2
                        s.wait_ge(aS, p1a_ev[idx])
                        if si >= 2:
                            s.wait_ge(tS, p1a_q2mm[si - 2])
                        act.c(s.activation(
                            sqe_sb[:, ss * c.TSUP:(ss + 1) * c.TSUP],
                            qe_sb[:, (idx % 4) * c.TSUP:
                                  (idx % 4 + 1) * c.TSUP], AF.Square))
                        p1a_sq[si] = act.n
                    SC(_sq)

                    def _q2(t, si=si):
                        ss = si % 2
                        t.wait_ge(aS, p1a_sq[si])
                        if si >= 2:
                            t.wait_ge(vS, p1a_q2ev[si - 2])
                        MM(t, psC[0:1, ss * 512:ss * 512 + c.TSUP],
                           onesc_s[:, :],
                           sqe_sb[:, ss * c.TSUP:(ss + 1) * c.TSUP],
                           start=True, stop=True)
                        pemark()
                        p1a_q2mm[si] = pe.n
                    TE(_q2)

                    def _q2e(v, si=si):
                        ss = si % 2
                        v.wait_ge(tS, p1a_q2mm[si])
                        if si >= 2:
                            qgp.w(v, "q2st")
                        dve.c(v.tensor_copy(
                            q2st[0:1, ss * c.TSUP:(ss + 1) * c.TSUP],
                            psC[0:1, ss * 512:ss * 512 + c.TSUP]))
                        p1a_q2ev[si] = dve.n
                    VE(_q2e)

                    def _q2w(g, ts=ts, cc=cc, si=si):
                        ss = si % 2
                        g.wait_ge(vS, p1a_q2ev[si])
                        g3 = cc % 3
                        hh = cc // 3
                        row = hh if g3 == 0 else c.NH + hh
                        scol = ts * c.TSUP
                        qgp.dma(g, "q2st", q2k2_d[row:row + 1,
                                                  scol:scol + c.TSUP],
                                q2st[0:1, ss * c.TSUP:(ss + 1) * c.TSUP])
                    GP(_q2w)

            if ts2 + 2 < c.NPASS * c.NTS:
                p1a_load(ts2 + 2)
            if ts2 % c.NTS == c.NTS - 1 and ts2 // c.NTS + 1 < c.NPASS:
                p1a_wqload(ts2 // c.NTS + 1)
            dq_emit_some(2)

        if os.environ.get("MK_UPTO") == "p1a":
            def _fin0(sy):
                qsc.w(sy, "sp0")
                qsc.w(sy, "sp1")
                qsc.w(sy, "sp2")
                qsc.w(sy, "sp3")
            SY(_fin0)
            st_p1a.close()
            st_dq.close()
            return nc

        nc.leave_named_scope("p1a", _sc1[0], False)
        _sc2 = nc.enter_named_scope("rows", False)
        def _rows_gp(g):
            g.wait_ge(vS, p1a_q2ev[nsq[0] - 1])
            qgp.w(g, "q2st")
            qgp.dma(g, "rows", rwf[:, :], q2k2_d[c.NH:2 * c.NH, :])
            rt["rows_k2"] = True
        GP(_rows_gp)

        def _rows_v(v):
            qgp.w(v, "rows")
            dve.c(v.tensor_reduce(rwm[:, :], rwf[:, :], AX, ALU.max))
            rt["rows_max"] = dve.n
        VE(_rows_v)

        def _rows_g2(g):
            g.wait_ge(vS, rt["rows_max"])
            qgp.dma(g, "rows", rwf[:, 0:c.TPC], q2k2_d[0:c.NH, 0:c.TPC])
        GP(_rows_g2)

        def _rows_v1(v):
            qgp.w(v, "rows")
            dve.c(v.tensor_scalar(rwf[:, 0:c.TPC], rwf[:, 0:c.TPC],
                                  rwm[:, 0:1], None, ALU.mult))
            v.wait_ge(vS, dve.n)
            dve.c(v.tensor_scalar(rwf[:, 0:c.TPC], rwf[:, 0:c.TPC], 1e-12,
                                  None, ALU.max))
            rt["rows_mul"] = dve.n
        VE(_rows_v1)

        def _rows_s(s):
            s.wait_ge(vS, rt["rows_mul"])
            act.c(s.activation(rwf[:, 0:c.TPC], rwf[:, 0:c.TPC], AF.Sqrt))
            rt["rows_sqrt"] = act.n
        SC(_rows_s)

        def _rows_v2(v):
            v.wait_ge(aS, rt["rows_sqrt"])
            dve.c(v.scalar_tensor_tensor(rwf[:, 0:c.TPC], rwf[:, 0:c.TPC],
                                         -1.0, aliq_sb[:, :], ALU.mult,
                                         ALU.add))
            v.wait_ge(vS, dve.n)
            dve.c(v.tensor_copy(rwfh[:, :], rwf[:, 0:c.TPC]))
            v.wait_ge(vS, dve.n)
            dve.c(v.tensor_tensor(rwfl[:, :], rwf[:, 0:c.TPC], rwfh[:, :],
                                  ALU.subtract))
            rt["rows_done"] = dve.n
        VE(_rows_v2)

        if os.environ.get("MK_UPTO") == "rows":
            def _fin0(sy):
                sy.wait_ge(vS, rt["rows_done"])
            SY(_fin0)
            st_p1a.close()
            st_dq.close()
            return nc

        nc.leave_named_scope("rows", _sc2[0], False)
        _sc3 = nc.enter_named_scope("p1b", False)
        st_p1a.close()
        st_p1b = contextlib.ExitStack()
        qt_sb = [sbin(st_p1b, f"qt{i}", [128, c.TPC], BF16) for i in range(2)]
        kt_sb = [sbin(st_p1b, f"kt{i}", [128, c.TB], BF16) for i in range(2)]
        vn_sb = [sbin(st_p1b, f"vn{i}", [128, c.TB], BF16) for i in range(2)]
        pt_sb = sbin(st_p1b, "pt_sb", [128, c.KC * c.TPC], BF16)
        frow_s = sbin(st_p1b, "frow_s", [2, c.TPC], BF16)
        rrow_s = sbin(st_p1b, "rrow_s", [1, 2 * c.TPC], F32)
        rrbf_s = sbin(st_p1b, "rrbf_s", [1, 2 * c.TPC], BF16)
        ot_f = sbin(st_p1b, "ot_f", [128, c.TPC], F32)
        ot_b = sbin(st_p1b, "ot_b", [128, c.TPC], BF16)

        p1b_smm = {}
        p1b_mask = {}
        p1b_exp = {}
        p1b_av = {}
        p1b_otf = {}
        p1b_otb = {}
        p1b_rr = {}
        p1b_done_pe = {}

        def qstart(ci):
            return 128 * ci if ci < c.KH else 0

        for hh in range(c.NH):
            idx = hh
            buf = idx % 2

            def _ld(sy, idx=idx, hh=hh, buf=buf):
                if idx == 0:
                    for kq in range(4):
                        qsc.w(sy, f"sp{kq}")
                    qgp.w(sy, "rows")
                    qgp.w(sy, "xcb0")
                    qgp.w(sy, "xcb1")
                    qgp.w(sy, "q2st")
                if idx >= 2:
                    sy.wait_ge(tS, p1b_done_pe[idx - 2])
                r0 = (3 * hh) * 128
                qsy.dma(sy, f"qk{buf}", qt_sb[buf][:, :],
                        qkv_sp[r0:r0 + 128, 0:c.TPC])
                qsy.dma(sy, f"qk{buf}", kt_sb[buf][:, :],
                        qkv_sp[r0 + 128:r0 + 256, :])
                for kc in range(c.KC):
                    qsy.dmat(sy, f"qk{buf}",
                             vn_sb[buf][:, kc * 128:(kc + 1) * 128],
                             qkv_sp[r0 + 256:r0 + 384,
                                    kc * 128:(kc + 1) * 128])
            SY(_ld)

            def _frow(g, idx=idx, hh=hh):
                if idx == 0:
                    g.wait_ge(vS, rt["rows_done"])
                if idx >= 1:
                    qgp.w(g, "frow")
                qgp.dma(g, "frow", frow_d[0:1, :], rwfh[hh:hh + 1, :])
                qgp.dma(g, "frow", frow_d[1:2, :], rwfl[hh:hh + 1, :])
                qgp.w(g, "frow")
                if idx >= 1:
                    g.wait_ge(tS, p1b_done_pe[idx - 1])
                qgp.dma(g, "frow", frow_s[:, :], frow_d[:, :])
            GP(_frow)

            for kci in range(c.KC):
                qs = qstart(kci)
                wdt = c.TPC - qs

                def _smm1(t, idx=idx, hh=hh, buf=buf, kci=kci, qs=qs,
                          wdt=wdt):
                    if kci == 0:
                        qsy.w(t, f"qk{buf}")
                        qgp.w(t, "frow")
                        if idx >= 1:
                            t.wait_ge(aS, p1b_otf[idx - 1])
                            t.wait_ge(vS, p1b_rr[idx - 1])
                    sp = (psA, psB)[kci % 2]
                    if kci >= 2:
                        t.wait_ge(aS, p1b_exp[(idx, kci - 2)])
                    for p0, pw in bpieces(qs, c.TPC):
                        cs = slice(p0, p0 + pw)
                        MM(t, sp[:, cs],
                           kt_sb[buf][:, kci * 128:(kci + 1) * 128],
                           qt_sb[buf][:, p0:p0 + pw],
                           start=True, stop=False)
                        MM(t, sp[:, cs], ones22_s[:, :],
                           frow_s[:, p0:p0 + pw],
                           start=False, stop=True)
                    pemark()
                    p1b_smm[(idx, kci)] = pe.n
                TE(_smm1)

                if kci < c.KH:
                    def _mask1(v, idx=idx, kci=kci, qs=qs):
                        sp = (psA, psB)[kci % 2]
                        v.wait_ge(tS, p1b_smm[(idx, kci)])
                        dve.c(v.tensor_tensor(sp[:, qs:qs + 128],
                                              sp[:, qs:qs + 128],
                                              masku_s[:, :], ALU.add))
                        p1b_mask[(idx, kci)] = dve.n
                    VE(_mask1)

                def _exp1(s, idx=idx, hh=hh, kci=kci, qs=qs):
                    sp = (psA, psB)[kci % 2]
                    s.wait_ge(tS, p1b_smm[(idx, kci)])
                    if (idx, kci) in p1b_mask:
                        s.wait_ge(vS, p1b_mask[(idx, kci)])
                    if idx >= 1 and kci < 2:
                        s.wait_ge(tS, p1b_av[idx - 1])
                    act.c(s.activation(
                        pt_sb[:, kci * c.TPC + qs:(kci + 1) * c.TPC],
                        sp[:, qs:c.TPC], AF.Exp,
                        bias=alik_s[:, hh * c.KC + kci:hh * c.KC + kci + 1],
                        scale=c.SCALE))
                    p1b_exp[(idx, kci)] = act.n
                SC(_exp1)

            def _rsav(t, idx=idx, buf=buf):
                for kci in range(c.KC):
                    qs = qstart(kci)
                    t.wait_ge(aS, p1b_exp[(idx, kci)])
                    first = kci == 0
                    last = kci == c.KC - 1
                    for p0, pw in bpieces(qs, c.TPC):
                        pts = pt_sb[:, kci * c.TPC + p0:
                                    kci * c.TPC + p0 + pw]
                        cs = slice(p0, p0 + pw)
                        MM(t, psD[0:1, cs], onesc_s[:, :], pts,
                           start=first, stop=last)
                        MM(t, psC[:, cs],
                           vn_sb[buf][:, kci * 128:(kci + 1) * 128],
                           pts, start=first, stop=last)
                pemark()
                p1b_av[idx] = pe.n
            TE(_rsav)

            def _rr(v, idx=idx):
                v.wait_ge(tS, p1b_av[idx])
                if idx >= 1:
                    qsc.w(v, "otw")
                dve.c(v.reciprocal(rrow_s[0:1, 0:c.TPC], psD[0:1, 0:c.TPC]))
                v.wait_ge(vS, dve.n)
                dve.c(v.tensor_copy(rrbf_s[0:1, 0:c.TPC],
                                    rrow_s[0:1, 0:c.TPC]))
                v.wait_ge(vS, dve.n)
                dve.c(v.tensor_tensor(rrow_s[0:1, c.TPC:2 * c.TPC],
                                      rrow_s[0:1, 0:c.TPC],
                                      rrbf_s[0:1, 0:c.TPC], ALU.subtract))
                v.wait_ge(vS, dve.n)
                dve.c(v.tensor_copy(rrbf_s[0:1, c.TPC:2 * c.TPC],
                                    rrow_s[0:1, c.TPC:2 * c.TPC]))
                p1b_rr[idx] = dve.n
            VE(_rr)

            def _otf(s, idx=idx):
                s.wait_ge(tS, p1b_av[idx])
                if idx >= 1:
                    s.wait_ge(vS, p1b_otb[idx - 1])
                act.c(s.copy(ot_f[:, :], psC[:, 0:c.TPC]))
                p1b_otf[idx] = act.n
            SC(_otf)

            def _otb(v, idx=idx):
                v.wait_ge(aS, p1b_otf[idx])
                if idx >= 1:
                    qsc.w(v, "otw")
                dve.c(v.tensor_copy(ot_b[:, :], ot_f[:, :]))
                p1b_otb[idx] = dve.n
            VE(_otb)

            def _otw(s, idx=idx, hh=hh):
                s.wait_ge(vS, p1b_otb[idx])
                qsc.dma(s, "otw", ot_d[hh, :, :], ot_b[:, :])
                s.wait_ge(vS, p1b_rr[idx])
                qsc.dma(s, "otw", rrl_d[2 * hh:2 * hh + 1, :],
                        rrbf_s[0:1, 0:c.TPC])
                qsc.dma(s, "otw", rrl_d[2 * hh + 1:2 * hh + 2, :],
                        rrbf_s[0:1, c.TPC:2 * c.TPC])
            SC(_otw)

            p1b_done_pe[idx] = pe.n
            dq_emit_some(2)

        if os.environ.get("MK_UPTO") == "p1b":
            def _fin0(sy):
                qsc.w(sy, "otw")
            SY(_fin0)
            st_p1b.close()
            st_dq.close()
            return nc

        nc.leave_named_scope("p1b", _sc3[0], False)
        _sc4 = nc.enter_named_scope("p2", False)
        st_p1b.close()
        st_p2 = contextlib.ExitStack()
        otm_sb = sbin(st_p2, "otm_sb", [128, c.NH * c.TPC], BF16)
        rr2_sb = sbin(st_p2, "rr2_sb", [2, 2 * c.TPC], BF16)
        wpl_sb = [sbin(st_p2, f"wpl{i}", [128, c.HC * 128], BF16)
                  for i in range(2)]
        xrt_sb = [sbin(st_p2, f"xrt{i}", [128, c.TPC], F32) for i in range(2)]
        rch_sb = sbin(st_p2, "rch_sb", [128, 2 * c.TPC], F32)
        sqp_sb = sbin(st_p2, "sqp_sb", [128, c.TPC], F32)
        ab_sb = sbin(st_p2, "ab_sb", [2, 2 * c.TPC], BF16)
        xlh_sb = sbin(st_p2, "xlh_sb", [128, 2 * c.TPC], BF16)
        xll_sb = sbin(st_p2, "xll_sb", [128, 2 * c.TPC], BF16)

        p2_norm_mm = {}
        p2_norm = {}
        p2_mm = {}
        p2_ev = {}
        p2_row = {}

        def _p2_ld(g):
            qsc.w(g, "otw")
            g.wait_ge(tS, p1b_done_pe[c.NH - 1])
            qsy.w(g, "qk0")
            qsy.w(g, "qk1")
            qgp.w(g, "frow")
            nq4 = c.NH // 4
            for q4 in range(4):
                qgp.dma(g, "p2ld",
                        otm_sb[:, q4 * nq4 * c.TPC:(q4 + 1) * nq4 * c.TPC]
                        .rearrange("p (h t) -> p h t", t=c.TPC),
                        ot_d[q4 * nq4:(q4 + 1) * nq4, :, :]
                        .rearrange("h p t -> p h t"))
        GP(_p2_ld)

        for hc in range(c.NH):
            def _nld(g, hc=hc):
                if hc == 0:
                    qgp.w(g, "p2ld")
                if hc >= 2:
                    g.wait_ge(tS, p2_norm_mm[hc - 2])
                qgp.dma(g, f"rr{hc % 2}",
                        rr2_sb[:, (hc % 2) * c.TPC:(hc % 2 + 1) * c.TPC],
                        rrl_d[2 * hc:2 * hc + 2, :])
            GP(_nld)

            def _nmm(t, hc=hc):
                qgp.w(t, f"rr{hc % 2}")
                pp = (psA, psB)[hc % 2]
                if hc >= 2:
                    t.wait_ge(vS, p2_norm[hc - 2])
                for h0, hw in chunks(c.TPC):
                    MM(t, pp[:, h0:h0 + hw], ones22_s[:, :],
                       rr2_sb[:, (hc % 2) * c.TPC + h0:
                              (hc % 2) * c.TPC + h0 + hw],
                       start=True, stop=True)
                pemark()
                p2_norm_mm[hc] = pe.n
            TE(_nmm)

            def _nmul(v, hc=hc):
                v.wait_ge(tS, p2_norm_mm[hc])
                pp = (psA, psB)[hc % 2]
                dve.c(v.tensor_tensor(otm_sb[:, hc * c.TPC:(hc + 1) * c.TPC],
                                      otm_sb[:, hc * c.TPC:(hc + 1) * c.TPC],
                                      pp[:, 0:c.TPC], ALU.mult))
                p2_norm[hc] = dve.n
            VE(_nmul)
            dq_emit_some(1)

        def p2_wld(oc):
            def _w(g, oc=oc):
                if oc >= 2:
                    g.wait_ge(tS, p2_mm[oc - 2])
                qgp.dma(g, f"wp{oc % 2}", wpl_sb[oc % 2][:, :]
                        .rearrange("p (h q) -> p h q", q=128),
                        wp4[oc, :, :, :].rearrange("h p q -> p h q"))
            GP(_w)

        def p2_xld(oc):
            def _x(sy, oc=oc):
                if oc < 2:
                    sy.wait_ge(tS, p1b_done_pe[c.NH - 1])
                    sy.wait_ge(aS, p1b_otf[c.NH - 1])
                    sy.wait_ge(vS, p1b_rr[c.NH - 1])
                    sy.wait_ge(vS, p1b_otb[c.NH - 1])
                    qsc.w(sy, "otw")
                if oc >= 2:
                    sy.wait_ge(vS, p2_ev[(oc - 2, "res")])
                qsy.dma(sy, f"xr{oc % 2}", xrt_sb[oc % 2][:, :],
                        xT3[oc, :, 0:c.TPC])
            SY(_x)

        p2_wld(0)
        p2_wld(1)
        p2_xld(0)
        p2_xld(1)
        for oc in range(c.HC):
            def _pmm(t, oc=oc):
                pp = (psA, psB)[oc % 2]
                qgp.w(t, f"wp{oc % 2}")
                if oc == 0:
                    t.wait_ge(vS, p2_norm[c.NH - 1])
                if oc >= 2:
                    t.wait_ge(aS, p2_ev[(oc - 2, "copy")])
                for hc in range(c.NH):
                    for h0, hw in chunks(c.TPC):
                        MM(t, pp[:, h0:h0 + hw],
                           wpl_sb[oc % 2][:, hc * 128:(hc + 1) * 128],
                           otm_sb[:, hc * c.TPC + h0:hc * c.TPC + h0 + hw],
                           start=(hc == 0), stop=(hc == c.NH - 1))
                pemark()
                p2_mm[oc] = pe.n
            TE(_pmm)
            if oc + 2 < c.HC:
                p2_wld(oc + 2)

            def _pev(s, oc=oc):
                pp = (psA, psB)[oc % 2]
                es = oc % 2
                s.wait_ge(tS, p2_mm[oc])
                if oc >= 2:
                    qsc.w(s, f"prest{oc % 2}")
                act.c(s.activation(rch_sb[:, es * c.TPC:(es + 1) * c.TPC],
                                   pp[:, 0:c.TPC], AF.Identity,
                                   bias=bpc_s[:, oc:oc + 1]))
                p2_ev[(oc, "copy")] = act.n
            SC(_pev)

            def _pres(v, oc=oc):
                es = oc % 2
                rr = rch_sb[:, es * c.TPC:(es + 1) * c.TPC]
                v.wait_ge(aS, p2_ev[(oc, "copy")])
                qsy.w(v, f"xr{oc % 2}")
                dve.c(v.tensor_tensor(rr, rr, xrt_sb[oc % 2][:, :], ALU.add))
                p2_ev[(oc, "res")] = dve.n
            VE(_pres)
            if oc + 2 < c.HC:
                p2_xld(oc + 2)

            def _psq(s, oc=oc):
                es = oc % 2
                s.wait_ge(vS, p2_ev[(oc, "res")])
                act.c(s.activation(sqp_sb[:, :],
                                   rch_sb[:, es * c.TPC:(es + 1) * c.TPC],
                                   AF.Square))
                p2_ev[(oc, "sq")] = act.n
            SC(_psq)

            def _prow(t, oc=oc):
                es = oc % 2
                t.wait_ge(vS, p2_ev[(oc, "res")])
                for h0, hw in chunks(c.TPC):
                    MM(t, psC[0:1, h0:h0 + hw], onescf_s[:, :],
                       rch_sb[:, es * c.TPC + h0:es * c.TPC + h0 + hw],
                       start=(oc == 0), stop=(oc == c.HC - 1))
                t.wait_ge(aS, p2_ev[(oc, "sq")])
                for h0, hw in chunks(c.TPC):
                    MM(t, psD[0:1, h0:h0 + hw], onescf_s[:, :],
                       sqp_sb[:, h0:h0 + hw],
                       start=(oc == 0), stop=(oc == c.HC - 1))
                pemark()
                p2_row[oc] = pe.n
            TE(_prow)

            def _pst(s, oc=oc):
                es = oc % 2
                s.wait_ge(tS, p2_row[oc])
                qsc.dma(s, f"prest{oc % 2}", pres_d[oc, :, :],
                        rch_sb[:, es * c.TPC:(es + 1) * c.TPC])
            SC(_pst)

            dq_emit_some(1)

        lnr_s = sbin(st_p2, "lnr_s", [1, 8 * c.TPC], F32)

        def _l1r(v):
            v.wait_ge(tS, p2_row[c.HC - 1])
            dve.c(v.tensor_scalar(lnr_s[0:1, 0:c.TPC], psC[0:1, 0:c.TPC],
                                  1.0 / c.DM, None, ALU.mult))
            dve.c(v.tensor_scalar(lnr_s[0:1, c.TPC:2 * c.TPC],
                                  psD[0:1, 0:c.TPC], 1.0 / c.DM, None,
                                  ALU.mult))
            v.wait_ge(vS, dve.n)
            dve.c(v.tensor_tensor(lnr_s[0:1, 2 * c.TPC:3 * c.TPC],
                                  lnr_s[0:1, 0:c.TPC], lnr_s[0:1, 0:c.TPC],
                                  ALU.mult))
            v.wait_ge(vS, dve.n)
            dve.c(v.tensor_tensor(lnr_s[0:1, c.TPC:2 * c.TPC],
                                  lnr_s[0:1, c.TPC:2 * c.TPC],
                                  lnr_s[0:1, 2 * c.TPC:3 * c.TPC],
                                  ALU.subtract))
            v.wait_ge(vS, dve.n)
            dve.c(v.tensor_scalar(lnr_s[0:1, c.TPC:2 * c.TPC],
                                  lnr_s[0:1, c.TPC:2 * c.TPC], 1e-5, None,
                                  ALU.add))
            rt["l1var"] = dve.n
        VE(_l1r)

        def _l1s(s):
            s.wait_ge(vS, rt["l1var"])
            act.c(s.activation(lnr_s[0:1, 3 * c.TPC:4 * c.TPC],
                               lnr_s[0:1, c.TPC:2 * c.TPC], AF.Sqrt))
            rt["l1std"] = act.n
        SC(_l1s)

        def _l1v2(v):
            v.wait_ge(aS, rt["l1std"])
            dve.c(v.reciprocal(lnr_s[0:1, 4 * c.TPC:5 * c.TPC],
                               lnr_s[0:1, 3 * c.TPC:4 * c.TPC]))
            v.wait_ge(vS, dve.n)
            dve.c(v.tensor_tensor(lnr_s[0:1, 5 * c.TPC:6 * c.TPC],
                                  lnr_s[0:1, 0:c.TPC],
                                  lnr_s[0:1, 4 * c.TPC:5 * c.TPC], ALU.mult))
            v.wait_ge(vS, dve.n)
            dve.c(v.tensor_scalar(lnr_s[0:1, 5 * c.TPC:6 * c.TPC],
                                  lnr_s[0:1, 5 * c.TPC:6 * c.TPC], -1.0,
                                  None, ALU.mult))
            rt["l1ab"] = dve.n
        VE(_l1v2)

        def _l1b1(g):
            g.wait_ge(vS, rt["l1ab"])
            qgp.dma(g, "l1b", abrow_d[0:1, :], lnr_s[0:1, 4 * c.TPC:5 * c.TPC])
            qgp.dma(g, "l1b", abrow_d[2:3, :], lnr_s[0:1, 5 * c.TPC:6 * c.TPC])
            qgp.w(g, "l1b")
            qgp.dma(g, "l1b", ab_sb[0:1, 0:c.TPC], abrow_d[0:1, :])
            qgp.dma(g, "l1b", ab_sb[0:1, c.TPC:2 * c.TPC], abrow_d[2:3, :])
        GP(_l1b1)

        def _l1lo(v):
            qgp.w(v, "l1b")
            dve.c(v.tensor_tensor(lnr_s[0:1, 6 * c.TPC:7 * c.TPC],
                                  lnr_s[0:1, 4 * c.TPC:5 * c.TPC],
                                  ab_sb[0:1, 0:c.TPC], ALU.subtract))
            dve.c(v.tensor_tensor(lnr_s[0:1, 7 * c.TPC:8 * c.TPC],
                                  lnr_s[0:1, 5 * c.TPC:6 * c.TPC],
                                  ab_sb[0:1, c.TPC:2 * c.TPC], ALU.subtract))
            rt["l1lo"] = dve.n
        VE(_l1lo)

        def _l1b2(g):
            g.wait_ge(vS, rt["l1lo"])
            qgp.dma(g, "l1b", abrow_d[1:2, :], lnr_s[0:1, 6 * c.TPC:7 * c.TPC])
            qgp.dma(g, "l1b", abrow_d[3:4, :], lnr_s[0:1, 7 * c.TPC:8 * c.TPC])
            qgp.w(g, "l1b")
            qgp.dma(g, "l1b", ab_sb[:, 0:c.TPC], abrow_d[0:2, :])
            qgp.dma(g, "l1b", ab_sb[:, c.TPC:2 * c.TPC], abrow_d[2:4, :])
        GP(_l1b2)

        def _l1bc(t):
            qgp.w(t, "l1b")
            for h0, hw in chunks(c.TPC):
                MM(t, psA[:, h0:h0 + hw], ones22_s[:, :],
                   ab_sb[:, h0:h0 + hw], start=True, stop=True)
                MM(t, psB[:, h0:h0 + hw], ones22_s[:, :],
                   ab_sb[:, c.TPC + h0:c.TPC + h0 + hw],
                   start=True, stop=True)
            pemark()
            rt["l1bc"] = pe.n
        TE(_l1bc)

        p2b = {}
        for oc in range(c.HC):
            def _n2ld(sy, oc=oc):
                if oc < 2:
                    qsc.w(sy, f"prest{oc % 2}")
                else:
                    sy.wait_ge(vS, p2b[(oc - 2, "lo")])
                qsy.dma(sy, f"prl{oc % 2}",
                        rch_sb[:, (oc % 2) * c.TPC:(oc % 2 + 1) * c.TPC],
                        pres_d[oc, :, :])
            SY(_n2ld)

            def _n2v(v, oc=oc):
                es = oc % 2
                rr = rch_sb[:, es * c.TPC:(es + 1) * c.TPC]
                qsy.w(v, f"prl{oc % 2}")
                if oc == 0:
                    v.wait_ge(tS, rt["l1bc"])
                if oc >= 2:
                    qsc.w(v, f"xst{oc % 2}")
                dve.c(v.tensor_tensor(rr, rr, psA[:, 0:c.TPC], ALU.mult))
                v.wait_ge(vS, dve.n)
                dve.c(v.tensor_tensor(rr, rr, psB[:, 0:c.TPC], ALU.add))
                v.wait_ge(vS, dve.n)
                xh = xlh_sb[:, es * c.TPC:(es + 1) * c.TPC]
                xl = xll_sb[:, es * c.TPC:(es + 1) * c.TPC]
                dve.c(v.tensor_copy(xh, rr))
                v.wait_ge(vS, dve.n)
                p2b[(oc, "hi")] = dve.n
                dve.c(v.tensor_tensor(xl, rr, xh, ALU.subtract))
                p2b[(oc, "lo")] = dve.n
            VE(_n2v)

            def _n2st(s, oc=oc):
                es = oc % 2
                xh = xlh_sb[:, es * c.TPC:(es + 1) * c.TPC]
                xl = xll_sb[:, es * c.TPC:(es + 1) * c.TPC]
                s.wait_ge(vS, p2b[(oc, "lo")])
                qsc.dma(s, f"xst{oc % 2}", xlnh_d[oc * 128:(oc + 1) * 128, :],
                        xh)
                qsc.dma(s, f"xst{oc % 2}", xlnl_d[oc * 128:(oc + 1) * 128, :],
                        xl)
            SC(_n2st)
            dq_emit_some(1)

        if os.environ.get("MK_UPTO") == "p2":
            def _fin0(sy):
                qsc.w(sy, "xst0")
                qsc.w(sy, "xst1")
            SY(_fin0)
            st_p2.close()
            st_dq.close()
            return nc

        nc.leave_named_scope("p2", _sc4[0], False)
        _sc5 = nc.enter_named_scope("p3a", False)
        st_p2.close()
        st_p3a = contextlib.ExitStack()
        xf_sb = sbin(st_p3a, "xf_sb", [128, c.HC * c.TPC], BF16)
        wgl_sb = [sbin(st_p3a, f"wgl{i}", [128, c.HC * 128], BF16)
                  for i in range(2)]
        wul_sb = [sbin(st_p3a, f"wul{i}", [128, c.HC * 128], BF16)
                  for i in range(2)]
        gu_sb = sbin(st_p3a, "gu_sb", [128, c.TPC], BF16)
        int_sb = [sbin(st_p3a, f"int{i}", [128, c.TPC], BF16)
                  for i in range(2)]

        xlnv = xlnh_d.rearrange("(h p) t -> h p t", p=128)
        p3_gu = {}
        p3_int = {}
        pgS = sem("pgS")

        def _pgate(sy):
            qsc.w(sy, "xst0")
            qsc.w(sy, "xst1")
            qsc.w(sy, "prest0")
            qsc.w(sy, "prest1")
            qsy.w(sy, "prl0")
            qsy.w(sy, "prl1")
            qsy.w(sy, "xr0")
            qsy.w(sy, "xr1")
            qgp.w(sy, "l1b")
            qgp.w(sy, "p2ld")
            qgp.w(sy, "rr0")
            qgp.w(sy, "rr1")
            qgp.w(sy, "wp0")
            qgp.w(sy, "wp1")
            sy.wait_ge(vS, p2b[(c.HC - 1, "lo")])
            sy.wait_ge(aS, rt["l1std"])
            sy.wait_ge(tS, rt["l1bc"])
            sy.nop().then_inc(pgS, 1)
        SY(_pgate)

        def _p3x(sy):
            sy.wait_ge(pgS, 1)
            nq4 = c.HC // 4
            for q4 in range(4):
                qsy.dma(sy, "p3x", xf_sb[:, q4 * nq4 * c.TPC:
                                         (q4 + 1) * nq4 * c.TPC]
                        .rearrange("p (h t) -> p h t", t=c.TPC),
                        xlnv[q4 * nq4:(q4 + 1) * nq4, :, :]
                        .rearrange("h p t -> p h t"))
        SY(_p3x)

        def p3_guload(j):
            def _w(s, j=j):
                sl = j % 2
                if j < 2:
                    s.wait_ge(pgS, 1)
                if j >= 2:
                    s.wait_ge(tS, p3_gu[j - 2])
                dq_wait_spilled(s, (min(j // 8, c.NCGU - 1) + 1)
                                * 2 * c.HC - 1)
                nq4 = c.HC // 4
                for q4 in range(4):
                    qsc.dma(s, f"wgu{sl}", wgl_sb[sl][:, q4 * nq4 * 128:
                                                      (q4 + 1) * nq4 * 128]
                            .rearrange("p (h q) -> p h q", q=128),
                            wg_deq[:, j, q4 * nq4:(q4 + 1) * nq4, :])
                    qsc.dma(s, f"wgu{sl}", wul_sb[sl][:, q4 * nq4 * 128:
                                                      (q4 + 1) * nq4 * 128]
                            .rearrange("p (h q) -> p h q", q=128),
                            wu_deq[:, j, q4 * nq4:(q4 + 1) * nq4, :])
            SC(_w)

        p3_guload(0)
        p3_guload(1)
        for j in range(c.JC):
            def _gu(t, j=j):
                sl = j % 2
                psG = (psA, psB)[j % 2]
                psU = (psC, psD)[j % 2]
                qsc.w(t, f"wgu{sl}")
                if j == 0:
                    qsy.w(t, "p3x")
                    t.wait_ge(tS, rt["l1bc"])
                    t.wait_ge(vS, p2b[(c.HC - 1, "lo")])
                if j >= 2:
                    t.wait_ge(vS, p3_int[j - 2])
                for hc in range(c.HC):
                    lg = wgl_sb[sl][:, hc * 128:(hc + 1) * 128]
                    lu = wul_sb[sl][:, hc * 128:(hc + 1) * 128]
                    for h0, hw in chunks(c.TPC):
                        xr_ = xf_sb[:, hc * c.TPC + h0:hc * c.TPC + h0 + hw]
                        MM(t, psG[:, h0:h0 + hw], lg, xr_,
                           start=(hc == 0), stop=(hc == c.HC - 1))
                        MM(t, psU[:, h0:h0 + hw], lu, xr_,
                           start=(hc == 0), stop=(hc == c.HC - 1))
                pemark()
                p3_gu[j] = pe.n
            TE(_gu)
            if j + 2 < c.JC:
                p3_guload(j + 2)

            def _silu(s, j=j):
                psG = (psA, psB)[j % 2]
                s.wait_ge(tS, p3_gu[j])
                if j >= 1:
                    s.wait_ge(vS, p3_int[j - 1])
                act.c(s.activation(gu_sb[:, :], psG[:, 0:c.TPC], AF.Silu))
                p3_gu[(j, "silu")] = act.n
            SC(_silu)

            def _mul(v, j=j):
                psU = (psC, psD)[j % 2]
                v.wait_ge(aS, p3_gu[(j, "silu")])
                if j >= 2:
                    qsy.w(v, f"ist{j % 2}")
                dve.c(v.tensor_tensor(int_sb[j % 2][:, :], gu_sb[:, :],
                                      psU[:, 0:c.TPC], ALU.mult))
                p3_int[j] = dve.n
            VE(_mul)

            def _ist(sy, j=j):
                sy.wait_ge(vS, p3_int[j])
                qsy.dma(sy, f"ist{j % 2}", inter_d[j, :, :],
                        int_sb[j % 2][:, :])
            SY(_ist)
            dq_emit_some(12)

        if os.environ.get("MK_UPTO") == "p3a":
            def _fin0(sy):
                qsy.w(sy, "ist0")
                qsy.w(sy, "ist1")
            SY(_fin0)
            st_p3a.close()
            st_dq.close()
            return nc

        nc.leave_named_scope("p3a", _sc5[0], False)
        _sc6 = nc.enter_named_scope("p3b", False)
        st_p3a.close()
        st_p3b = contextlib.ExitStack()
        JH = c.JC // 2
        wdl_sb = [sbin(st_p3b, f"wdl{i}", [128, JH * 512], BF16)
                  for i in range(2)]
        it_sb = [sbin(st_p3b, f"it{i}", [128, c.JC * 128], BF16)
                 for i in range(2)]
        yev_sb = [sbin(st_p3b, f"yev{i}", [128, 512], F32) for i in range(2)]
        l2a_sb = sbin(st_p3b, "l2a_sb", [128, c.DM], F32)
        DH = c.DM // 2
        l2h_sb = sbin(st_p3b, "l2h_sb", [128, DH], BF16)

        NDB = c.DM // 512
        p3_dn = {}
        p3_ev = {}
        p3_ln = {}

        def p3_wdload(db, half):
            def _w(s, db=db, half=half):
                sl = half
                if db >= 1:
                    s.wait_ge(tS, p3_dn[(db - 1, c.CPT - 1)])
                if db == 0:
                    qsy.w(s, "ist0")
                    qsy.w(s, "ist1")
                dq_wait_spilled(s, NGU_T + (db // 2 + 1) * c.JC - 1)
                j0 = half * JH
                qsc.dma(s, f"wdq{sl}", wdl_sb[sl][:, :]
                        .rearrange("p (j q) -> p j q", q=512),
                        wd_deq[j0:j0 + JH, :, db * 512:(db + 1) * 512]
                        .rearrange("j p q -> p j q"))
            SC(_w)

        def p3_itload(db, tc):
            def _i(sy, db=db, tc=tc):
                idx = db * c.CPT + tc
                if idx >= 2:
                    pdb, ptc = (idx - 2) // c.CPT, (idx - 2) % c.CPT
                    sy.wait_ge(tS, p3_dn[(pdb, ptc)])
                if idx == 0:
                    qsy.w(sy, "ist0")
                    qsy.w(sy, "ist1")
                qsy.dma(sy, f"it{idx % 2}", it_sb[idx % 2][:, :]
                        .rearrange("p (j q) -> p j q", q=128),
                        inter_d[:, :, tc * 128:(tc + 1) * 128]
                        .rearrange("j p q -> p j q"))
            SY(_i)

        p3_wdload(0, 0)
        p3_wdload(0, 1)
        for pre in range(min(2, NDB * c.CPT)):
            p3_itload(pre // c.CPT, pre % c.CPT)
        for db in range(NDB):
            for tc in range(c.CPT):
                idx = db * c.CPT + tc
                slot = idx % 4
                pp = (psA, psB)[slot // 2]
                pcol = (slot % 2) * 512

                def _dn(t, db=db, tc=tc, idx=idx, pp=pp, pcol=pcol):
                    if tc == 0:
                        qsc.w(t, "wdq0")
                        qsc.w(t, "wdq1")
                    qsy.w(t, f"it{idx % 2}")
                    if idx >= 4:
                        t.wait_ge(aS, p3_ev[((idx - 4) // c.CPT,
                                            (idx - 4) % c.CPT)])
                    for j in range(c.JC):
                        half, jl = divmod(j, JH)
                        lh = it_sb[idx % 2][:, j * 128:(j + 1) * 128]
                        MM(t, pp[:, pcol:pcol + 512], lh,
                           wdl_sb[half][:, jl * 512:(jl + 1) * 512],
                           start=(j == 0), stop=(j == c.JC - 1))
                    pemark()
                    p3_dn[(db, tc)] = pe.n
                TE(_dn)
                if idx + 2 < NDB * c.CPT:
                    p3_itload((idx + 2) // c.CPT, (idx + 2) % c.CPT)

                def _ev(s, db=db, tc=tc, idx=idx, pp=pp, pcol=pcol):
                    es = idx % 2
                    s.wait_ge(tS, p3_dn[(db, tc)])
                    if idx >= 2:
                        qsc.w(s, f"ydw{idx % 2}")
                    act.c(s.copy(yev_sb[es][:, :], pp[:, pcol:pcol + 512]))
                    p3_ev[(db, tc)] = act.n
                    s.wait_ge(aS, p3_ev[(db, tc)])
                    qsc.dma(s, f"ydw{idx % 2}",
                            yd[tc * 128:(tc + 1) * 128,
                               db * 512:(db + 1) * 512], yev_sb[es][:, :])
                SC(_ev)
                dq_emit_some(6)
            if db + 1 < NDB:
                p3_wdload(db + 1, 0)
                p3_wdload(db + 1, 1)

        def emit_ln2(w2):
            def _l2ld(sy, w2=w2):
                if w2 == 0:
                    qsc.w(sy, "ydw0")
                    qsc.w(sy, "ydw1")
                if w2 >= 1:
                    qsy.w(sy, "yst")
                qsy.dma(sy, "l2ld", l2a_sb[:, :],
                        yd[w2 * 128:(w2 + 1) * 128, :])
                qsy.dmat(sy, "l2ld", l2h_sb[:, :],
                         xlnh_d[0:DH, w2 * 128:(w2 + 1) * 128])
            SY(_l2ld)

            rsrc = [(xlnh_d, 1), (xlnl_d, 0), (xlnl_d, 1)]
            for rnd in range(4):
                def _l2a0(v, w2=w2, rnd=rnd):
                    qsy.w(v, "l2ld")
                    hf = rnd % 2
                    dve.c(v.tensor_tensor(l2a_sb[:, hf * DH:(hf + 1) * DH],
                                          l2a_sb[:, hf * DH:(hf + 1) * DH],
                                          l2h_sb[:, :], ALU.add))
                    p3_ln[(w2, "add", rnd)] = dve.n
                VE(_l2a0)
                if rnd < 3:
                    def _l2ld2(sy, w2=w2, rnd=rnd):
                        sy.wait_ge(vS, p3_ln[(w2, "add", rnd)])
                        src, hf = rsrc[rnd]
                        qsy.dmat(sy, "l2ld", l2h_sb[:, :],
                                 src[hf * DH:(hf + 1) * DH,
                                     w2 * 128:(w2 + 1) * 128])
                    SY(_l2ld2)

            def _l2a(v, w2=w2):
                v.wait_ge(vS, p3_ln[(w2, "add", 3)])
                dve.c(v.tensor_reduce(rows_s[:, 0:1], l2a_sb[:, :], AX,
                                      ALU.add))
                p3_ln[(w2, "sum")] = dve.n
            VE(_l2a)

            def _l2sq(s, w2=w2):
                s.wait_ge(vS, p3_ln[(w2, "sum")])
                act.c(s.activation(l2h_sb[:, :], l2a_sb[:, 0:DH], AF.Square,
                                   accum_out=rows_s[:, 1:2]))
                s.wait_ge(aS, act.n)
                act.c(s.activation(l2h_sb[:, :], l2a_sb[:, DH:c.DM],
                                   AF.Square, accum_out=rows_s[:, 9:10]))
                p3_ln[(w2, "sq")] = act.n
            SC(_l2sq)

            def _l2v0(v, w2=w2):
                v.wait_ge(aS, p3_ln[(w2, "sq")])
                dve.c(v.tensor_tensor(rows_s[:, 1:2], rows_s[:, 1:2],
                                      rows_s[:, 9:10], ALU.add))
                v.wait_ge(vS, dve.n)
                dve.c(v.tensor_scalar(rows_s[:, 2:3], rows_s[:, 0:1],
                                      1.0 / c.DM, None, ALU.mult))
                v.wait_ge(vS, dve.n)
                dve.c(v.tensor_tensor(rows_s[:, 3:4], rows_s[:, 2:3],
                                      rows_s[:, 2:3], ALU.mult))
                v.wait_ge(vS, dve.n)
                dve.c(v.tensor_scalar(rows_s[:, 4:5], rows_s[:, 1:2],
                                      1.0 / c.DM, None, ALU.mult))
                v.wait_ge(vS, dve.n)
                dve.c(v.tensor_tensor(rows_s[:, 5:6], rows_s[:, 4:5],
                                      rows_s[:, 3:4], ALU.subtract))
                v.wait_ge(vS, dve.n)
                dve.c(v.tensor_scalar(rows_s[:, 5:6], rows_s[:, 5:6], 1e-5,
                                      None, ALU.add))
                p3_ln[(w2, "var")] = dve.n
            VE(_l2v0)

            def _l2s(s, w2=w2):
                s.wait_ge(vS, p3_ln[(w2, "var")])
                act.c(s.activation(rows_s[:, 6:7], rows_s[:, 5:6], AF.Sqrt))
                p3_ln[(w2, "std")] = act.n
            SC(_l2s)

            def _l2ap(v, w2=w2):
                v.wait_ge(aS, p3_ln[(w2, "std")])
                dve.c(v.reciprocal(rows_s[:, 7:8], rows_s[:, 6:7]))
                v.wait_ge(vS, dve.n)
                dve.c(v.tensor_scalar(l2a_sb[:, :], l2a_sb[:, :],
                                      rows_s[:, 2:3], rows_s[:, 7:8],
                                      ALU.subtract, ALU.mult))
                p3_ln[(w2, "apply")] = dve.n
            VE(_l2ap)

            def _l2y(sy, w2=w2):
                sy.wait_ge(vS, p3_ln[(w2, "apply")])
                qsy.dma(sy, "yst", y[w2 * 128:(w2 + 1) * 128, :],
                        l2a_sb[:, :])
            SY(_l2y)

        for w2 in range(c.CPT):
            emit_ln2(w2)

        nc.leave_named_scope("p3b", _sc6[0], False)
        def _fin(sy):
            qsy.w(sy, "yst")
        SY(_fin)
        st_p3b.close()
        st_dq.close()

    return nc


_CACHE = {}


def kernel(L0_attn_hidden_states, L0_attn_w_attn, L0_attn_b_attn,
           L0_attn_w_proj, L0_attn_b_proj, L0_ffn_gate_codes,
           L0_ffn_gate_scales, L0_ffn_up_codes, L0_ffn_up_scales,
           L0_ffn_down_codes, L0_ffn_down_scales):
    cfg = Cfg()
    in_maps = stage_inputs(
        cfg, L0_attn_hidden_states, L0_attn_w_attn, L0_attn_b_attn,
        L0_attn_w_proj, L0_attn_b_proj, L0_ffn_gate_codes, L0_ffn_gate_scales,
        L0_ffn_up_codes, L0_ffn_up_scales, L0_ffn_down_codes,
        L0_ffn_down_scales)
    if "nc" not in _CACHE:
        _CACHE["nc"] = build_program(cfg)
    nc = _CACHE["nc"]
    res = run_bass_kernel_spmd(nc, in_maps, core_ids=list(range(cfg.NC)))
    out = np.concatenate([r["y"] for r in res.results], axis=0)
    return out.reshape(cfg.B, cfg.S, cfg.DM).astype(np.float32)
